# version 02
# baseline (speedup 1.0000x reference)
import numpy as np
import ml_dtypes
import concourse.bass as bass
import concourse.mybir as mybir
from concourse.bass_utils import run_bass_kernel_spmd



_ISIZE = {mybir.dt.float32: 4, mybir.dt.bfloat16: 2, mybir.dt.uint8: 1,
          mybir.dt.int32: 4}


class Arena:

    def __init__(self, nc, nbytes, name="arena"):
        self.t = nc.alloc_sbuf_tensor(name, [128, nbytes], mybir.dt.uint8)
        self.nbytes = nbytes
        self.cur = 0
        self.peak = 0

    def alloc(self, name, shape, dtype):
        assert len(shape) == 2 and shape[0] <= 128
        per = shape[1] * _ISIZE[dtype]
        off = (self.cur + 31) & ~31
        self.cur = off + per
        self.peak = max(self.peak, self.cur)
        assert self.cur <= self.nbytes, f"arena overflow at {name}: {self.cur}"
        return self.t[0:shape[0], off:off + per].bitcast(dtype)

    def reset(self):
        self.cur = 0


class Sched:
    BARRIER_EXCLUDE = set()

    def __init__(self, nc, semnames):
        self.nc = nc
        self.sems = {n: nc.alloc_semaphore(n) for n in semnames}
        self.dry = True
        self.cnt = {n: 0 for n in semnames}
        self.marks = {}
        self.cur = None
        self.h = None
        self.waited = {}
        self.prefix = ""

    def begin_emit(self, tag, handle):
        self.dry = False
        self.cur = tag
        self.h = handle
        self.waited = {}

    def op(self, tag, name, *args, inc=None, mark=None, **kw):
        if self.dry:
            if inc is not None:
                sem, k = inc if isinstance(inc, tuple) else (inc, 1)
                self.cnt[sem] += k
                if mark is not None:
                    self.marks[(self.prefix, mark)] = (sem, self.cnt[sem])
            elif mark is not None:
                raise ValueError(f"mark {mark} without inc")
            return None
        if self.cur != tag:
            return None
        inst = getattr(self.h, name)(*args, **kw)
        if inc is not None:
            sem, k = inc if isinstance(inc, tuple) else (inc, 1)
            inst.then_inc(self.sems[sem], k)
        return inst

    def barrier(self, key):
        k = (self.prefix, "bar", key)
        if self.dry:
            self.marks[k] = dict(self.cnt)
            return
        snap = self.marks[k]
        for sem, v in snap.items():
            if sem in self.BARRIER_EXCLUDE:
                continue
            if v <= 0 or self.waited.get(sem, 0) >= v:
                continue
            self.waited[sem] = v
            self.h.wait_ge(self.sems[sem], v)

    def wait(self, tag, mark):
        if self.dry or self.cur != tag:
            return
        if isinstance(mark, tuple):
            sem, v = mark
        else:
            sem, v = self.marks[(self.prefix, mark)]
        if v <= 0 or self.waited.get(sem, 0) >= v:
            return
        self.waited[sem] = v
        self.h.wait_ge(self.sems[sem], v)


def _mamba_mod():

    F32 = mybir.dt.float32
    BF16 = mybir.dt.bfloat16
    AF = mybir.ActivationFunctionType
    ALU = mybir.AluOpType

    L, D, E, NST, R = 4096, 1024, 2048, 16, 64
    LC = 512
    NCH = L // LC
    DT, KD, MT = E // 128, D // 128, D // 128
    XP = LC + 4
    EPS = 1e-5


    def prep_mamba_weights(ins):
        f32 = np.float32
        w = {}
        w["wxzT"] = np.ascontiguousarray(np.asarray(ins["L0_attn_in_proj_w"]).T).astype(f32)
        w["convw"] = np.ascontiguousarray(
            np.asarray(ins["L0_attn_conv_w"])[:, 0, :].reshape(DT, 128, 4).transpose(1, 0, 2).reshape(128, DT * 4)).astype(f32)
        w["convb"] = np.ascontiguousarray(np.asarray(ins["L0_attn_conv_b"]).reshape(DT, 128).T).astype(f32)
        w["xprojT"] = np.ascontiguousarray(np.asarray(ins["L0_attn_x_proj_w"]).T).astype(f32)
        w["dtwT"] = np.ascontiguousarray(np.asarray(ins["L0_attn_dt_proj_w"]).T).astype(f32)
        w["dtb"] = np.ascontiguousarray(np.asarray(ins["L0_attn_dt_proj_b"]).reshape(DT, 128).T).astype(f32)
        w["alog"] = np.ascontiguousarray(
            np.asarray(ins["L0_attn_A_log"]).reshape(DT, 128, NST).transpose(1, 0, 2).reshape(128, DT * NST)).astype(f32)
        w["dvec"] = np.ascontiguousarray(np.asarray(ins["L0_attn_D"]).reshape(DT, 128).T).astype(f32)
        w["woT"] = np.ascontiguousarray(np.asarray(ins["L0_attn_out_proj_w"]).T).astype(f32)
        w["ident"] = np.eye(128, dtype=ml_dtypes.bfloat16)
        return w


    DBG_CHUNK = 0
    PARANOID = False
    ZERO_CARRY = False
    ZERO_A = False
    POISON = False


    def make_mamba_phase(nc, sch, alloc, P, io):
        debug = False
        dbgbc = False
        dbg = {}
        x = io["x"]; acc1 = io["out"]
        wxzT = io["wxzT"]; convw = io["convw"]; convb = io["convb"]
        xprojT = io["xprojT"]; dtwT = io["dtwT"]; dtb = io["dtb"]
        alog = io["alog"]; dvec = io["dvec"]; woT = io["woT"]
        ident_d = io["ident"]; d_bc = io["d_bc"]
        A = alloc
        xt = A("xt", [128, 4 * D], F32)
        xnb = A("xnb", [128, D], BF16)
        xnT = A("xnT", [128, KD * LC], BF16)
        wbuf = A("wbuf", [128, 2 * KD * 128], BF16)
        szb = A("szb", [128, DT * LC], BF16)
        xpad = A("xpad", [128, DT * XP], BF16)
        xcb = A("xcb", [128, DT * LC], BF16)
        dlt = A("dlt", [128, DT * LC], F32)
        ub = A("ub", [128, DT * LC], BF16)
        yb = A("yb", [128, DT * LC], F32)
        abuf = A("abuf", [128, 2 * LC], F32)
        bbuf = A("bbuf", [128, LC], BF16)
        hbuf = A("hbuf", [128, LC], F32)
        tmul = A("tmul", [128, LC], F32)
        carry = A("carry", [128, DT * NST], F32)
        Bb = A("Bb", [128, 2 * LC], BF16)
        Cb = A("Cb", [128, 2 * LC], BF16)
        dt_in_b = A("dt_in_b", [64, LC], BF16)
        bm_s = A("bm_s", [NST, LC], F32)
        cm_s = A("cm_s", [NST, LC], F32)
        convacc = A("convacc", [128, 2 * LC], BF16)
        tex = A("tex", [128, LC], F32)
        wobuf = A("wobuf", [128, 2 * DT * 128], BF16)
        ofm = A("ofm", [128, 2 * 512], BF16)
        cw_sb = A("cw_sb", [128, DT * 4], F32)
        cb_sb = A("cb_sb", [128, DT], F32)
        dtb_sb = A("dtb_sb", [128, DT], F32)
        dv_sb = A("dv_sb", [128, DT], F32)
        A_sb = A("A_sb", [128, DT * NST], F32)
        xproj_sb = A("xproj_sb", [128, DT * 96], BF16)
        dtw_sb = A("dtw_sb", [64, DT * 128], BF16)
        id_sb = A("id_sb", [128, 128], BF16)
        lnst = A("lnst", [128, 16], F32)
        eps_sb = A("eps_sb", [128, 1], F32)
        nbig_sb = A("nbig_sb", [128, 1], F32)
        sq_f = A("sq_f", [128, D], F32)


        pP = [P[0], P[1]]
        pC = P[2][0:96, :]
        pD = P[3][0:16, :]
        pE = [P[4], P[5]]
        pT = [P[6][:, 0:64].bitcast(BF16), P[7][:, 0:64].bitcast(BF16)]

        LSUM, LSQ, NEGM, M2, VAR, SD, RSTD = 0, 2, 4, 6, 8, 10, 12

        def schedule(S):
            S.op("gp", "dma_start", out=cw_sb[:], in_=convw, inc=("qc", 16))
            S.op("gp", "dma_start", out=cb_sb[:], in_=convb, inc=("qc", 16))
            S.op("gp", "dma_start", out=dtb_sb[:], in_=dtb, inc=("qc", 16))
            S.op("gp", "dma_start", out=dv_sb[:], in_=dvec, inc=("qc", 16))
            S.op("gp", "dma_start", out=A_sb[:], in_=alog, inc=("qc", 16))
            S.op("gp", "dma_start",
                 out=xproj_sb[:].rearrange("p (d n) -> p d n", d=DT),
                 in_=xprojT.rearrange("(d p) n -> p d n", p=128), inc=("qc", 16))
            S.op("gp", "dma_start",
                 out=dtw_sb[:].rearrange("p (d n) -> p d n", d=DT),
                 in_=dtwT.rearrange("p (d n) -> p d n", d=DT), inc=("qc", 16))
            S.op("gp", "dma_start", out=id_sb[:], in_=ident_d, inc=("qc", 16),
                 mark="consts")
            S.wait("s", "consts")
            S.op("s", "activation", A_sb[:], A_sb[:], AF.Exp, inc="sS", mark="Aexp")
            S.wait("v", "Aexp")
            S.op("v", "tensor_scalar_mul", A_sb[:], A_sb[:], -1.0)
            S.op("v", "memset", carry[:], 0.0)
            S.op("v", "memset", eps_sb[:], EPS)
            S.op("v", "memset", nbig_sb[:], -100.0)
            for dt_ in range(DT):
                S.op("v", "memset", xpad[:, dt_ * XP: dt_ * XP + 3], 0.0,
                     inc=("sv", 1) if dt_ == DT - 1 else None,
                     mark="pre" if dt_ == DT - 1 else None)

            for c in range(NCH):
                if c > 0:
                    S.wait("sync", f"store{c-1}")
                S.op("sync", "dma_start",
                     out=xt[:].rearrange("p (t d) -> p t d", t=4),
                     in_=x[c * LC:(c + 1) * LC, :].rearrange("(t p) d -> p t d", p=128),
                     inc=("qx", 16), mark=f"x{c}")

                for t in range(4):
                    par = t % 2
                    S.wait("s", f"x{c}")
                    if c > 0 or t >= 1:
                        pc, pt_ = (c, t - 1) if t >= 1 else (c - 1, 3)
                        S.wait("s", f"var{pc}.{pt_}")
                    dump = xnb[:, 0:D]
                    xsl = xt[:, t * D:(t + 1) * D]
                    S.op("s", "activation", sq_f[:, 0:D], xsl,
                         AF.Square, inc="sS", mark=f"lnstat{c}.{t}")
                    S.wait("v", f"lnstat{c}.{t}")
                    S.wait("v", f"x{c}")
                    S.op("v", "reduce_sum", lnst[:, LSUM + par:LSUM + par + 1],
                         xsl, axis=mybir.AxisListType.X)
                    S.op("v", "reduce_sum", lnst[:, LSQ + par:LSQ + par + 1],
                         sq_f[:, 0:D], axis=mybir.AxisListType.X,
                         inc="sv", mark=f"lnred{c}.{t}")
                    S.wait("v", f"lnred{c}.{t}")
                    S.op("v", "tensor_scalar_mul", lnst[:, NEGM + par:NEGM + par + 1],
                         lnst[:, LSUM + par:LSUM + par + 1], -1.0 / D,
                         inc="sv", mark=f"negm{c}.{t}")
                    S.wait("v", f"negm{c}.{t}")
                    S.op("v", "tensor_mul", lnst[:, M2 + par:M2 + par + 1],
                         lnst[:, NEGM + par:NEGM + par + 1], lnst[:, NEGM + par:NEGM + par + 1],
                         inc="sv", mark=f"m2{c}.{t}")
                    S.wait("v", f"m2{c}.{t}")
                    S.op("v", "scalar_tensor_tensor", lnst[:, VAR + par:VAR + par + 1],
                         lnst[:, LSQ + par:LSQ + par + 1], 1.0 / D,
                         lnst[:, M2 + par:M2 + par + 1], ALU.mult, ALU.subtract,
                         inc="sv", mark=f"var{c}.{t}")
                    S.wait("s", f"var{c}.{t}")
                    S.op("s", "activation", lnst[:, SD + par:SD + par + 1],
                         lnst[:, VAR + par:VAR + par + 1], AF.Sqrt, bias=eps_sb[:, 0:1],
                         inc="sS", mark=f"sd{c}.{t}")
                    S.wait("v", f"sd{c}.{t}")
                    if c > 0 or t >= 1:
                        pc, pt_ = (c, t - 1) if t >= 1 else (c - 1, 3)
                        S.wait("v", f"xtr{pc}.{pt_}.7")
                    S.op("v", "reciprocal", lnst[:, RSTD + par:RSTD + par + 1],
                         lnst[:, SD + par:SD + par + 1],
                         inc="sv", mark=f"rstd{c}.{t}")
                    S.wait("v", f"rstd{c}.{t}")
                    S.op("v", "tensor_scalar", dump, xsl,
                         lnst[:, NEGM + par:NEGM + par + 1],
                         lnst[:, RSTD + par:RSTD + par + 1],
                         ALU.add, ALU.mult, inc="sv", mark=f"xn{c}.{t}")
                    S.wait("t", f"xn{c}.{t}")
                    if c == 0 and t == 0:
                        S.wait("t", "consts")
                    for kd in range(KD):
                        idx = 8 * t + kd
                        if idx >= 2:
                            S.wait("t", f"trev{c}.{(idx - 2) // 8}.{(idx - 2) % 8}")
                        elif c > 0:
                            S.wait("t", f"oadd{c-1}.{MT-1}.3")
                        S.op("t", "transpose",
                             pT[idx % 2][:],
                             xnb[:, kd * 128:(kd + 1) * 128],
                             id_sb[:], inc="pe", mark=f"xtr{c}.{t}.{kd}")
                        S.wait("s", f"xtr{c}.{t}.{kd}")
                        S.op("s", "copy",
                             xnT[:, kd * LC + t * 128: kd * LC + (t + 1) * 128],
                             pT[idx % 2][:], inc="sS", mark=f"trev{c}.{t}.{kd}")

                if PARANOID:
                    S.barrier(("lnT", c))
                for et in range(2 * DT):
                    if c > 0 or et >= 2:
                        pc, pet = (c, et - 2) if et >= 2 else (c - 1, 2 * DT - 2 + et)
                        S.wait("gp", f"inmm{pc}.{pet}")
                    S.op("gp", "dma_start",
                         out=wbuf[:, (et % 2) * KD * 128:(et % 2 + 1) * KD * 128]
                             .rearrange("p (k e) -> p k e", k=KD),
                         in_=wxzT[:, et * 128:(et + 1) * 128]
                             .rearrange("(k p) e -> p k e", p=128),
                         inc=(f"qwA{et % 2}", 16), mark=f"wld{c}.{et}")

                for et in range(2 * DT):
                    S.wait("t", f"wld{c}.{et}")
                    S.wait("t", f"trev{c}.3.7")
                    if c > 0 or et >= 2:
                        pc, pet = (c, et - 2) if et >= 2 else (c - 1, MT - 2 + et)
                        S.wait("t", f"inev{pc}.{pet}" if et >= 2 else f"ofmev{c-1}.{MT-2+et}")
                    for kd in range(KD):
                        S.op("t", "matmul", pP[et % 2][:],
                             wbuf[:, (et % 2) * KD * 128 + kd * 128:(et % 2) * KD * 128 + (kd + 1) * 128],
                             xnT[:, kd * LC:(kd + 1) * LC],
                             start=(kd == 0), stop=(kd == KD - 1),
                             inc="pe" if kd == KD - 1 else None,
                             mark=f"inmm{c}.{et}" if kd == KD - 1 else None)
                    S.wait("s", f"inmm{c}.{et}")
                    if et < DT:
                        if c > 0:
                            S.wait("s", f"halo{c-1}")
                        else:
                            S.wait("s", "pre")
                        S.op("s", "copy", xpad[:, et * XP + 3: et * XP + 3 + LC],
                             pP[et % 2][:], inc="sS", mark=f"inev{c}.{et}")
                    else:
                        dt_ = et - DT
                        if c > 0:
                            S.wait("s", f"gate{c-1}.{DT-1}")
                        S.op("s", "activation", szb[:, dt_ * LC:(dt_ + 1) * LC],
                             pP[et % 2][:], AF.Silu, inc="sS", mark=f"inev{c}.{et}")

                if PARANOID:
                    S.barrier(("inproj", c))
                for dt_ in range(DT):
                    par = dt_ % 2
                    S.wait("v", f"inev{c}.{dt_}")
                    if c > 0 or dt_ >= 2:
                        pc, pdt = (c, dt_ - 2) if dt_ >= 2 else (c - 1, DT - 2 + dt_)
                        S.wait("v", f"csilu{pc}.{pdt}")
                    base = dt_ * XP
                    acc = convacc[:, par * LC:(par + 1) * LC]
                    S.op("v", "tensor_scalar_mul", acc, xpad[:, base:base + LC],
                         cw_sb[:, dt_ * 4: dt_ * 4 + 1])
                    for k in (1, 2, 3):
                        S.op("v", "scalar_tensor_tensor", acc,
                             xpad[:, base + k: base + k + LC],
                             cw_sb[:, dt_ * 4 + k: dt_ * 4 + k + 1],
                             acc, ALU.mult, ALU.add,
                             inc="sv" if k == 3 else None,
                             mark=f"conv{c}.{dt_}" if k == 3 else None)
                    S.wait("s", f"conv{c}.{dt_}")
                    S.op("s", "activation", xcb[:, dt_ * LC:(dt_ + 1) * LC],
                         acc, AF.Silu, bias=cb_sb[:, dt_:dt_ + 1],
                         inc="sS", mark=f"csilu{c}.{dt_}")
                for dt_ in range(DT):
                    base = dt_ * XP
                    S.op("v", "tensor_copy", xpad[:, base:base + 3],
                         xpad[:, base + LC:base + LC + 3],
                         inc="sv" if dt_ == DT - 1 else None,
                         mark=f"halo{c}" if dt_ == DT - 1 else None)

                if PARANOID:
                    S.barrier(("conv", c))
                if c > 0:
                    S.wait("t", f"xdblev{c-1}")
                for dt_ in range(DT):
                    S.wait("t", f"csilu{c}.{dt_}")
                    S.op("t", "matmul", pC[0:80, :],
                         xproj_sb[:, dt_ * 96: dt_ * 96 + 80],
                         xcb[:, dt_ * LC:(dt_ + 1) * LC],
                         start=(dt_ == 0), stop=(dt_ == DT - 1),
                         inc="pe" if dt_ == DT - 1 else None,
                         mark=f"xdbl1{c}" if dt_ == DT - 1 else None)
                    S.op("t", "matmul", pD[:, :],
                         xproj_sb[:, dt_ * 96 + 80: dt_ * 96 + 96],
                         xcb[:, dt_ * LC:(dt_ + 1) * LC],
                         start=(dt_ == 0), stop=(dt_ == DT - 1),
                         inc="pe" if dt_ == DT - 1 else None,
                         mark=f"xdbl2{c}" if dt_ == DT - 1 else None)
                S.wait("v", f"xdbl1{c}")
                S.wait("v", f"xdbl2{c}")
                if c > 0:
                    S.wait("v", f"bounce{c-1}")
                S.op("v", "tensor_copy", dt_in_b[:], pC[0:64, :])
                S.op("v", "tensor_copy", bm_s[:], pC[64:80, :])
                S.op("v", "tensor_copy", cm_s[:], pD[:, :], inc="sv", mark=f"xdblev{c}")

                S.wait("gp", f"xdblev{c}")
                if c >= 2:
                    S.wait("gp", f"bc{c-2}.{NST-1}")
                S.op("gp", "dma_start", out=d_bc[c % 2, 0], in_=bm_s[:],
                     inc=(f"qbo{c % 2}", 16))
                S.op("gp", "dma_start", out=d_bc[c % 2, 1], in_=cm_s[:],
                     inc=(f"qbo{c % 2}", 16), mark=f"bounce{c}")

                for n in range(NST):
                    npar = n % 2
                    S.wait("sync", f"bounce{c}")
                    if c > 0 or n >= 2:
                        pc, pn = (c, n - 2) if n >= 2 else (c - 1, NST - 2 + n)
                        S.wait("sync", f"yacc{pc}.{pn * DT + DT - 1}")
                    S.op("sync", "dma_start", out=Bb[:, npar * LC:(npar + 1) * LC],
                         in_=d_bc[c % 2, 0, n:n + 1, :].broadcast_to([128, LC]),
                         inc=(f"qb{npar}", 16))
                    S.op("sync", "dma_start", out=Cb[:, npar * LC:(npar + 1) * LC],
                         in_=d_bc[c % 2, 1, n:n + 1, :].broadcast_to([128, LC]),
                         inc=(f"qb{npar}", 16), mark=f"bc{c}.{n}")
                    if debug and c == DBG_CHUNK and dbgbc and n in (0, 1):
                        S.wait("sync", f"yacc{c}.{n * DT + DT - 1}")
                        S.op("sync", "dma_start", out=dbg[f"dbg2_Bb{n}"],
                             in_=Bb[:, npar * LC:(npar + 1) * LC], inc=("qd", 16))
                        S.op("sync", "dma_start", out=dbg[f"dbg2_Cb{n}"],
                             in_=Cb[:, npar * LC:(npar + 1) * LC],
                             inc=("qd", 16), mark=f"dump_bc{n}")

                if PARANOID:
                    S.barrier(("xdbl", c))
                for dt_ in range(DT):
                    S.wait("t", f"xdblev{c}")
                    if c > 0 or dt_ >= 2:
                        pc, pdt = (c, dt_ - 2) if dt_ >= 2 else (c - 1, DT - 2 + dt_)
                        S.wait("t", f"dabs{pc}.{pdt}")
                    S.op("t", "matmul", pE[dt_ % 2][:],
                         dtw_sb[:, dt_ * 128:(dt_ + 1) * 128], dt_in_b[:],
                         start=True, stop=True, inc="pe", mark=f"dmm{c}.{dt_}")
                    S.wait("s", f"dmm{c}.{dt_}")
                    if c > 0:
                        S.wait("s", f"uw{c-1}.{DT-1}")
                    if c > 0 or dt_ >= 1:
                        pc, pdt = (c, dt_ - 1) if dt_ >= 1 else (c - 1, DT - 1)
                        S.wait("s", f"dadd{pc}.{pdt}")
                    ppar = pE[dt_ % 2][:]
                    txs = tex[:, 0:LC]
                    dsl = dlt[:, dt_ * LC:(dt_ + 1) * LC]
                    S.op("s", "activation", dsl, ppar, AF.Relu, bias=dtb_sb[:, dt_:dt_ + 1])
                    S.op("s", "activation", txs, ppar, AF.Abs, bias=dtb_sb[:, dt_:dt_ + 1],
                         inc="sS", mark=f"dabs{c}.{dt_}")
                    S.op("s", "activation", txs, txs, AF.Exp, scale=-1.0)
                    S.op("s", "activation", txs, txs, AF.Ln, bias=1.0,
                         inc="sS", mark=f"dln{c}.{dt_}")
                    S.wait("v", f"dln{c}.{dt_}")
                    S.op("v", "tensor_add", dsl, dsl, txs, inc="sv", mark=f"dadd{c}.{dt_}")

                if PARANOID:
                    S.barrier(("delta", c))
                for dt_ in range(DT):
                    sl = slice(dt_ * LC, (dt_ + 1) * LC)
                    if c > 0 and dt_ == 0:
                        S.wait("v", f"omm{c-1}.{MT-1}")
                    S.op("v", "tensor_mul", ub[:, sl], dlt[:, sl], xcb[:, sl],
                         inc="sv", mark=f"uw{c}.{dt_}")

                if PARANOID:
                    S.barrier(("u", c))
                if ZERO_CARRY and c > 0:
                    S.op("v", "memset", carry[:], 0.0)
                for n in range(NST):
                    npar = n % 2
                    if PARANOID:
                        S.barrier(("scann", c, n))
                    S.wait("v", f"bc{c}.{n}")
                    for dt_ in range(DT):
                        it = n * DT + dt_
                        sl = slice(dt_ * LC, (dt_ + 1) * LC)
                        S.wait("s", f"dadd{c}.{dt_}")
                        if c == 0 and it == 0:
                            S.wait("s", "pre")
                        if c > 0 or it >= 2:
                            pc, pit = (c, it - 2) if it >= 2 else (c - 1, NST * DT - 2 + it)
                            S.wait("s", f"scan{pc}.{pit}")
                        S.op("s", "activation", abuf[:, (it % 2) * LC:(it % 2 + 1) * LC],
                             dlt[:, sl], AF.Exp,
                             scale=(0.0 if ZERO_A else A_sb[:, dt_ * NST + n: dt_ * NST + n + 1]),
                             bias=(nbig_sb[:, 0:1] if ZERO_A else 0.0),
                             inc="sS", mark=f"aexp{c}.{it}")
                        S.op("v", "tensor_mul", bbuf[:], ub[:, sl],
                             Bb[:, npar * LC:(npar + 1) * LC])
                        S.wait("v", f"aexp{c}.{it}")
                        S.op("v", "tensor_tensor_scan", hbuf[:],
                             abuf[:, (it % 2) * LC:(it % 2 + 1) * LC], bbuf[:],
                             carry[:, dt_ * NST + n: dt_ * NST + n + 1],
                             ALU.mult, ALU.add, inc="sv", mark=f"scan{c}.{it}")
                        S.wait("v", f"scan{c}.{it}")
                        S.op("v", "tensor_copy",
                             carry[:, dt_ * NST + n: dt_ * NST + n + 1],
                             hbuf[:, LC - 1:LC])
                        if n == 0:
                            S.op("v", "tensor_mul", yb[:, sl], hbuf[:],
                                 Cb[:, npar * LC:(npar + 1) * LC],
                                 inc="sv", mark=f"yacc{c}.{it}")
                        else:
                            S.op("v", "tensor_mul", tmul[:], hbuf[:],
                                 Cb[:, npar * LC:(npar + 1) * LC])
                            S.op("v", "tensor_add", yb[:, sl], yb[:, sl], tmul[:],
                                 inc="sv", mark=f"yacc{c}.{it}")

                if PARANOID:
                    S.barrier(("scan", c))
                for dt_ in range(DT):
                    sl = slice(dt_ * LC, (dt_ + 1) * LC)
                    S.op("v", "scalar_tensor_tensor", yb[:, sl], xcb[:, sl],
                         dv_sb[:, dt_:dt_ + 1], yb[:, sl], ALU.mult, ALU.add)
                    S.op("v", "tensor_mul", ub[:, sl], yb[:, sl], szb[:, sl],
                         inc="sv", mark=f"gate{c}.{dt_}")

                if PARANOID:
                    S.barrier(("gate", c))
                for mt in range(MT):
                    if c > 0 or mt >= 2:
                        pc, pmt = (c, mt - 2) if mt >= 2 else (c - 1, MT - 2 + mt)
                        S.wait("gp", f"omm{pc}.{pmt}")
                    S.op("gp", "dma_start",
                         out=wobuf[:, (mt % 2) * DT * 128:(mt % 2 + 1) * DT * 128]
                             .rearrange("p (k m) -> p k m", k=DT),
                         in_=woT[:, mt * 128:(mt + 1) * 128]
                             .rearrange("(k p) m -> p k m", p=128),
                         inc=(f"qwB{mt % 2}", 16), mark=f"wo{c}.{mt}")

                for mt in range(MT):
                    S.wait("t", f"wo{c}.{mt}")
                    S.wait("t", f"gate{c}.{DT-1}")
                    if mt >= 2:
                        S.wait("t", f"ofmev{c}.{mt-2}")
                    else:
                        S.wait("t", f"inev{c}.{2 * DT - 2 + mt}")
                    for kt in range(DT):
                        S.op("t", "matmul", pP[mt % 2][:],
                             wobuf[:, (mt % 2) * DT * 128 + kt * 128:(mt % 2) * DT * 128 + (kt + 1) * 128],
                             ub[:, kt * LC:(kt + 1) * LC],
                             start=(kt == 0), stop=(kt == DT - 1),
                             inc="pe" if kt == DT - 1 else None,
                             mark=f"omm{c}.{mt}" if kt == DT - 1 else None)
                    S.wait("s", f"omm{c}.{mt}")
                    if mt >= 2:
                        S.wait("s", f"otr{c}.{mt-2}.3")
                    elif c > 0:
                        S.wait("s", f"otr{c-1}.{MT-2+mt}.3")
                    S.op("s", "activation", ofm[:, (mt % 2) * 512:(mt % 2 + 1) * 512],
                         pP[mt % 2][:], AF.Copy, inc="sS", mark=f"ofmev{c}.{mt}")
                    S.wait("t", f"ofmev{c}.{mt}")
                    for t in range(4):
                        idx = 4 * mt + t
                        if idx >= 2:
                            S.wait("t", f"oadd{c}.{(idx - 2) // 4}.{(idx - 2) % 4}")
                        S.op("t", "transpose", pT[idx % 2][:],
                             ofm[:, (mt % 2) * 512 + t * 128:(mt % 2) * 512 + (t + 1) * 128],
                             id_sb[:], inc="pe", mark=f"otr{c}.{mt}.{t}")
                        S.wait("v", f"otr{c}.{mt}.{t}")
                        dst = xt[:, t * D + mt * 128: t * D + (mt + 1) * 128]
                        S.op("v", "tensor_add", dst, pT[idx % 2][:], dst,
                             inc="sv", mark=f"oadd{c}.{mt}.{t}")

                if debug and c == DBG_CHUNK:
                    dc = DBG_CHUNK
                    for nm, src, mk in [
                        ("dbg_xnT", xnT[:], f"trev{dc}.3.7"), ("dbg_xpad", xpad[:], f"inev{dc}.15"),
                        ("dbg_szb", szb[:], f"inev{dc}.31"), ("dbg_xcb", xcb[:], f"csilu{dc}.15"),
                        ("dbg_dlt", dlt[:], f"dadd{dc}.15"), ("dbg_dtin", dt_in_b[:], f"xdblev{dc}"),
                        ("dbg_bm", bm_s[:], f"xdblev{dc}"), ("dbg_cm", cm_s[:], f"xdblev{dc}"),
                        ("dbg_Bb", Bb[:, 0:LC], f"bc{dc}.0"), ("dbg_ub", ub[:], f"uw{dc}.15"),
                        ("dbg_yb", yb[:], f"yacc{dc}.{NST * DT - 1}"),
                        ("dbg_gate", ub[:], f"gate{dc}.{DT-1}"),
                        ("dbg_wobuf", wobuf[:, 0:DT * 128], f"wo{dc}.0"),
                        ("dbg_ofm", ofm[:, 0:512], f"ofmev{dc}.0"),
                        ("dbg_Asb", A_sb[:], f"gate{dc}.{DT-1}"),
                        ("dbg_carry", carry[:], f"gate{dc}.{DT-1}"),
                    ]:
                        S.wait("sync", mk)
                        S.op("sync", "dma_start", out=dbg[nm], in_=src,
                             inc=("qd", 16), mark=f"dump_{nm}")

                if PARANOID:
                    S.barrier(("oproj", c))
                S.wait("sync", f"oadd{c}.{MT-1}.3")
                S.op("sync", "dma_start",
                     out=acc1[c * LC:(c + 1) * LC, :].rearrange("(t p) d -> p t d", p=128),
                     in_=xt[:].rearrange("p (t d) -> p t d", t=4),
                     inc=("qs", 16), mark=f"store{c}")

            S.wait("sync", f"store{NCH-1}")
            if debug:
                S.wait("sync", "dump_dbg_carry")

        return schedule


    return prep_mamba_weights, make_mamba_phase, E, NST, R, LC, DT


_prep_mamba_weights, _make_mamba_phase, _MA_E, _MA_NST, _MA_R, _MA_LC, _MA_DT = _mamba_mod()


def _moe0_mod():

    F32 = mybir.dt.float32
    BF16 = mybir.dt.bfloat16
    AF = mybir.ActivationFunctionType
    ALU = mybir.AluOpType
    AX = mybir.AxisListType.X

    L, D = 4096, 1024
    NE = 16
    IT = 11
    KD = D // 128
    MT = D // 128
    TH = L // 2
    NTT = TH // 128
    TB = TH // 512
    NXP = NE + 2
    EPS = 1e-5
    POISON = False


    def prep_moe0_weights(ins):
        f32 = np.float32
        g = lambda k: np.asarray(ins[k])
        w = {}
        w["m_gate"] = np.ascontiguousarray(g("L0_ffn_gate_w")).astype(f32)
        w["m_gateb"] = np.ascontiguousarray(g("L0_ffn_gate_b")[None, :]).astype(f32)
        w1 = np.concatenate([g("L0_ffn_w1"),
                             g("L0_ffn_sw1").T.reshape(2, 1408, D).transpose(0, 2, 1)], 0)
        w3 = np.concatenate([g("L0_ffn_w3"),
                             g("L0_ffn_sw3").T.reshape(2, 1408, D).transpose(0, 2, 1)], 0)
        w2 = np.concatenate([g("L0_ffn_w2"),
                             g("L0_ffn_sw2").reshape(2, 1408, D)], 0)
        w["m_w1"] = np.ascontiguousarray(w1).astype(f32)
        w["m_w3"] = np.ascontiguousarray(w3).astype(f32)
        w["m_w2"] = np.ascontiguousarray(w2).astype(f32)
        b1 = np.concatenate([g("L0_ffn_b1"), g("L0_ffn_sb1").reshape(2, 1408)], 0)
        b3 = np.concatenate([g("L0_ffn_b3"), g("L0_ffn_sb3").reshape(2, 1408)], 0)
        w["m_b1"] = np.ascontiguousarray(
            b1.reshape(NXP, IT, 128).transpose(2, 0, 1).reshape(128, NXP * IT)).astype(f32)
        w["m_b3"] = np.ascontiguousarray(
            b3.reshape(NXP, IT, 128).transpose(2, 0, 1).reshape(128, NXP * IT)).astype(f32)
        b2 = np.concatenate([g("L0_ffn_b2"), g("L0_ffn_sb2")[None, :],
                             np.zeros((1, D), f32)], 0)
        w["m_b2"] = np.ascontiguousarray(b2.reshape(1, NXP * D)).astype(f32)
        w["m_ident"] = np.eye(128, dtype=ml_dtypes.bfloat16)
        return w


    def make_moe0_phase(nc, sch, alloc, P, io):
        x = io["x"]; acc = io["acc"]; out = io["out"]
        gate = io["m_gate"]; gateb = io["m_gateb"]
        w1 = io["m_w1"]; w3 = io["m_w3"]; w2 = io["m_w2"]
        b1 = io["m_b1"]; b3 = io["m_b3"]; b2 = io["m_b2"]
        ident_d = io["m_ident"]; d_cw = io["m_d_cw"]
        A = alloc
        xq = A("xq", [128, 4 * D], F32)
        xnb = A("xnb", [128, D], BF16)
        xnT = A("xnT", [128, KD * TH], BF16)
        yac = A("yac", [128, MT * TH], BF16)
        w13b = A("w13b", [128, IT * 2 * KD * 128], BF16)
        w2r = A("w2r", [128, MT * IT * 128], BF16)
        abuf = A("abuf", [128, IT * 512], BF16)
        hsb = A("hsb", [128, 2 * 512], BF16)
        cwt = A("cwt", [128, NTT * NE], F32)
        cwb16 = A("cwb16", [128, NE], BF16)
        cwT = A("cwT", [16, NTT * 128], F32)
        cwbc = A("cwbc", [128, 2 * TH], BF16)
        cwrow = A("cwrow", [1, 2 * TH], BF16)
        onesr = A("onesr", [1, TH], BF16)
        gb_sb = A("gb_sb", [128, NE], F32)
        gate_sb = A("gate_sb", [128, NE * KD], BF16)
        b1_sb = A("b1_sb", [128, NXP * IT], F32)
        b3_sb = A("b3_sb", [128, NXP * IT], F32)
        b2row = A("b2row", [1, 2 * D], BF16)
        id_sb = A("id_sb", [128, 128], BF16)
        lnst = A("lnst", [128, 16], F32)
        gst = A("gst", [128, 8], F32)
        msk = A("msk", [128, NE], F32)
        eps_sb = A("eps_sb", [128, 1], F32)
        nbig_sb = A("nbig_sb", [128, 1], F32)
        sq_f = A("sq_f", [128, D], F32)


        pT = [P[6][:, 0:64].bitcast(BF16), P[7][:, 0:64].bitcast(BF16)]
        pH = [P[0], P[1]]
        pG = [P[2], P[3]]
        pY = [P[4], P[5]]

        LSUM, LSQ, NEGM, M2, VAR, SD, RSTD = 0, 2, 4, 6, 8, 10, 12

        def schedule(S):
            S.op("gp", "dma_start", out=gb_sb[:], in_=gateb.broadcast_to([128, NE]),
                 inc=("qc", 16))
            S.op("gp", "dma_start",
                 out=gate_sb[:].rearrange("p (k n) -> p k n", k=KD),
                 in_=gate.rearrange("(k p) n -> p k n", p=128), inc=("qc", 16))
            S.op("gp", "dma_start", out=b1_sb[:], in_=b1, inc=("qc", 16))
            S.op("gp", "dma_start", out=b3_sb[:], in_=b3, inc=("qc", 16))
            S.op("gp", "dma_start", out=id_sb[:], in_=ident_d, inc=("qc", 16),
                 mark="consts")
            S.wait("v", "consts")
            S.op("v", "memset", eps_sb[:], EPS)
            S.op("v", "memset", nbig_sb[:], -100.0)
            S.op("v", "memset", onesr[:], 1.0, inc="sv", mark="pre")

            for hf in range(2):
                t0 = hf * TH
                for q in range(4):
                    if hf > 0 or q > 0:
                        ph, pq = (hf, q - 1) if q > 0 else (hf - 1, 3)
                        S.wait("sync", f"xtr{ph}.{pq * 4 + 3}.7")
                        if q == 0:
                            S.wait("sync", f"st{hf-1}.3")
                    S.op("sync", "dma_start",
                         out=xq[:].rearrange("p (t d) -> p t d", t=4),
                         in_=x[t0 + q * 512: t0 + (q + 1) * 512, :]
                             .rearrange("(t p) d -> p t d", p=128),
                         inc=("qx", 16), mark=f"x{hf}.{q}")
                    for tl in range(4):
                        t = q * 4 + tl
                        S.wait("s", f"x{hf}.{q}")
                        if hf > 0 or t >= 1:
                            ph, pt = (hf, t - 1) if t >= 1 else (hf - 1, NTT - 1)
                            S.wait("s", f"var{ph}.{pt}")
                        xsl = xq[:, tl * D:(tl + 1) * D]
                        S.op("s", "activation", sq_f[:, 0:D], xsl, AF.Square,
                             inc="sS", mark=f"lnstat{hf}.{t}")
                        S.wait("v", f"lnstat{hf}.{t}")
                        S.wait("v", f"x{hf}.{q}")
                        S.op("v", "reduce_sum", lnst[:, LSUM:LSUM + 1], xsl, axis=AX)
                        S.op("v", "reduce_sum", lnst[:, LSQ:LSQ + 1], sq_f[:, 0:D],
                             axis=AX, inc="sv", mark=f"lnred{hf}.{t}")
                        S.wait("v", f"lnred{hf}.{t}")
                        S.op("v", "tensor_scalar_mul", lnst[:, NEGM:NEGM + 1],
                             lnst[:, LSUM:LSUM + 1], -1.0 / D, inc="sv", mark=f"negm{hf}.{t}")
                        S.wait("v", f"negm{hf}.{t}")
                        S.op("v", "tensor_mul", lnst[:, M2:M2 + 1],
                             lnst[:, NEGM:NEGM + 1], lnst[:, NEGM:NEGM + 1],
                             inc="sv", mark=f"m2_{hf}.{t}")
                        S.wait("v", f"m2_{hf}.{t}")
                        S.op("v", "scalar_tensor_tensor", lnst[:, VAR:VAR + 1],
                             lnst[:, LSQ:LSQ + 1], 1.0 / D, lnst[:, M2:M2 + 1],
                             ALU.mult, ALU.subtract, inc="sv", mark=f"var{hf}.{t}")
                        S.wait("s", f"var{hf}.{t}")
                        S.op("s", "activation", lnst[:, SD:SD + 1], lnst[:, VAR:VAR + 1],
                             AF.Sqrt, bias=eps_sb[:, 0:1], inc="sS", mark=f"sd{hf}.{t}")
                        S.wait("v", f"sd{hf}.{t}")
                        if hf > 0 or t >= 1:
                            ph, pt = (hf, t - 1) if t >= 1 else (hf - 1, NTT - 1)
                            S.wait("v", f"xtr{ph}.{pt}.7")
                        S.op("v", "reciprocal", lnst[:, RSTD:RSTD + 1],
                             lnst[:, SD:SD + 1], inc="sv", mark=f"rstd{hf}.{t}")
                        S.wait("v", f"rstd{hf}.{t}")
                        S.op("v", "tensor_scalar", xnb[:, 0:D], xsl,
                             lnst[:, NEGM:NEGM + 1], lnst[:, RSTD:RSTD + 1],
                             ALU.add, ALU.mult, inc="sv", mark=f"xn{hf}.{t}")
                        S.wait("t", f"xn{hf}.{t}")
                        if hf == 0 and t == 0:
                            S.wait("t", "consts")
                        for kd in range(KD):
                            idx = 8 * t + kd
                            if idx >= 2:
                                S.wait("t", f"trev{hf}.{(idx-2) // 8}.{(idx-2) % 8}")
                            elif hf > 0:
                                S.wait("t", f"fadd{hf-1}.{NTT-1}.{7}")
                            for_mark = f"xtr{hf}.{t}.{kd}"
                            S.op("t", "transpose", pT[idx % 2][:],
                                 xnb[:, kd * 128:(kd + 1) * 128], id_sb[:],
                                 inc="pe", mark=for_mark)
                            S.wait("s", for_mark)
                            S.op("s", "copy",
                                 xnT[:, kd * TH + t * 128: kd * TH + (t + 1) * 128],
                                 pT[idx % 2][:], inc="sS", mark=f"trev{hf}.{t}.{kd}")

                for t in range(NTT):
                    S.wait("t", f"trev{hf}.{t}.7")
                    if hf > 0 and t == 0:
                        S.wait("t", f"yadd{hf-1}.{NXP-1}.{TB-1}.{MT-1}")
                    if t >= 1:
                        S.wait("t", f"glog{hf}.{t-1}")
                    for kd in range(KD):
                        S.op("t", "matmul", pY[0][:, 0:NE],
                             xnT[:, kd * TH + t * 128: kd * TH + (t + 1) * 128],
                             gate_sb[:, kd * NE:(kd + 1) * NE],
                             start=(kd == 0), stop=(kd == KD - 1),
                             inc="pe" if kd == KD - 1 else None,
                             mark=f"gmm{hf}.{t}" if kd == KD - 1 else None)
                    S.wait("v", f"gmm{hf}.{t}")
                    csl = cwt[:, t * NE:(t + 1) * NE]
                    S.op("v", "tensor_add", csl, pY[0][:, 0:NE], gb_sb[:],
                         inc="sv", mark=f"glog{hf}.{t}")
                    S.wait("v", f"glog{hf}.{t}")
                    S.op("v", "reduce_max", gst[:, 0:1], csl, axis=AX, negate=True,
                         inc="sv", mark=f"gmax{hf}.{t}")
                    S.wait("s", f"gmax{hf}.{t}")
                    S.op("s", "activation", csl, csl, AF.Exp, bias=gst[:, 0:1],
                         inc="sS", mark=f"gexp{hf}.{t}")
                    S.wait("v", f"gexp{hf}.{t}")
                    S.op("v", "reduce_sum", gst[:, 1:2], csl, axis=AX,
                         inc="sv", mark=f"gsum{hf}.{t}")
                    S.wait("v", f"gsum{hf}.{t}")
                    S.op("v", "reciprocal", gst[:, 2:3], gst[:, 1:2],
                         inc="sv", mark=f"grec{hf}.{t}")
                    S.wait("v", f"grec{hf}.{t}")
                    S.op("v", "tensor_scalar_mul", csl, csl, gst[:, 2:3],
                         inc="sv", mark=f"gsc{hf}.{t}")
                    S.wait("v", f"gsc{hf}.{t}")
                    S.op("v", "reduce_max", gst[:, 3:4], csl, axis=AX,
                         inc="sv", mark=f"gm1{hf}.{t}")
                    S.wait("v", f"gm1{hf}.{t}")
                    S.op("v", "tensor_single_scalar", msk[:], csl, gst[:, 3:4],
                         ALU.is_equal, inc="sv", mark=f"geq{hf}.{t}")
                    S.wait("v", f"geq{hf}.{t}")
                    S.op("v", "scalar_tensor_tensor", msk[:], msk[:],
                         nbig_sb[:, 0:1], csl, ALU.mult, ALU.add,
                         inc="sv", mark=f"gmsk{hf}.{t}")
                    S.wait("v", f"gmsk{hf}.{t}")
                    S.op("v", "reduce_max", gst[:, 5:6], msk[:], axis=AX,
                         inc="sv", mark=f"gm2{hf}.{t}")
                    S.wait("v", f"gm2{hf}.{t}")
                    S.op("v", "tensor_single_scalar", msk[:], csl, gst[:, 5:6],
                         ALU.is_ge, inc="sv", mark=f"gge{hf}.{t}")
                    S.wait("v", f"gge{hf}.{t}")
                    S.op("v", "tensor_mul", csl, csl, msk[:],
                         inc="sv", mark=f"gcw{hf}.{t}")
                    S.wait("v", f"gcw{hf}.{t}")
                    S.op("v", "tensor_copy", cwb16[:], csl, inc="sv", mark=f"gcb{hf}.{t}")
                    S.wait("t", f"gcb{hf}.{t}")
                    if t >= 2:
                        S.wait("t", f"cwev{hf}.{t-2}")
                    else:
                        S.wait("t", f"trev{hf}.{NTT-1}.{6 + t}")
                    S.op("t", "transpose", pT[t % 2][0:16, :], cwb16[:], id_sb[:],
                         inc="pe", mark=f"ctr{hf}.{t}")
                    S.wait("s", f"ctr{hf}.{t}")
                    S.op("s", "copy", cwT[:, t * 128:(t + 1) * 128], pT[t % 2][0:16, :],
                         inc="sS", mark=f"cwev{hf}.{t}")
                S.wait("gp", f"cwev{hf}.{NTT-1}")
                if hf > 0:
                    S.wait("gp", f"cwb{hf-1}.{NE-1}")
                S.op("gp", "dma_start", out=d_cw, in_=cwT[:], inc=("qcw", 16),
                     mark=f"cwd{hf}")

                for ei in range(NXP):
                    epar = ei % 2
                    shared = ei >= NE
                    if not shared:
                        S.wait("sync", f"cwd{hf}")
                    if hf > 0 or ei >= 2:
                        ph, pei = (hf, ei - 2) if ei >= 2 else (hf - 1, NXP - 2 + ei)
                        S.wait("sync", f"yadd{ph}.{pei}.{TB-1}.{MT-1}")
                    if not shared:
                        S.op("sync", "dma_start", out=cwbc[:, epar * TH:(epar + 1) * TH],
                             in_=d_cw[ei:ei + 1, :].broadcast_to([128, TH]),
                             inc=(f"qbc{epar}", 16))
                        S.op("sync", "dma_start", out=cwrow[:, epar * TH:(epar + 1) * TH],
                             in_=d_cw[ei:ei + 1, :], inc=(f"qbc{epar}", 16))
                    if hf > 0 or ei >= 2:
                        S.wait("gp", f"yadd{ph}.{pei}.{TB-1}.{MT-1}")
                    S.op("gp", "dma_start", out=b2row[0:1, epar * D:(epar + 1) * D],
                         in_=b2[0:1, ei * D:(ei + 1) * D], inc=(f"qbc{epar}", 16),
                         mark=f"cwb{hf}.{ei}")
                    for it in range(IT):
                        if hf > 0 or ei > 0:
                            ph, pei = (hf, ei - 1) if ei > 0 else (hf - 1, NXP - 1)
                            S.wait("gp", f"hgmm{ph}.{pei}.{TB-1}.{it}")
                        if it >= 2:
                            S.wait("gp", f"hgmm{hf}.{ei}.0.{it-2}")
                        wsl = w13b[:, it * 2 * KD * 128:(it * 2 + 2) * KD * 128]
                        S.op("gp", "dma_start",
                             out=wsl[:, 0:KD * 128].rearrange("p (k e) -> p k e", k=KD),
                             in_=w1[ei, :, it * 128:(it + 1) * 128]
                                 .rearrange("(k p) e -> p k e", p=128),
                             inc=(f"qw{it % 2}", 16))
                        S.op("gp", "dma_start",
                             out=wsl[:, KD * 128:2 * KD * 128].rearrange("p (k e) -> p k e", k=KD),
                             in_=w3[ei, :, it * 128:(it + 1) * 128]
                                 .rearrange("(k p) e -> p k e", p=128),
                             inc=(f"qw{it % 2}", 16), mark=f"w13{hf}.{ei}.{it}")
                    for dt in range(MT):
                        if hf > 0 or ei > 0:
                            ph, pei = (hf, ei - 1) if ei > 0 else (hf - 1, NXP - 1)
                            S.wait("gp", f"w2mm{ph}.{pei}.{TB-1}.{dt}")
                        if dt >= 2:
                            S.wait("gp", f"w2mm{hf}.{ei}.0.{dt-2}")
                        S.op("gp", "dma_start",
                             out=w2r[:, dt * IT * 128:(dt + 1) * IT * 128]
                                 .rearrange("p (k m) -> p k m", k=IT),
                             in_=w2[ei, :, dt * 128:(dt + 1) * 128]
                                 .rearrange("(k p) m -> p k m", p=128),
                             inc=(f"qv{dt % 2}", 16), mark=f"w2l{hf}.{ei}.{dt}")
                    for tb in range(TB):
                        for it in range(IT):
                            ipar = it % 2
                            wsl = w13b[:, it * 2 * KD * 128:(it * 2 + 2) * KD * 128]
                            S.wait("t", f"w13{hf}.{ei}.{it}")
                            if hf == 0 and ei == 0 and tb == 0:
                                S.wait("t", f"cwev{hf}.{NTT-1}")
                            if hf > 0 or ei > 0 or tb > 0 or it >= 2:
                                pit, ph, pei, ptb = it - 2, hf, ei, tb
                                if pit < 0:
                                    ptb = tb - 1
                                    if ptb < 0:
                                        pei, ptb = ei - 1, TB - 1
                                        if pei < 0:
                                            ph, pei = hf - 1, NXP - 1
                                    pit = IT + pit
                                S.wait("t", f"aprime{ph}.{pei}.{ptb}.{pit}")
                            for kd in range(KD):
                                S.op("t", "matmul", pH[ipar][:],
                                     wsl[:, kd * 128:(kd + 1) * 128],
                                     xnT[:, kd * TH + tb * 512: kd * TH + (tb + 1) * 512],
                                     start=(kd == 0), stop=(kd == KD - 1))
                            for kd in range(KD):
                                S.op("t", "matmul", pG[ipar][:],
                                     wsl[:, KD * 128 + kd * 128: KD * 128 + (kd + 1) * 128],
                                     xnT[:, kd * TH + tb * 512: kd * TH + (tb + 1) * 512],
                                     start=(kd == 0), stop=(kd == KD - 1),
                                     inc="pe" if kd == KD - 1 else None,
                                     mark=f"hgmm{hf}.{ei}.{tb}.{it}" if kd == KD - 1 else None)
                            S.wait("s", f"hgmm{hf}.{ei}.{tb}.{it}")
                            if hf > 0 or ei > 0 or tb > 0 or it >= 2:
                                S.wait("s", f"aprime{ph}.{pei}.{ptb}.{pit}")
                            hss = hsb[:, ipar * 512:(ipar + 1) * 512]
                            S.op("s", "activation", hss, pH[ipar][:], AF.Silu,
                                 bias=b1_sb[:, ei * IT + it: ei * IT + it + 1],
                                 inc="sS", mark=f"hsil{hf}.{ei}.{tb}.{it}")
                            S.wait("v", f"hsil{hf}.{ei}.{tb}.{it}")
                            if not shared:
                                S.wait("v", f"cwb{hf}.{ei}")
                            if hf > 0 or ei > 0 or tb > 0:
                                ptb2, ph2, pei2 = tb - 1, hf, ei
                                if ptb2 < 0:
                                    pei2 = ei - 1
                                    ptb2 = TB - 1
                                    if pei2 < 0:
                                        ph2, pei2 = hf - 1, NXP - 1
                                S.wait("v", f"w2mm{ph2}.{pei2}.{ptb2}.{MT-1}")
                            asl = abuf[:, it * 512:(it + 1) * 512]
                            S.op("v", "scalar_tensor_tensor", asl, pG[ipar][:],
                                 b3_sb[:, ei * IT + it: ei * IT + it + 1], hss,
                                 ALU.add, ALU.mult,
                                 inc=None if not shared else "sv",
                                 mark=None if not shared else f"aprime{hf}.{ei}.{tb}.{it}")
                            if not shared:
                                S.op("v", "tensor_mul", asl, asl,
                                     cwbc[:, epar * TH + tb * 512: epar * TH + (tb + 1) * 512],
                                     inc="sv", mark=f"aprime{hf}.{ei}.{tb}.{it}")
                        for dt in range(MT):
                            dpar = dt % 2
                            S.wait("t", f"w2l{hf}.{ei}.{dt}")
                            S.wait("t", f"aprime{hf}.{ei}.{tb}.{IT-1}")
                            if hf > 0 or ei > 0 or tb > 0 or dt >= 2:
                                pdt, ph3, pei3, ptb3 = dt - 2, hf, ei, tb
                                if pdt < 0:
                                    ptb3 = tb - 1
                                    if ptb3 < 0:
                                        pei3, ptb3 = ei - 1, TB - 1
                                        if pei3 < 0:
                                            ph3, pei3 = hf - 1, NXP - 1
                                    pdt = MT + pdt
                                S.wait("t", f"yadd{ph3}.{pei3}.{ptb3}.{pdt}")
                            for it in range(IT):
                                S.op("t", "matmul", pY[dpar][:],
                                     w2r[:, dt * IT * 128 + it * 128: dt * IT * 128 + (it + 1) * 128],
                                     abuf[:, it * 512:(it + 1) * 512],
                                     start=(it == 0), stop=False)
                            rrow = (onesr[0:1, tb * 512:(tb + 1) * 512] if shared
                                    else cwrow[0:1, epar * TH + tb * 512: epar * TH + (tb + 1) * 512])
                            S.wait("t", f"cwb{hf}.{ei}")
                            S.op("t", "matmul", pY[dpar][:],
                                 b2row[0:1, epar * D + dt * 128: epar * D + (dt + 1) * 128],
                                 rrow, start=False, stop=True,
                                 inc="pe", mark=f"w2mm{hf}.{ei}.{tb}.{dt}")
                            S.wait("v", f"w2mm{hf}.{ei}.{tb}.{dt}")
                            ysl = yac[:, dt * TH + tb * 512: dt * TH + (tb + 1) * 512]
                            if ei == 0:
                                if hf > 0:
                                    S.wait("v", f"ftr{hf-1}.{NTT-1}.{MT-1}")
                                S.op("v", "tensor_copy", ysl, pY[dpar][:],
                                     inc="sv", mark=f"yadd{hf}.{ei}.{tb}.{dt}")
                            else:
                                S.op("v", "tensor_add", ysl, pY[dpar][:], ysl,
                                     inc="sv", mark=f"yadd{hf}.{ei}.{tb}.{dt}")

                for q in range(4):
                    S.wait("sync", f"xtr{hf}.{NTT-1}.{7}")
                    if q > 0:
                        S.wait("sync", f"st{hf}.{q-1}")
                    S.op("sync", "dma_start",
                         out=xq[:].rearrange("p (t d) -> p t d", t=4),
                         in_=acc[t0 + q * 512: t0 + (q + 1) * 512, :]
                             .rearrange("(t p) d -> p t d", p=128),
                         inc=("qa", 16), mark=f"a{hf}.{q}")
                    for tl in range(4):
                        t = q * 4 + tl
                        for dt in range(MT):
                            idx = 8 * t + dt
                            S.wait("t", f"yadd{hf}.{NXP-1}.{t // 4}.{dt}")
                            if idx >= 2:
                                S.wait("t", f"fadd{hf}.{(idx-2) // 8}.{(idx-2) % 8}")
                            elif hf == 0 and q == 0:
                                S.wait("t", f"cwev{hf}.{NTT-1}")
                            else:
                                pass
                            S.op("t", "transpose", pT[idx % 2][:],
                                 yac[:, dt * TH + t * 128: dt * TH + (t + 1) * 128],
                                 id_sb[:], inc="pe", mark=f"ftr{hf}.{t}.{dt}")
                            S.wait("v", f"ftr{hf}.{t}.{dt}")
                            S.wait("v", f"a{hf}.{q}")
                            dst = xq[:, tl * D + dt * 128: tl * D + (dt + 1) * 128]
                            S.op("v", "tensor_add", dst, pT[idx % 2][:], dst,
                                 inc="sv", mark=f"fadd{hf}.{t}.{dt}")
                    S.wait("sync", f"fadd{hf}.{q * 4 + 3}.{7}")
                    S.op("sync", "dma_start",
                         out=out[t0 + q * 512: t0 + (q + 1) * 512, :]
                             .rearrange("(t p) d -> p t d", p=128),
                         in_=xq[:].rearrange("p (t d) -> p t d", t=4),
                         inc=("qs", 16), mark=f"st{hf}.{q}")

            S.wait("sync", f"st{1}.3")

        return schedule


    return prep_moe0_weights, make_moe0_phase, NE, NXP, IT, TH


_prep_moe0_weights, _make_moe0_phase, _MO_NE, _MO_NXP, _MO_IT, _MO_TH = _moe0_mod()


def _attn1_mod():

    F32 = mybir.dt.float32
    BF16 = mybir.dt.bfloat16
    AF = mybir.ActivationFunctionType
    ALU = mybir.AluOpType
    AX = mybir.AxisListType.X

    L, D = 4096, 1024
    NH, HD = 16, 64
    LC = 512
    NCH = L // LC
    KD = D // 128
    OT = D // 128
    EPS = 1e-5
    ATT_EPS = 1e-12
    POISON = False


    def prep_attn1_weights(ins):
        f32 = np.float32
        g = lambda k: np.asarray(ins[k])
        w = {}
        w["a_wqT"] = np.ascontiguousarray(g("L1_attn_Wq").T).astype(f32)
        w["a_wkT"] = np.ascontiguousarray(g("L1_attn_Wk").T).astype(f32)
        w["a_wvT"] = np.ascontiguousarray(g("L1_attn_Wv").T).astype(f32)
        w["a_woT"] = np.ascontiguousarray(g("L1_attn_Wo").T).astype(f32)
        w["a_ident"] = np.eye(128, dtype=ml_dtypes.bfloat16)
        w["a_maskT"] = np.triu(np.ones((128, 128))).astype(ml_dtypes.bfloat16)
        return w


    def make_attn1_phase(nc, sch, alloc, P, io):
        x = io["x"]; out = io["out"]
        wqT = io["a_wqT"]; wkT = io["a_wkT"]; wvT = io["a_wvT"]; woT = io["a_woT"]
        ident_d = io["a_ident"]; mask_d = io["a_maskT"]
        A = alloc
        xt = A("xt", [128, 4 * D], F32)
        xnb = A("xnb", [128, D], BF16)
        xnT = A("xnT", [128, KD * LC], BF16)
        wq_sb = A("wq_sb", [128, KD * D], BF16)
        wk_sb = A("wk_sb", [128, KD * D], BF16)
        wv_sb = A("wv_sb", [128, KD * D], BF16)
        wo_sb = A("wo_sb", [128, KD * D], BF16)
        qTb = A("qTb", [128, OT * LC], BF16)
        kb = A("kb", [128, 4 * D], BF16)
        kTb = A("kTb", [128, OT * LC], BF16)
        vb = A("vb", [128, 4 * D], BF16)
        attnT = A("attnT", [128, OT * LC], BF16)
        Sst_f = A("Sst_f", [128, (NH // 2) * HD], F32)
        Sst_b = A("Sst_b", [128, (NH // 2) * HD], BF16)
        z_f = A("z_f", [128, NH // 2], F32)
        zb = A("zb", [128, NH // 2], BF16)
        ones_sb = A("ones_sb", [128, 1], BF16)
        ones64 = A("ones64", [1, 64], BF16)
        scm = A("scm", [128, 128], BF16)
        dbc = A("dbc", [128, 128], F32)
        denr = A("denr", [1, 128], F32)
        denrb = A("denrb", [1, 128], BF16)
        otmp = A("otmp", [128, 128], F32)
        qf = A("qf", [128, 512], F32)
        tmin = A("tmin", [128, 512], F32)
        tmax = A("tmax", [128, 512], F32)
        mask_sb = A("mask_sb", [128, 128], BF16)
        id_sb = A("id_sb", [128, 128], BF16)
        lnst = A("lnst", [128, 16], F32)
        eps_sb = A("eps_sb", [128, 1], F32)
        e12_sb = A("e12_sb", [128, 1], F32)
        sq_f = A("sq_f", [128, D], F32)
        ofm = A("ofm", [128, 2 * 512], BF16)


        pQ = [P[0], P[1]]
        pT = P[6][:, 0:64].bitcast(BF16)
        pSc = P[2][:, 0:128]
        pN = P[3][:, 0:128]
        pden = P[4][0:1, 0:128]
        pDb = P[5][:, 0:128]
        pSt = P[7][:, 0:64]

        LSUM, LSQ, NEGM, M2, VAR, SD, RSTD = 0, 2, 4, 6, 8, 10, 12

        def schedule(S):
            for nm, dst, src in [("wq", wq_sb, wqT), ("wk", wk_sb, wkT),
                                 ("wv", wv_sb, wvT), ("wo", wo_sb, woT)]:
                S.op("gp", "dma_start",
                     out=dst[:].rearrange("p (k o) -> p k o", k=KD),
                     in_=src.rearrange("(k p) o -> p k o", p=128), inc=("qc", 16))
            S.op("gp", "dma_start", out=id_sb[:], in_=ident_d, inc=("qc", 16))
            S.op("gp", "dma_start", out=mask_sb[:], in_=mask_d, inc=("qc", 16),
                 mark="consts")
            S.wait("v", "consts")
            S.op("v", "memset", eps_sb[:], EPS)
            S.op("v", "memset", e12_sb[:], ATT_EPS)
            S.op("v", "memset", ones_sb[:], 1.0)
            S.op("v", "memset", ones64[:], 1.0)
            S.op("v", "memset", Sst_f[:], 0.0)
            S.op("v", "memset", Sst_b[:], 0.0)
            S.op("v", "memset", z_f[:], 0.0)
            S.op("v", "memset", zb[:], 0.0, inc="sv", mark="pre")

            for c in range(NCH):
                if c > 0:
                    S.wait("sync", f"st{c-1}")
                S.op("sync", "dma_start",
                     out=xt[:].rearrange("p (t d) -> p t d", t=4),
                     in_=x[c * LC:(c + 1) * LC, :].rearrange("(t p) d -> p t d", p=128),
                     inc=("qx", 16), mark=f"x{c}")
                for t in range(4):
                    S.wait("s", f"x{c}")
                    if c > 0 or t >= 1:
                        pc, pt_ = (c, t - 1) if t >= 1 else (c - 1, 3)
                        S.wait("s", f"var{pc}.{pt_}")
                    xsl = xt[:, t * D:(t + 1) * D]
                    S.op("s", "activation", sq_f[:, 0:D], xsl, AF.Square,
                         inc="sS", mark=f"lnstat{c}.{t}")
                    S.wait("v", f"lnstat{c}.{t}")
                    S.wait("v", f"x{c}")
                    S.op("v", "reduce_sum", lnst[:, LSUM:LSUM + 1], xsl, axis=AX)
                    S.op("v", "reduce_sum", lnst[:, LSQ:LSQ + 1], sq_f[:, 0:D],
                         axis=AX, inc="sv", mark=f"lnred{c}.{t}")
                    S.wait("v", f"lnred{c}.{t}")
                    S.op("v", "tensor_scalar_mul", lnst[:, NEGM:NEGM + 1],
                         lnst[:, LSUM:LSUM + 1], -1.0 / D, inc="sv", mark=f"negm{c}.{t}")
                    S.wait("v", f"negm{c}.{t}")
                    S.op("v", "tensor_mul", lnst[:, M2:M2 + 1],
                         lnst[:, NEGM:NEGM + 1], lnst[:, NEGM:NEGM + 1],
                         inc="sv", mark=f"m2_{c}.{t}")
                    S.wait("v", f"m2_{c}.{t}")
                    S.op("v", "scalar_tensor_tensor", lnst[:, VAR:VAR + 1],
                         lnst[:, LSQ:LSQ + 1], 1.0 / D, lnst[:, M2:M2 + 1],
                         ALU.mult, ALU.subtract, inc="sv", mark=f"var{c}.{t}")
                    S.wait("s", f"var{c}.{t}")
                    S.op("s", "activation", lnst[:, SD:SD + 1], lnst[:, VAR:VAR + 1],
                         AF.Sqrt, bias=eps_sb[:, 0:1], inc="sS", mark=f"sd{c}.{t}")
                    S.wait("v", f"sd{c}.{t}")
                    if c > 0 or t >= 1:
                        pc, pt_ = (c, t - 1) if t >= 1 else (c - 1, 3)
                        S.wait("v", f"xtr{pc}.{pt_}.7")
                    S.op("v", "reciprocal", lnst[:, RSTD:RSTD + 1],
                         lnst[:, SD:SD + 1], inc="sv", mark=f"rstd{c}.{t}")
                    S.wait("v", f"rstd{c}.{t}")
                    S.op("v", "tensor_scalar", xnb[:, 0:D], xsl,
                         lnst[:, NEGM:NEGM + 1], lnst[:, RSTD:RSTD + 1],
                         ALU.add, ALU.mult, inc="sv", mark=f"xn{c}.{t}")
                    S.wait("t", f"xn{c}.{t}")
                    if c == 0 and t == 0:
                        S.wait("t", "consts")
                    for kd in range(KD):
                        idx = 8 * t + kd
                        if idx >= 1:
                            S.wait("t", f"trev{c}.{(idx-1) // 8}.{(idx-1) % 8}")
                        elif c > 0:
                            S.wait("t", f"oadd{c-1}.{OT-1}.3")
                        S.op("t", "transpose", pT[:],
                             xnb[:, kd * 128:(kd + 1) * 128], id_sb[:],
                             inc="pe", mark=f"xtr{c}.{t}.{kd}")
                        S.wait("s", f"xtr{c}.{t}.{kd}")
                        S.op("s", "copy",
                             xnT[:, kd * LC + t * 128: kd * LC + (t + 1) * 128],
                             pT[:], inc="sS", mark=f"trev{c}.{t}.{kd}")

                for ot in range(OT):
                    par = ot % 2
                    S.wait("t", f"trev{c}.3.7")
                    if ot >= 2:
                        S.wait("t", f"qel{c}.{ot - 2}")
                    elif c > 0:
                        S.wait("t", f"ofm{c-1}.{OT - 2 + ot}")
                    for kd in range(KD):
                        S.op("t", "matmul", pQ[par][:],
                             wq_sb[:, kd * D + ot * 128: kd * D + (ot + 1) * 128],
                             xnT[:, kd * LC:(kd + 1) * LC],
                             start=(kd == 0), stop=(kd == KD - 1),
                             inc="pe" if kd == KD - 1 else None,
                             mark=f"qmm{c}.{ot}" if kd == KD - 1 else None)
                    S.wait("v", f"qmm{c}.{ot}")
                    S.op("v", "tensor_copy", qf[:], pQ[par][:], inc="sv",
                         mark=f"qev{c}.{ot}")
                    S.wait("v", f"qev{c}.{ot}")
                    S.op("v", "tensor_scalar_min", tmin[:], qf[:], 0.0,
                         inc="sv", mark=f"qmin{c}.{ot}")
                    S.wait("s", f"qmin{c}.{ot}")
                    S.op("s", "activation", tmin[:], tmin[:], AF.Exp,
                         inc="sS", mark=f"qexp{c}.{ot}")
                    S.op("v", "tensor_scalar_max", tmax[:], qf[:], 0.0,
                         inc="sv", mark=f"qmax{c}.{ot}")
                    S.wait("v", f"qexp{c}.{ot}")
                    S.wait("v", f"qmax{c}.{ot}")
                    S.op("v", "tensor_add", qTb[:, ot * LC:(ot + 1) * LC],
                         tmin[:], tmax[:], inc="sv", mark=f"qel{c}.{ot}")

                for t in range(4):
                    par = t % 2
                    if c > 0 or t >= 1:
                        pass
                    S.wait("t", f"qel{c}.{OT-2}")
                    S.wait("t", f"qel{c}.{OT-1}")
                    if t >= 1:
                        S.wait("t", f"vev{c}.{t-1}.1")
                    elif c > 0:
                        S.wait("t", f"kel{c-1}.3")
                    for half in range(2):
                        if half == 1:
                            S.wait("t", f"kev{c}.{t}.0")
                        for kd in range(KD):
                            S.op("t", "matmul", pQ[par][:],
                                 xnT[:, kd * LC + t * 128: kd * LC + (t + 1) * 128],
                                 wk_sb[:, kd * D + half * 512: kd * D + (half + 1) * 512],
                                 start=(kd == 0), stop=(kd == KD - 1),
                                 inc="pe" if kd == KD - 1 else None,
                                 mark=f"kmm{c}.{t}.{half}" if kd == KD - 1 else None)
                        S.wait("v", f"kmm{c}.{t}.{half}")
                        S.op("v", "tensor_copy", qf[:], pQ[par][:], inc="sv",
                             mark=f"kev{c}.{t}.{half}")
                        S.wait("v", f"kev{c}.{t}.{half}")
                        S.op("v", "tensor_scalar_min", tmin[:], qf[:], 0.0,
                             inc="sv", mark=f"kmin{c}.{t}.{half}")
                        S.wait("s", f"kmin{c}.{t}.{half}")
                        S.op("s", "activation", tmin[:], tmin[:], AF.Exp,
                             inc="sS", mark=f"kexp{c}.{t}.{half}")
                        S.op("v", "tensor_scalar_max", tmax[:], qf[:], 0.0,
                             inc="sv", mark=f"kmax{c}.{t}.{half}")
                        S.wait("v", f"kexp{c}.{t}.{half}")
                        S.wait("v", f"kmax{c}.{t}.{half}")
                        S.op("v", "tensor_add",
                             kb[:, t * D + half * 512: t * D + (half + 1) * 512],
                             tmin[:], tmax[:], inc="sv",
                             mark=f"kel{c}.{t}" if half == 1 else None)
                    vpar = 1 - par
                    if t >= 1:
                        S.wait("t", f"kev{c}.{t-1}.1")
                    elif c > 0:
                        S.wait("t", f"ofm{c-1}.{OT-1}")
                    for half in range(2):
                        if half == 1:
                            S.wait("t", f"vev{c}.{t}.0")
                        for kd in range(KD):
                            S.op("t", "matmul", pQ[vpar][:],
                                 xnT[:, kd * LC + t * 128: kd * LC + (t + 1) * 128],
                                 wv_sb[:, kd * D + half * 512: kd * D + (half + 1) * 512],
                                 start=(kd == 0), stop=(kd == KD - 1),
                                 inc="pe" if kd == KD - 1 else None,
                                 mark=f"vmm{c}.{t}.{half}" if kd == KD - 1 else None)
                        S.wait("s", f"vmm{c}.{t}.{half}")
                        S.op("s", "copy",
                             vb[:, t * D + half * 512: t * D + (half + 1) * 512],
                             pQ[vpar][:], inc="sS", mark=f"vev{c}.{t}.{half}")
                for t in range(4):
                    for kd in range(KD):
                        idx = 8 * t + kd
                        S.wait("t", f"kel{c}.{t}")
                        if idx >= 1:
                            S.wait("t", f"ktev{c}.{(idx-1) // 8}.{(idx-1) % 8}")
                        else:
                            S.wait("t", f"trev{c}.3.7")
                        S.op("t", "transpose", pT[:],
                             kb[:, t * D + kd * 128: t * D + (kd + 1) * 128],
                             id_sb[:], inc="pe", mark=f"ktr{c}.{t}.{kd}")
                        S.wait("s", f"ktr{c}.{t}.{kd}")
                        S.op("s", "copy",
                             kTb[:, kd * LC + t * 128: kd * LC + (t + 1) * 128],
                             pT[:], inc="sS", mark=f"ktev{c}.{t}.{kd}")

                for sb_ in range(4):
                    for h in range(NH):
                        it = sb_ * NH + h
                        ot, ph = h // 2, (h % 2) * 64
                        hc = h // 2
                        qsl = qTb[ph:ph + 64, ot * LC + sb_ * 128: ot * LC + (sb_ + 1) * 128]
                        ksl = kTb[ph:ph + 64, ot * LC + sb_ * 128: ot * LC + (sb_ + 1) * 128]
                        vsl = vb[:, sb_ * D + h * 64: sb_ * D + (h + 1) * 64]
                        kbsl = kb[:, sb_ * D + h * 64: sb_ * D + (h + 1) * 64]
                        S.wait("t", f"qel{c}.{ot}")
                        S.wait("t", f"ktev{c}.{sb_}.{7}")
                        if c > 0 or it >= 1:
                            pc2, pit = (c, it - 1) if it >= 1 else (c - 1, 63)
                            S.wait("t", f"scm{pc2}.{pit}")
                        S.op("t", "matmul", pSc[:], ksl, qsl, start=True, stop=True,
                             inc="pe", mark=f"sc{c}.{it}")
                        S.wait("v", f"sc{c}.{it}")
                        S.op("v", "tensor_mul", scm[:], pSc[:], mask_sb[:],
                             inc="sv", mark=f"scm{c}.{it}")
                        S.wait("t", f"scm{c}.{it}")
                        if c > 0 or sb_ >= 1:
                            pit2, pc2 = it - 16, c
                            if pit2 < 0:
                                pc2, pit2 = c - 1, 48 + h
                            S.wait("t", f"zbf{pc2}.{pit2}")
                            S.wait("t", f"sbf{pc2}.{pit2}")
                        if c > 0 or it >= 1:
                            pc2, pit = (c, it - 1) if it >= 1 else (c - 1, 63)
                            S.wait("t", f"denr{pc2}.{pit}")
                        S.op("t", "matmul", pden[:], zb[ph:ph + 64, hc:hc + 1], qsl,
                             start=True, stop=False)
                        S.op("t", "matmul", pden[:], ones_sb[:], scm[:],
                             start=False, stop=True, inc="pe", mark=f"den{c}.{it}")
                        if c > 0 or it >= 1:
                            pc2, pit = (c, it - 1) if it >= 1 else (c - 1, 63)
                            S.wait("t", f"oev{pc2}.{pit}")
                        S.op("t", "matmul", pN[ph:ph + 64, :],
                             Sst_b[ph:ph + 64, hc * HD:(hc + 1) * HD], qsl,
                             start=True, stop=False, tile_position=(ph, ph))
                        S.op("t", "matmul", pN[ph:ph + 64, :], vsl, scm[:],
                             start=False, stop=True, tile_position=(0, ph),
                             inc="pe", mark=f"num{c}.{it}")
                        S.wait("v", f"den{c}.{it}")
                        S.op("v", "reciprocal", denr[:], pden[:], inc="sv",
                             mark=f"denr{c}.{it}")
                        S.wait("v", f"denr{c}.{it}")
                        S.op("v", "tensor_copy", denrb[:], denr[:], inc="sv",
                             mark=f"denb{c}.{it}")
                        S.wait("t", f"denb{c}.{it}")
                        if c > 0 or it >= 1:
                            pc2, pit = (c, it - 1) if it >= 1 else (c - 1, 63)
                            S.wait("t", f"dbc{pc2}.{pit}")
                        S.op("t", "matmul", pDb[ph:ph + 64, :], ones64[:], denrb[:],
                             start=True, stop=True, tile_position=(0, ph),
                             inc="pe", mark=f"dbm{c}.{it}")
                        S.wait("v", f"dbm{c}.{it}")
                        S.op("v", "tensor_copy", dbc[ph:ph + 64, :], pDb[ph:ph + 64, :],
                             inc="sv", mark=f"dbc{c}.{it}")
                        S.wait("v", f"num{c}.{it}")
                        S.wait("v", f"dbc{c}.{it}")
                        S.op("v", "tensor_mul", otmp[ph:ph + 64, :], pN[ph:ph + 64, :],
                             dbc[ph:ph + 64, :], inc="sv", mark=f"omul{c}.{it}")
                        S.wait("v", f"omul{c}.{it}")
                        S.op("v", "tensor_scalar_add",
                             attnT[ph:ph + 64, ot * LC + sb_ * 128: ot * LC + (sb_ + 1) * 128],
                             otmp[ph:ph + 64, :], e12_sb[ph:ph + 64, 0:1],
                             inc="sv", mark=f"oev{c}.{it}")
                        if c > 0 or it >= 1:
                            pc2, pit = (c, it - 1) if it >= 1 else (c - 1, 63)
                            S.wait("t", f"sadd{pc2}.{pit}")
                        S.op("t", "matmul", pSt[ph:ph + 64, :], kbsl, vsl,
                             start=True, stop=True, tile_position=(0, ph),
                             inc="pe", mark=f"stu{c}.{it}")
                        S.wait("v", f"stu{c}.{it}")
                        S.op("v", "tensor_add", Sst_f[ph:ph + 64, hc * HD:(hc + 1) * HD],
                             pSt[ph:ph + 64, :], Sst_f[ph:ph + 64, hc * HD:(hc + 1) * HD],
                             inc="sv", mark=f"sadd{c}.{it}")
                        S.wait("v", f"sadd{c}.{it}")
                        S.op("v", "tensor_copy", Sst_b[ph:ph + 64, hc * HD:(hc + 1) * HD],
                             Sst_f[ph:ph + 64, hc * HD:(hc + 1) * HD],
                             inc="sv", mark=f"sbf{c}.{it}")
                        S.op("v", "reduce_sum", lnst[ph:ph + 64, 14:15], ksl, axis=AX,
                             inc="sv", mark=f"zred{c}.{it}")
                        S.wait("v", f"zred{c}.{it}")
                        S.op("v", "tensor_add", z_f[ph:ph + 64, hc:hc + 1],
                             lnst[ph:ph + 64, 14:15], z_f[ph:ph + 64, hc:hc + 1],
                             inc="sv", mark=f"zadd{c}.{it}")
                        S.wait("v", f"zadd{c}.{it}")
                        S.op("v", "tensor_copy", zb[ph:ph + 64, hc:hc + 1],
                             z_f[ph:ph + 64, hc:hc + 1],
                             inc="sv", mark=f"zbf{c}.{it}")

                for mt in range(OT):
                    par = mt % 2
                    S.wait("t", f"oev{c}.{63}")
                    S.wait("t", f"vev{c}.3.1")
                    S.wait("t", f"kel{c}.3")
                    if c > 0 or mt >= 2:
                        pc2, pmt = (c, mt - 2) if mt >= 2 else (c - 1, OT - 2 + mt)
                        S.wait("t", f"ofm{pc2}.{pmt}")
                    for kd in range(KD):
                        S.op("t", "matmul", pQ[par][:],
                             wo_sb[:, kd * D + mt * 128: kd * D + (mt + 1) * 128],
                             attnT[:, kd * LC:(kd + 1) * LC],
                             start=(kd == 0), stop=(kd == KD - 1),
                             inc="pe" if kd == KD - 1 else None,
                             mark=f"omm{c}.{mt}" if kd == KD - 1 else None)
                    S.wait("s", f"omm{c}.{mt}")
                    if c > 0 or mt >= 2:
                        pc2, pmt = (c, mt - 2) if mt >= 2 else (c - 1, OT - 2 + mt)
                        S.wait("s", f"otr{pc2}.{pmt}.3")
                    S.op("s", "activation", ofm[:, par * 512:(par + 1) * 512],
                         pQ[par][:], AF.Copy, inc="sS", mark=f"ofm{c}.{mt}")
                    S.wait("t", f"ofm{c}.{mt}")
                    for t in range(4):
                        idx = 4 * mt + t
                        if c > 0 or idx >= 1:
                            pc2, pidx = (c, idx - 1) if idx >= 1 else (c - 1, 31)
                            S.wait("t", f"oadd{pc2}.{pidx // 4}.{pidx % 4}")
                        S.op("t", "transpose", pT[:],
                             ofm[:, par * 512 + t * 128: par * 512 + (t + 1) * 128],
                             id_sb[:], inc="pe", mark=f"otr{c}.{mt}.{t}")
                        S.wait("v", f"otr{c}.{mt}.{t}")
                        dst = xt[:, t * D + mt * 128: t * D + (mt + 1) * 128]
                        S.op("v", "tensor_add", dst, pT[:], dst,
                             inc="sv", mark=f"oadd{c}.{mt}.{t}")
                S.wait("sync", f"oadd{c}.{OT-1}.3")
                S.op("sync", "dma_start",
                     out=out[c * LC:(c + 1) * LC, :].rearrange("(t p) d -> p t d", p=128),
                     in_=xt[:].rearrange("p (t d) -> p t d", t=4),
                     inc=("qs", 16), mark=f"st{c}")

            S.wait("sync", f"st{NCH-1}")

        return schedule


    return prep_attn1_weights, make_attn1_phase


_prep_attn1_weights, _make_attn1_phase = _attn1_mod()


def _moe1_mod():

    F32 = mybir.dt.float32
    BF16 = mybir.dt.bfloat16
    AF = mybir.ActivationFunctionType
    ALU = mybir.AluOpType
    AX = mybir.AxisListType.X

    L, D = 4096, 1024
    NE = 8
    NPE = 16
    IT = 16
    KD = D // 128
    MT = D // 128
    TQ = L // 4
    NTT = TQ // 128
    TB = TQ // 512
    EPS = 1e-5
    POISON = False


    def prep_moe1_weights(ins):
        f32 = np.float32
        g = lambda k: np.asarray(ins[k])
        w = {}
        w["n_gate"] = np.ascontiguousarray(g("L1_ffn_gate_kernel")).astype(f32)
        w["n_w0"] = np.ascontiguousarray(g("L1_ffn_w0")).astype(f32)
        w["n_w1"] = np.ascontiguousarray(g("L1_ffn_w1")).astype(f32)
        w["n_wo"] = np.ascontiguousarray(g("L1_ffn_wo")).astype(f32)
        w["n_ident"] = np.eye(128, dtype=ml_dtypes.bfloat16)
        return w


    def make_moe1_phase(nc, sch, alloc, P, io):
        debug = False
        x = io["x"]; acc = io["acc"]; out = io["out"]
        gate = io["n_gate"]
        w0 = io["n_w0"]; w1 = io["n_w1"]; wo = io["n_wo"]
        ident_d = io["n_ident"]; d_cw = io["n_d_cw"]
        A = alloc
        xq = A("xq", [128, 4 * D], F32)
        xnb = A("xnb", [128, D], BF16)
        xnT = A("xnT", [128, KD * TQ], BF16)
        yac = A("yac", [128, MT * TQ], BF16)
        w13b = A("w13b", [128, IT * 2 * KD * 128], BF16)
        w2r = A("w2r", [128, MT * IT * 128], BF16)
        abuf = A("abuf", [128, IT * 512], BF16)
        hsb = A("hsb", [128, 2 * 512], BF16)
        cwt = A("cwt", [128, NTT * NE], F32)
        cwb16 = A("cwb16", [128, NE], BF16)
        cwT = A("cwT", [NE, NTT * 128], F32)
        cwbc = A("cwbc", [128, 2 * TQ], BF16)
        gate_sb = A("gate_sb", [128, NE * KD], BF16)
        id_sb = A("id_sb", [128, 128], BF16)
        lnst = A("lnst", [128, 16], F32)
        gst = A("gst", [128, 8], F32)
        msk = A("msk", [128, NE], F32)
        eps_sb = A("eps_sb", [128, 1], F32)
        nbig_sb = A("nbig_sb", [128, 1], F32)
        sq_f = A("sq_f", [128, D], F32)


        pT = [P[6][:, 0:64].bitcast(BF16), P[7][:, 0:64].bitcast(BF16)]
        pH = [P[0], P[1]]
        pG = [P[2], P[3]]
        pY = [P[4], P[5]]

        LSUM, LSQ, NEGM, M2, VAR, SD, RSTD = 0, 2, 4, 6, 8, 10, 12

        def schedule(S):
            S.op("gp", "dma_start",
                 out=gate_sb[:].rearrange("p (k n) -> p k n", k=KD),
                 in_=gate.rearrange("(k p) n -> p k n", p=128), inc=("qc", 16))
            S.op("gp", "dma_start", out=id_sb[:], in_=ident_d, inc=("qc", 16),
                 mark="consts")
            S.wait("v", "consts")
            S.op("v", "memset", eps_sb[:], EPS)
            S.op("v", "memset", nbig_sb[:], -100.0, inc="sv", mark="pre")

            for hf in range(L // TQ):
                t0 = hf * TQ
                for q in range(2):
                    if hf > 0 or q > 0:
                        ph, pq = (hf, q - 1) if q > 0 else (hf - 1, 1)
                        S.wait("sync", f"xtr{ph}.{pq * 4 + 3}.7")
                        if q == 0:
                            S.wait("sync", f"st{hf-1}.1")
                    S.op("sync", "dma_start",
                         out=xq[:].rearrange("p (t d) -> p t d", t=4),
                         in_=x[t0 + q * 512: t0 + (q + 1) * 512, :]
                             .rearrange("(t p) d -> p t d", p=128),
                         inc=("qx", 16), mark=f"x{hf}.{q}")
                    for tl in range(4):
                        t = q * 4 + tl
                        S.wait("s", f"x{hf}.{q}")
                        if hf > 0 or t >= 1:
                            ph, pt = (hf, t - 1) if t >= 1 else (hf - 1, NTT - 1)
                            S.wait("s", f"var{ph}.{pt}")
                        xsl = xq[:, tl * D:(tl + 1) * D]
                        S.op("s", "activation", sq_f[:, 0:D], xsl, AF.Square,
                             inc="sS", mark=f"lnstat{hf}.{t}")
                        S.wait("v", f"lnstat{hf}.{t}")
                        S.wait("v", f"x{hf}.{q}")
                        S.op("v", "reduce_sum", lnst[:, LSUM:LSUM + 1], xsl, axis=AX)
                        S.op("v", "reduce_sum", lnst[:, LSQ:LSQ + 1], sq_f[:, 0:D],
                             axis=AX, inc="sv", mark=f"lnred{hf}.{t}")
                        S.wait("v", f"lnred{hf}.{t}")
                        S.op("v", "tensor_scalar_mul", lnst[:, NEGM:NEGM + 1],
                             lnst[:, LSUM:LSUM + 1], -1.0 / D, inc="sv", mark=f"negm{hf}.{t}")
                        S.wait("v", f"negm{hf}.{t}")
                        S.op("v", "tensor_mul", lnst[:, M2:M2 + 1],
                             lnst[:, NEGM:NEGM + 1], lnst[:, NEGM:NEGM + 1],
                             inc="sv", mark=f"m2_{hf}.{t}")
                        S.wait("v", f"m2_{hf}.{t}")
                        S.op("v", "scalar_tensor_tensor", lnst[:, VAR:VAR + 1],
                             lnst[:, LSQ:LSQ + 1], 1.0 / D, lnst[:, M2:M2 + 1],
                             ALU.mult, ALU.subtract, inc="sv", mark=f"var{hf}.{t}")
                        S.wait("s", f"var{hf}.{t}")
                        S.op("s", "activation", lnst[:, SD:SD + 1], lnst[:, VAR:VAR + 1],
                             AF.Sqrt, bias=eps_sb[:, 0:1], inc="sS", mark=f"sd{hf}.{t}")
                        S.wait("v", f"sd{hf}.{t}")
                        if hf > 0 or t >= 1:
                            ph, pt = (hf, t - 1) if t >= 1 else (hf - 1, NTT - 1)
                            S.wait("v", f"xtr{ph}.{pt}.7")
                        S.op("v", "reciprocal", lnst[:, RSTD:RSTD + 1],
                             lnst[:, SD:SD + 1], inc="sv", mark=f"rstd{hf}.{t}")
                        S.wait("v", f"rstd{hf}.{t}")
                        S.op("v", "tensor_scalar", xnb[:, 0:D], xsl,
                             lnst[:, NEGM:NEGM + 1], lnst[:, RSTD:RSTD + 1],
                             ALU.add, ALU.mult, inc="sv", mark=f"xn{hf}.{t}")
                        S.wait("t", f"xn{hf}.{t}")
                        if hf == 0 and t == 0:
                            S.wait("t", "consts")
                        for kd in range(KD):
                            idx = 8 * t + kd
                            if idx >= 2:
                                S.wait("t", f"trev{hf}.{(idx-2) // 8}.{(idx-2) % 8}")
                            elif hf > 0:
                                S.wait("t", f"fadd{hf-1}.{NTT-1}.{7}")
                            S.op("t", "transpose", pT[idx % 2][:],
                                 xnb[:, kd * 128:(kd + 1) * 128], id_sb[:],
                                 inc="pe", mark=f"xtr{hf}.{t}.{kd}")
                            S.wait("s", f"xtr{hf}.{t}.{kd}")
                            S.op("s", "copy",
                                 xnT[:, kd * TQ + t * 128: kd * TQ + (t + 1) * 128],
                                 pT[idx % 2][:], inc="sS", mark=f"trev{hf}.{t}.{kd}")

                for t in range(NTT):
                    S.wait("t", f"trev{hf}.{t}.7")
                    if hf > 0 and t == 0:
                        S.wait("t", f"yadd{hf-1}.{NPE-1}.{TB-1}.{MT-1}")
                    if t >= 1:
                        S.wait("t", f"glog{hf}.{t-1}")
                    for kd in range(KD):
                        S.op("t", "matmul", pY[0][:, 0:NE],
                             xnT[:, kd * TQ + t * 128: kd * TQ + (t + 1) * 128],
                             gate_sb[:, kd * NE:(kd + 1) * NE],
                             start=(kd == 0), stop=(kd == KD - 1),
                             inc="pe" if kd == KD - 1 else None,
                             mark=f"gmm{hf}.{t}" if kd == KD - 1 else None)
                    S.wait("v", f"gmm{hf}.{t}")
                    csl = cwt[:, t * NE:(t + 1) * NE]
                    S.op("v", "tensor_copy", csl, pY[0][:, 0:NE],
                         inc="sv", mark=f"glog{hf}.{t}")
                    S.wait("v", f"glog{hf}.{t}")
                    S.op("v", "reduce_max", gst[:, 0:1], csl, axis=AX, negate=True,
                         inc="sv", mark=f"gmax{hf}.{t}")
                    S.wait("v", f"gmax{hf}.{t}")
                    S.op("v", "tensor_scalar_mul", gst[:, 3:4], gst[:, 0:1], -1.0,
                         inc="sv", mark=f"gm1p{hf}.{t}")
                    S.wait("v", f"gm1p{hf}.{t}")
                    S.op("v", "tensor_single_scalar", msk[:], csl, gst[:, 3:4],
                         ALU.is_equal, inc="sv", mark=f"geq{hf}.{t}")
                    S.wait("v", f"geq{hf}.{t}")
                    S.op("v", "scalar_tensor_tensor", msk[:], msk[:],
                         nbig_sb[:, 0:1], csl, ALU.mult, ALU.add,
                         inc="sv", mark=f"gmsk{hf}.{t}")
                    S.wait("v", f"gmsk{hf}.{t}")
                    S.op("v", "reduce_max", gst[:, 5:6], msk[:], axis=AX,
                         inc="sv", mark=f"gm2{hf}.{t}")
                    S.wait("v", f"gm2{hf}.{t}")
                    S.op("v", "tensor_single_scalar", msk[:], csl, gst[:, 5:6],
                         ALU.is_ge, inc="sv", mark=f"gge{hf}.{t}")
                    S.wait("s", f"gmax{hf}.{t}")
                    S.wait("s", f"gge{hf}.{t}")
                    S.op("s", "activation", csl, csl, AF.Exp, bias=gst[:, 0:1],
                         inc="sS", mark=f"gexp{hf}.{t}")
                    S.wait("v", f"gexp{hf}.{t}")
                    S.wait("v", f"gge{hf}.{t}")
                    S.op("v", "tensor_mul", csl, csl, msk[:],
                         inc="sv", mark=f"gcun{hf}.{t}")
                    S.wait("v", f"gcun{hf}.{t}")
                    S.op("v", "reduce_sum", gst[:, 1:2], csl, axis=AX,
                         inc="sv", mark=f"gsum{hf}.{t}")
                    S.wait("v", f"gsum{hf}.{t}")
                    S.op("v", "reciprocal", gst[:, 2:3], gst[:, 1:2],
                         inc="sv", mark=f"grec{hf}.{t}")
                    S.wait("v", f"grec{hf}.{t}")
                    S.op("v", "tensor_scalar_mul", csl, csl, gst[:, 2:3],
                         inc="sv", mark=f"gcw{hf}.{t}")
                    S.wait("v", f"gcw{hf}.{t}")
                    S.op("v", "tensor_copy", cwb16[:], csl, inc="sv", mark=f"gcb{hf}.{t}")
                    S.wait("t", f"gcb{hf}.{t}")
                    if t >= 2:
                        S.wait("t", f"cwev{hf}.{t-2}")
                    else:
                        S.wait("t", f"trev{hf}.{NTT-1}.{6 + t}")
                    S.op("t", "transpose", pT[t % 2][0:NE, :], cwb16[:], id_sb[:],
                         inc="pe", mark=f"ctr{hf}.{t}")
                    S.wait("s", f"ctr{hf}.{t}")
                    S.op("s", "copy", cwT[:, t * 128:(t + 1) * 128], pT[t % 2][0:NE, :],
                         inc="sS", mark=f"cwev{hf}.{t}")
                S.wait("gp", f"cwev{hf}.{NTT-1}")
                if hf > 0:
                    S.wait("gp", f"cwb{hf-1}.{NPE-1}")
                S.op("gp", "dma_start", out=d_cw, in_=cwT[:], inc=("qcw", 16),
                     mark=f"cwd{hf}")
                if debug and hf == 0:
                    S.wait("sync", f"gcw{hf}.{NTT-1}")
                    S.op("sync", "dma_start", out=dbg_cwt, in_=cwt[:], inc=("qs", 16))
                    S.wait("sync", f"cwev{hf}.{NTT-1}")
                    S.op("sync", "dma_start", out=dbg_cwT, in_=cwT[:], inc=("qs", 16))

                for ei in range(NPE):
                    e, gh = ei // 2, ei % 2
                    epar = ei % 2
                    S.wait("sync", f"cwd{hf}")
                    if hf > 0 or ei >= 2:
                        ph, pei = (hf, ei - 2) if ei >= 2 else (hf - 1, NPE - 2 + ei)
                        S.wait("sync", f"yadd{ph}.{pei}.{TB-1}.{MT-1}")
                    S.op("sync", "dma_start", out=cwbc[:, epar * TQ:(epar + 1) * TQ],
                         in_=d_cw[e:e + 1, :].broadcast_to([128, TQ]),
                         inc=(f"qbc{epar}", 16), mark=f"cwb{hf}.{ei}")
                    if debug and hf == 0 and ei == 0:
                        S.wait("sync", f"cwb{hf}.{ei}")
                        S.op("sync", "dma_start", out=dbg_cwbc,
                             in_=cwbc[:, 0:TQ], inc=("qs", 16))
                    if hf > 0 or ei > 0:
                        ph, pei = (hf, ei - 1) if ei > 0 else (hf - 1, NPE - 1)
                        S.wait("gp", f"hgmm{ph}.{pei}.{TB-1}.{IT-1}")
                    S.op("gp", "dma_start",
                         out=w13b[:, 0:IT * KD * 128]
                             .rearrange("p (k i e) -> p k i e", k=KD, i=IT),
                         in_=w0[e, :, gh * 2048:(gh + 1) * 2048]
                             .rearrange("(k p) (i e) -> p k i e", p=128, e=128),
                         inc=("qw0", 16), mark=f"w13a{hf}.{ei}")
                    S.op("gp", "dma_start",
                         out=w13b[:, IT * KD * 128:2 * IT * KD * 128]
                             .rearrange("p (k i e) -> p k i e", k=KD, i=IT),
                         in_=w1[e, :, gh * 2048:(gh + 1) * 2048]
                             .rearrange("(k p) (i e) -> p k i e", p=128, e=128),
                         inc=("qw1", 16), mark=f"w13{hf}.{ei}")
                    if hf > 0 or ei > 0:
                        ph, pei = (hf, ei - 1) if ei > 0 else (hf - 1, NPE - 1)
                        S.wait("gp", f"w2mm{ph}.{pei}.{TB-1}.{MT-1}")
                    S.op("gp", "dma_start",
                         out=w2r[:].rearrange("p (k d m) -> p k d m", k=IT, d=MT),
                         in_=wo[e, gh * 2048:(gh + 1) * 2048, :]
                             .rearrange("(k p) (d m) -> p k d m", p=128, m=128),
                         inc=("qv", 16), mark=f"w2l{hf}.{ei}")
                    for tb in range(TB):
                        for it in range(IT):
                            ipar = it % 2
                            S.wait("t", f"w13a{hf}.{ei}")
                            S.wait("t", f"w13{hf}.{ei}")
                            if hf == 0 and ei == 0 and tb == 0:
                                S.wait("t", f"cwev{hf}.{NTT-1}")
                            if hf > 0 or ei > 0 or tb > 0 or it >= 2:
                                pit, ph, pei, ptb = it - 2, hf, ei, tb
                                if pit < 0:
                                    ptb = tb - 1
                                    if ptb < 0:
                                        pei, ptb = ei - 1, TB - 1
                                        if pei < 0:
                                            ph, pei = hf - 1, NPE - 1
                                    pit = IT + pit
                                S.wait("t", f"aprime{ph}.{pei}.{ptb}.{pit}")
                            for kd in range(KD):
                                S.op("t", "matmul", pH[ipar][:],
                                     w13b[:, (kd * IT + it) * 128:(kd * IT + it + 1) * 128],
                                     xnT[:, kd * TQ + tb * 512: kd * TQ + (tb + 1) * 512],
                                     start=(kd == 0), stop=(kd == KD - 1))
                            for kd in range(KD):
                                S.op("t", "matmul", pG[ipar][:],
                                     w13b[:, IT * KD * 128 + (kd * IT + it) * 128:
                                          IT * KD * 128 + (kd * IT + it + 1) * 128],
                                     xnT[:, kd * TQ + tb * 512: kd * TQ + (tb + 1) * 512],
                                     start=(kd == 0), stop=(kd == KD - 1),
                                     inc="pe" if kd == KD - 1 else None,
                                     mark=f"hgmm{hf}.{ei}.{tb}.{it}" if kd == KD - 1 else None)
                            S.wait("s", f"hgmm{hf}.{ei}.{tb}.{it}")
                            if hf > 0 or ei > 0 or tb > 0 or it >= 2:
                                S.wait("s", f"aprime{ph}.{pei}.{ptb}.{pit}")
                            hss = hsb[:, ipar * 512:(ipar + 1) * 512]
                            S.op("s", "activation", hss, pH[ipar][:], AF.Silu,
                                 inc="sS", mark=f"hsil{hf}.{ei}.{tb}.{it}")
                            S.wait("v", f"hsil{hf}.{ei}.{tb}.{it}")
                            S.wait("v", f"cwb{hf}.{ei}")
                            if hf > 0 or ei > 0 or tb > 0:
                                ptb2, ph2, pei2 = tb - 1, hf, ei
                                if ptb2 < 0:
                                    pei2, ptb2 = ei - 1, TB - 1
                                    if pei2 < 0:
                                        ph2, pei2 = hf - 1, NPE - 1
                                S.wait("v", f"w2mm{ph2}.{pei2}.{ptb2}.{MT-1}")
                            asl = abuf[:, it * 512:(it + 1) * 512]
                            S.op("v", "tensor_mul", asl, pG[ipar][:], hss)
                            S.op("v", "tensor_mul", asl, asl,
                                 cwbc[:, epar * TQ + tb * 512: epar * TQ + (tb + 1) * 512],
                                 inc="sv", mark=f"aprime{hf}.{ei}.{tb}.{it}")
                        for dt in range(MT):
                            dpar = dt % 2
                            S.wait("t", f"w2l{hf}.{ei}")
                            S.wait("t", f"aprime{hf}.{ei}.{tb}.{IT-1}")
                            if hf > 0 or ei > 0 or tb > 0 or dt >= 2:
                                pdt, ph3, pei3, ptb3 = dt - 2, hf, ei, tb
                                if pdt < 0:
                                    ptb3 = tb - 1
                                    if ptb3 < 0:
                                        pei3, ptb3 = ei - 1, TB - 1
                                        if pei3 < 0:
                                            ph3, pei3 = hf - 1, NPE - 1
                                    pdt = MT + pdt
                                S.wait("t", f"yadd{ph3}.{pei3}.{ptb3}.{pdt}")
                            for it in range(IT):
                                S.op("t", "matmul", pY[dpar][:],
                                     w2r[:, (it * MT + dt) * 128:(it * MT + dt + 1) * 128],
                                     abuf[:, it * 512:(it + 1) * 512],
                                     start=(it == 0), stop=(it == IT - 1),
                                     inc="pe" if it == IT - 1 else None,
                                     mark=f"w2mm{hf}.{ei}.{tb}.{dt}" if it == IT - 1 else None)
                            S.wait("v", f"w2mm{hf}.{ei}.{tb}.{dt}")
                            ysl = yac[:, dt * TQ + tb * 512: dt * TQ + (tb + 1) * 512]
                            if ei == 0:
                                if hf > 0:
                                    S.wait("v", f"ftr{hf-1}.{NTT-1}.{MT-1}")
                                S.op("v", "tensor_copy", ysl, pY[dpar][:],
                                     inc="sv", mark=f"yadd{hf}.{ei}.{tb}.{dt}")
                            else:
                                S.op("v", "tensor_add", ysl, pY[dpar][:], ysl,
                                     inc="sv", mark=f"yadd{hf}.{ei}.{tb}.{dt}")

                for q in range(2):
                    S.wait("sync", f"xtr{hf}.{NTT-1}.{7}")
                    if q > 0:
                        S.wait("sync", f"st{hf}.{q-1}")
                    S.op("sync", "dma_start",
                         out=xq[:].rearrange("p (t d) -> p t d", t=4),
                         in_=acc[t0 + q * 512: t0 + (q + 1) * 512, :]
                             .rearrange("(t p) d -> p t d", p=128),
                         inc=("qa", 16), mark=f"a{hf}.{q}")
                    for tl in range(4):
                        t = q * 4 + tl
                        for dt in range(MT):
                            idx = 8 * t + dt
                            S.wait("t", f"yadd{hf}.{NPE-1}.{t // 4}.{dt}")
                            if idx >= 2:
                                S.wait("t", f"fadd{hf}.{(idx-2) // 8}.{(idx-2) % 8}")
                            else:
                                S.wait("t", f"cwev{hf}.{NTT-1}")
                            S.op("t", "transpose", pT[idx % 2][:],
                                 yac[:, dt * TQ + t * 128: dt * TQ + (t + 1) * 128],
                                 id_sb[:], inc="pe", mark=f"ftr{hf}.{t}.{dt}")
                            S.wait("v", f"ftr{hf}.{t}.{dt}")
                            S.wait("v", f"a{hf}.{q}")
                            dst = xq[:, tl * D + dt * 128: tl * D + (dt + 1) * 128]
                            S.op("v", "tensor_add", dst, pT[idx % 2][:], dst,
                                 inc="sv", mark=f"fadd{hf}.{t}.{dt}")
                    S.wait("sync", f"fadd{hf}.{q * 4 + 3}.{7}")
                    S.op("sync", "dma_start",
                         out=out[t0 + q * 512: t0 + (q + 1) * 512, :]
                             .rearrange("(t p) d -> p t d", p=128),
                         in_=xq[:].rearrange("p (t d) -> p t d", t=4),
                         inc=("qs", 16), mark=f"st{hf}.{q}")

            S.wait("sync", f"st{L // TQ - 1}.1")

        return schedule


    return prep_moe1_weights, make_moe1_phase, NE, TQ


_prep_moe1_weights, _make_moe1_phase, _MO_NE2, _MO_TQ2 = _moe1_mod()



_L, _D = 4096, 1024
F32 = mybir.dt.float32
BF16 = mybir.dt.bfloat16


def _prep_weights(ins):
    w = {}
    w.update(_prep_mamba_weights(ins))
    w.update(_prep_moe0_weights(ins))
    w.update(_prep_attn1_weights(ins))
    w.update(_prep_moe1_weights(ins))
    return w


def _build_fused():
    nc = bass.Bass("TRN2", target_bir_lowering=False)
    L, D = _L, _D
    dx = nc.dram_tensor("x", [L, D], F32, kind="ExternalInput").ap()
    dout = nc.dram_tensor("out", [L, D], F32, kind="ExternalOutput").ap()
    d_acc1 = nc.dram_tensor("d_acc1", [L, D], F32, kind="Internal").ap()
    d_x1 = nc.dram_tensor("d_x1", [L, D], F32, kind="Internal").ap()
    d_acc2 = nc.dram_tensor("d_acc2", [L, D], F32, kind="Internal").ap()

    def din(nm, shp, dt=F32):
        return nc.dram_tensor(nm, shp, dt, kind="ExternalInput").ap()

    E, NST, R, LCm, DTm = _MA_E, _MA_NST, _MA_R, _MA_LC, _MA_DT
    ioA = dict(x=dx, out=d_acc1,
               wxzT=din("wxzT", [D, 2 * E]), convw=din("convw", [128, DTm * 4]),
               convb=din("convb", [128, DTm]), xprojT=din("xprojT", [E, 96]),
               dtwT=din("dtwT", [R, E]), dtb=din("dtb", [128, DTm]),
               alog=din("alog", [128, DTm * NST]), dvec=din("dvec", [128, DTm]),
               woT=din("woT", [E, D]), ident=din("ident", [128, 128], BF16),
               d_bc=nc.dram_tensor("d_bc", [2, 2, NST, LCm], BF16, kind="Internal").ap())
    NEb, NXP, ITb, TH = _MO_NE, _MO_NXP, _MO_IT, _MO_TH
    ioB = dict(x=dx, acc=d_acc1, out=d_x1,
               m_gate=din("m_gate", [D, NEb]), m_gateb=din("m_gateb", [1, NEb]),
               m_w1=din("m_w1", [NXP, D, ITb * 128]),
               m_w3=din("m_w3", [NXP, D, ITb * 128]),
               m_w2=din("m_w2", [NXP, ITb * 128, D]),
               m_b1=din("m_b1", [128, NXP * ITb]),
               m_b3=din("m_b3", [128, NXP * ITb]),
               m_b2=din("m_b2", [1, NXP * D]),
               m_ident=din("m_ident", [128, 128], BF16),
               m_d_cw=nc.dram_tensor("d_cw0", [NEb, TH], BF16, kind="Internal").ap())
    ioC = dict(x=d_x1, out=d_acc2,
               a_wqT=din("a_wqT", [D, D]), a_wkT=din("a_wkT", [D, D]),
               a_wvT=din("a_wvT", [D, D]), a_woT=din("a_woT", [D, D]),
               a_ident=din("a_ident", [128, 128], BF16),
               a_maskT=din("a_maskT", [128, 128], BF16))
    NEd, TQ = _MO_NE2, _MO_TQ2
    ioD = dict(x=d_x1, acc=d_acc2, out=dout,
               n_gate=din("n_gate", [D, NEd]),
               n_w0=din("n_w0", [NEd, D, 4096]),
               n_w1=din("n_w1", [NEd, D, 4096]),
               n_wo=din("n_wo", [NEd, 4096, D]),
               n_ident=din("n_ident", [128, 128], BF16),
               n_d_cw=nc.dram_tensor("d_cw1", [NEd, TQ], BF16, kind="Internal").ap())

    sch = Sched(nc, ["qx", "qa", "qs", "qc", "qwA0", "qwA1", "qwB0", "qwB1",
                     "qbo0", "qbo1", "qb0", "qb1", "qw0", "qw1", "qv", "qv0",
                     "qv1", "qcw", "qbc0", "qbc1", "pe", "sv", "sS"])
    ar = Arena(nc, 212000)
    P = [nc.alloc_psum_tensor(f"P{i}", [128, 512], F32) for i in range(8)]

    phases = []
    for pfx, maker, io in [("A", _make_mamba_phase, ioA),
                           ("B", _make_moe0_phase, ioB),
                           ("C", _make_attn1_phase, ioC),
                           ("D", _make_moe1_phase, ioD)]:
        ar.reset()
        phases.append((pfx, maker(nc, sch, ar.alloc, P, io)))

    def schedule(S):
        for pfx, ph in phases:
            S.prefix = pfx
            ph(S)
            S.barrier("end")
        S.prefix = ""

    schedule(sch)
    with nc.Block() as blk:
        @blk.sync
        def _(h):
            sch.begin_emit("sync", h)
            schedule(sch)

        @blk.gpsimd
        def _(h):
            sch.begin_emit("gp", h)
            schedule(sch)

        @blk.vector
        def _(h):
            sch.begin_emit("v", h)
            schedule(sch)

        @blk.scalar
        def _(h):
            sch.begin_emit("s", h)
            schedule(sch)

        @blk.tensor
        def _(h):
            sch.begin_emit("t", h)
            schedule(sch)

    return nc


_NCORES = 8


def kernel(**inputs):
    ins = {k: np.asarray(v) for k, v in inputs.items()}
    X = np.ascontiguousarray(ins["L0_attn_hidden_states"].astype(np.float32))
    w = _prep_weights(ins)
    nc = _build_fused()
    res = run_bass_kernel_spmd(
        nc, [dict(x=X[b], **w) for b in range(_NCORES)],
        core_ids=list(range(_NCORES)))
    out = np.stack([np.asarray(r["out"]) for r in res.results])
    return out.astype(np.float32)


# version 03
# speedup vs baseline: 1.0278x; 1.0278x over previous
import numpy as np
import ml_dtypes
import concourse.bass as bass
import concourse.mybir as mybir
from concourse.bass_utils import run_bass_kernel_spmd


_ISIZE = {mybir.dt.float32: 4, mybir.dt.bfloat16: 2, mybir.dt.uint8: 1,
          mybir.dt.int32: 4}


class Arena:

    def __init__(self, nc, nbytes, name="arena"):
        self.t = nc.alloc_sbuf_tensor(name, [128, nbytes], mybir.dt.uint8)
        self.nbytes = nbytes
        self.cur = 0
        self.peak = 0

    def alloc(self, name, shape, dtype):
        assert len(shape) == 2 and shape[0] <= 128
        per = shape[1] * _ISIZE[dtype]
        off = (self.cur + 31) & ~31
        self.cur = off + per
        self.peak = max(self.peak, self.cur)
        assert self.cur <= self.nbytes, f"arena overflow at {name}: {self.cur}"
        return self.t[0:shape[0], off:off + per].bitcast(dtype)

    def reset(self):
        self.cur = 0


class Sched:
    BARRIER_EXCLUDE = set()

    def __init__(self, nc, semnames):
        self.nc = nc
        self.sems = {n: nc.alloc_semaphore(n) for n in semnames}
        self.dry = True
        self.cnt = {n: 0 for n in semnames}
        self.marks = {}
        self.cur = None
        self.h = None
        self.waited = {}
        self.prefix = ""

    def begin_emit(self, tag, handle):
        self.dry = False
        self.cur = tag
        self.h = handle
        self.waited = {}

    def op(self, tag, name, *args, inc=None, mark=None, **kw):
        if self.dry:
            if inc is not None:
                sem, k = inc if isinstance(inc, tuple) else (inc, 1)
                self.cnt[sem] += k
                if mark is not None:
                    self.marks[(self.prefix, mark)] = (sem, self.cnt[sem])
            elif mark is not None:
                raise ValueError(f"mark {mark} without inc")
            return None
        if self.cur != tag:
            return None
        inst = getattr(self.h, name)(*args, **kw)
        if inc is not None:
            sem, k = inc if isinstance(inc, tuple) else (inc, 1)
            inst.then_inc(self.sems[sem], k)
        return inst

    def barrier(self, key):
        k = (self.prefix, "bar", key)
        if self.dry:
            self.marks[k] = dict(self.cnt)
            return
        snap = self.marks[k]
        for sem, v in snap.items():
            if sem in self.BARRIER_EXCLUDE:
                continue
            if v <= 0 or self.waited.get(sem, 0) >= v:
                continue
            self.waited[sem] = v
            self.h.wait_ge(self.sems[sem], v)

    def wait(self, tag, mark):
        if self.dry or self.cur != tag:
            return
        if isinstance(mark, tuple):
            sem, v = mark
        else:
            sem, v = self.marks[(self.prefix, mark)]
        if v <= 0 or self.waited.get(sem, 0) >= v:
            return
        self.waited[sem] = v
        self.h.wait_ge(self.sems[sem], v)


def _mamba_mod():

    F32 = mybir.dt.float32
    BF16 = mybir.dt.bfloat16
    AF = mybir.ActivationFunctionType
    ALU = mybir.AluOpType

    L, D, E, NST, R = 4096, 1024, 2048, 16, 64
    LC = 512
    NCH = L // LC
    DT, KD, MT = E // 128, D // 128, D // 128
    XP = LC + 4
    EPS = 1e-5


    def prep_mamba_weights(ins):
        f32 = np.float32
        w = {}
        w["wxzT"] = np.ascontiguousarray(np.asarray(ins["L0_attn_in_proj_w"]).T).astype(f32)
        w["convw"] = np.ascontiguousarray(
            np.asarray(ins["L0_attn_conv_w"])[:, 0, :].reshape(DT, 128, 4).transpose(1, 0, 2).reshape(128, DT * 4)).astype(f32)
        w["convb"] = np.ascontiguousarray(np.asarray(ins["L0_attn_conv_b"]).reshape(DT, 128).T).astype(f32)
        w["xprojT"] = np.ascontiguousarray(np.asarray(ins["L0_attn_x_proj_w"]).T).astype(f32)
        w["dtwT"] = np.ascontiguousarray(np.asarray(ins["L0_attn_dt_proj_w"]).T).astype(f32)
        w["dtb"] = np.ascontiguousarray(np.asarray(ins["L0_attn_dt_proj_b"]).reshape(DT, 128).T).astype(f32)
        w["alog"] = np.ascontiguousarray(
            np.asarray(ins["L0_attn_A_log"]).reshape(DT, 128, NST).transpose(1, 0, 2).reshape(128, DT * NST)).astype(f32)
        w["dvec"] = np.ascontiguousarray(np.asarray(ins["L0_attn_D"]).reshape(DT, 128).T).astype(f32)
        w["woT"] = np.ascontiguousarray(np.asarray(ins["L0_attn_out_proj_w"]).T).astype(f32)
        w["ident"] = np.eye(128, dtype=ml_dtypes.bfloat16)
        return w


    DBG_CHUNK = 0
    PARANOID = False
    ZERO_CARRY = False
    ZERO_A = False
    POISON = False


    def make_mamba_phase(nc, sch, alloc, P, io):
        debug = False
        dbgbc = False
        dbg = {}
        x = io["x"]; acc1 = io["out"]
        wxzT = io["wxzT"]; convw = io["convw"]; convb = io["convb"]
        xprojT = io["xprojT"]; dtwT = io["dtwT"]; dtb = io["dtb"]
        alog = io["alog"]; dvec = io["dvec"]; woT = io["woT"]
        ident_d = io["ident"]; d_bc = io["d_bc"]
        A = alloc
        xt = A("xt", [128, 4 * D], F32)
        xnb = A("xnb", [128, D], BF16)
        xnT = A("xnT", [128, KD * LC], BF16)
        wbuf = A("wbuf", [128, 2 * KD * 128], BF16)
        szb = A("szb", [128, DT * LC], BF16)
        xpad = A("xpad", [128, DT * XP], BF16)
        xcb = A("xcb", [128, DT * LC], BF16)
        dlt = A("dlt", [128, DT * LC], BF16)
        ball = A("ball", [128, DT * LC], BF16)
        ub = A("ub", [128, DT * LC], BF16)
        yb = A("yb", [128, DT * LC], BF16)
        abuf = A("abuf", [128, 2 * LC], F32)
        bbuf = A("bbuf", [128, LC], BF16)
        hbuf = A("hbuf", [128, LC], BF16)
        tmul = A("tmul", [128, LC], BF16)
        carry = A("carry", [128, DT * NST], F32)
        Bb = A("Bb", [128, 2 * LC], BF16)
        Cb = A("Cb", [128, 2 * LC], BF16)
        dt_in_b = A("dt_in_b", [64, LC], BF16)
        bm_s = A("bm_s", [NST, LC], F32)
        cm_s = A("cm_s", [NST, LC], F32)
        convacc = A("convacc", [128, 2 * LC], BF16)
        tex = A("tex", [128, LC], F32)
        wobuf = A("wobuf", [128, 2 * DT * 128], BF16)
        ofm = A("ofm", [128, 2 * 512], BF16)
        cw_sb = A("cw_sb", [128, DT * 4], F32)
        cb_sb = A("cb_sb", [128, DT], F32)
        dtb_sb = A("dtb_sb", [128, DT], F32)
        dv_sb = A("dv_sb", [128, DT], F32)
        A_sb = A("A_sb", [128, DT * NST], F32)
        xproj_sb = A("xproj_sb", [128, DT * 96], BF16)
        dtw_sb = A("dtw_sb", [64, DT * 128], BF16)
        id_sb = A("id_sb", [128, 128], BF16)
        lnst = A("lnst", [128, 16], F32)
        eps_sb = A("eps_sb", [128, 1], F32)
        nbig_sb = A("nbig_sb", [128, 1], F32)
        sq_f = A("sq_f", [128, D], F32)


        pP = [P[0], P[1]]
        pC = P[2][0:96, :]
        pD = P[3][0:16, :]
        pE = [P[4], P[5]]
        pT = [P[6][:, 0:64].bitcast(BF16), P[7][:, 0:64].bitcast(BF16)]

        LSUM, LSQ, NEGM, M2, VAR, SD, RSTD = 0, 2, 4, 6, 8, 10, 12

        def schedule(S):
            S.op("gp", "dma_start", out=cw_sb[:], in_=convw, inc=("qc", 16))
            S.op("gp", "dma_start", out=cb_sb[:], in_=convb, inc=("qc", 16))
            S.op("gp", "dma_start", out=dtb_sb[:], in_=dtb, inc=("qc", 16))
            S.op("gp", "dma_start", out=dv_sb[:], in_=dvec, inc=("qc", 16))
            S.op("gp", "dma_start", out=A_sb[:], in_=alog, inc=("qc", 16))
            S.op("gp", "dma_start",
                 out=xproj_sb[:].rearrange("p (d n) -> p d n", d=DT),
                 in_=xprojT.rearrange("(d p) n -> p d n", p=128), inc=("qc", 16))
            S.op("gp", "dma_start",
                 out=dtw_sb[:].rearrange("p (d n) -> p d n", d=DT),
                 in_=dtwT.rearrange("p (d n) -> p d n", d=DT), inc=("qc", 16))
            S.op("gp", "dma_start", out=id_sb[:], in_=ident_d, inc=("qc", 16),
                 mark="consts")
            S.wait("s", "consts")
            S.op("s", "activation", A_sb[:], A_sb[:], AF.Exp, inc="sS", mark="Aexp")
            S.wait("v", "Aexp")
            S.op("v", "tensor_scalar_mul", A_sb[:], A_sb[:], -1.0)
            S.op("v", "memset", carry[:], 0.0)
            S.op("v", "memset", eps_sb[:], EPS)
            S.op("v", "memset", nbig_sb[:], -100.0)
            for dt_ in range(DT):
                S.op("v", "memset", xpad[:, dt_ * XP: dt_ * XP + 3], 0.0,
                     inc=("sv", 1) if dt_ == DT - 1 else None,
                     mark="pre" if dt_ == DT - 1 else None)

            for c in range(NCH):
                if c > 0:
                    S.wait("sync", f"store{c-1}")
                S.op("sync", "dma_start",
                     out=xt[:].rearrange("p (t d) -> p t d", t=4),
                     in_=x[c * LC:(c + 1) * LC, :].rearrange("(t p) d -> p t d", p=128),
                     inc=("qx", 16), mark=f"x{c}")

                for t in range(4):
                    par = t % 2
                    S.wait("s", f"x{c}")
                    if c > 0 or t >= 1:
                        pc, pt_ = (c, t - 1) if t >= 1 else (c - 1, 3)
                        S.wait("s", f"var{pc}.{pt_}")
                    dump = xnb[:, 0:D]
                    xsl = xt[:, t * D:(t + 1) * D]
                    S.op("s", "activation", sq_f[:, 0:D], xsl,
                         AF.Square, inc="sS", mark=f"lnstat{c}.{t}")
                    S.wait("v", f"lnstat{c}.{t}")
                    S.wait("v", f"x{c}")
                    S.op("v", "reduce_sum", lnst[:, LSUM + par:LSUM + par + 1],
                         xsl, axis=mybir.AxisListType.X)
                    S.op("v", "reduce_sum", lnst[:, LSQ + par:LSQ + par + 1],
                         sq_f[:, 0:D], axis=mybir.AxisListType.X,
                         inc="sv", mark=f"lnred{c}.{t}")
                    S.wait("v", f"lnred{c}.{t}")
                    S.op("v", "tensor_scalar_mul", lnst[:, NEGM + par:NEGM + par + 1],
                         lnst[:, LSUM + par:LSUM + par + 1], -1.0 / D,
                         inc="sv", mark=f"negm{c}.{t}")
                    S.wait("v", f"negm{c}.{t}")
                    S.op("v", "tensor_mul", lnst[:, M2 + par:M2 + par + 1],
                         lnst[:, NEGM + par:NEGM + par + 1], lnst[:, NEGM + par:NEGM + par + 1],
                         inc="sv", mark=f"m2{c}.{t}")
                    S.wait("v", f"m2{c}.{t}")
                    S.op("v", "scalar_tensor_tensor", lnst[:, VAR + par:VAR + par + 1],
                         lnst[:, LSQ + par:LSQ + par + 1], 1.0 / D,
                         lnst[:, M2 + par:M2 + par + 1], ALU.mult, ALU.subtract,
                         inc="sv", mark=f"var{c}.{t}")
                    S.wait("s", f"var{c}.{t}")
                    S.op("s", "activation", lnst[:, SD + par:SD + par + 1],
                         lnst[:, VAR + par:VAR + par + 1], AF.Sqrt, bias=eps_sb[:, 0:1],
                         inc="sS", mark=f"sd{c}.{t}")
                    S.wait("v", f"sd{c}.{t}")
                    if c > 0 or t >= 1:
                        pc, pt_ = (c, t - 1) if t >= 1 else (c - 1, 3)
                        S.wait("v", f"xtr{pc}.{pt_}.7")
                    S.op("v", "reciprocal", lnst[:, RSTD + par:RSTD + par + 1],
                         lnst[:, SD + par:SD + par + 1],
                         inc="sv", mark=f"rstd{c}.{t}")
                    S.wait("v", f"rstd{c}.{t}")
                    S.op("v", "tensor_scalar", dump, xsl,
                         lnst[:, NEGM + par:NEGM + par + 1],
                         lnst[:, RSTD + par:RSTD + par + 1],
                         ALU.add, ALU.mult, inc="sv", mark=f"xn{c}.{t}")
                    S.wait("t", f"xn{c}.{t}")
                    if c == 0 and t == 0:
                        S.wait("t", "consts")
                    for kd in range(KD):
                        idx = 8 * t + kd
                        if idx >= 2:
                            S.wait("t", f"trev{c}.{(idx - 2) // 8}.{(idx - 2) % 8}")
                        elif c > 0:
                            S.wait("t", f"oadd{c-1}.{MT-1}.3")
                        S.op("t", "transpose",
                             pT[idx % 2][:],
                             xnb[:, kd * 128:(kd + 1) * 128],
                             id_sb[:], inc="pe", mark=f"xtr{c}.{t}.{kd}")
                        S.wait("s", f"xtr{c}.{t}.{kd}")
                        S.op("s", "copy",
                             xnT[:, kd * LC + t * 128: kd * LC + (t + 1) * 128],
                             pT[idx % 2][:], inc="sS", mark=f"trev{c}.{t}.{kd}")

                if PARANOID:
                    S.barrier(("lnT", c))
                for et in range(2 * DT):
                    if c > 0 or et >= 2:
                        pc, pet = (c, et - 2) if et >= 2 else (c - 1, 2 * DT - 2 + et)
                        S.wait("gp", f"inmm{pc}.{pet}")
                    S.op("gp", "dma_start",
                         out=wbuf[:, (et % 2) * KD * 128:(et % 2 + 1) * KD * 128]
                             .rearrange("p (k e) -> p k e", k=KD),
                         in_=wxzT[:, et * 128:(et + 1) * 128]
                             .rearrange("(k p) e -> p k e", p=128),
                         inc=(f"qwA{et % 2}", 16), mark=f"wld{c}.{et}")

                for et in range(2 * DT):
                    S.wait("t", f"wld{c}.{et}")
                    S.wait("t", f"trev{c}.3.7")
                    if c > 0 or et >= 2:
                        pc, pet = (c, et - 2) if et >= 2 else (c - 1, MT - 2 + et)
                        S.wait("t", f"inev{pc}.{pet}" if et >= 2 else f"ofmev{c-1}.{MT-2+et}")
                    for kd in range(KD):
                        S.op("t", "matmul", pP[et % 2][:],
                             wbuf[:, (et % 2) * KD * 128 + kd * 128:(et % 2) * KD * 128 + (kd + 1) * 128],
                             xnT[:, kd * LC:(kd + 1) * LC],
                             start=(kd == 0), stop=(kd == KD - 1),
                             inc="pe" if kd == KD - 1 else None,
                             mark=f"inmm{c}.{et}" if kd == KD - 1 else None)
                    S.wait("s", f"inmm{c}.{et}")
                    if et < DT:
                        if c > 0:
                            S.wait("s", f"halo{c-1}")
                        else:
                            S.wait("s", "pre")
                        S.op("s", "copy", xpad[:, et * XP + 3: et * XP + 3 + LC],
                             pP[et % 2][:], inc="sS", mark=f"inev{c}.{et}")
                    else:
                        dt_ = et - DT
                        if c > 0:
                            S.wait("s", f"gate{c-1}.{DT-1}")
                        S.op("s", "activation", szb[:, dt_ * LC:(dt_ + 1) * LC],
                             pP[et % 2][:], AF.Silu, inc="sS", mark=f"inev{c}.{et}")

                if PARANOID:
                    S.barrier(("inproj", c))
                for dt_ in range(DT):
                    par = dt_ % 2
                    S.wait("v", f"inev{c}.{dt_}")
                    if c > 0 or dt_ >= 2:
                        pc, pdt = (c, dt_ - 2) if dt_ >= 2 else (c - 1, DT - 2 + dt_)
                        S.wait("v", f"csilu{pc}.{pdt}")
                    base = dt_ * XP
                    acc = convacc[:, par * LC:(par + 1) * LC]
                    S.op("v", "tensor_scalar_mul", acc, xpad[:, base:base + LC],
                         cw_sb[:, dt_ * 4: dt_ * 4 + 1])
                    for k in (1, 2, 3):
                        S.op("v", "scalar_tensor_tensor", acc,
                             xpad[:, base + k: base + k + LC],
                             cw_sb[:, dt_ * 4 + k: dt_ * 4 + k + 1],
                             acc, ALU.mult, ALU.add,
                             inc="sv" if k == 3 else None,
                             mark=f"conv{c}.{dt_}" if k == 3 else None)
                    S.wait("s", f"conv{c}.{dt_}")
                    S.op("s", "activation", xcb[:, dt_ * LC:(dt_ + 1) * LC],
                         acc, AF.Silu, bias=cb_sb[:, dt_:dt_ + 1],
                         inc="sS", mark=f"csilu{c}.{dt_}")
                for dt_ in range(DT):
                    base = dt_ * XP
                    S.op("v", "tensor_copy", xpad[:, base:base + 3],
                         xpad[:, base + LC:base + LC + 3],
                         inc="sv" if dt_ == DT - 1 else None,
                         mark=f"halo{c}" if dt_ == DT - 1 else None)

                if PARANOID:
                    S.barrier(("conv", c))
                if c > 0:
                    S.wait("t", f"xdblev{c-1}")
                for dt_ in range(DT):
                    S.wait("t", f"csilu{c}.{dt_}")
                    S.op("t", "matmul", pC[0:80, :],
                         xproj_sb[:, dt_ * 96: dt_ * 96 + 80],
                         xcb[:, dt_ * LC:(dt_ + 1) * LC],
                         start=(dt_ == 0), stop=(dt_ == DT - 1),
                         inc="pe" if dt_ == DT - 1 else None,
                         mark=f"xdbl1{c}" if dt_ == DT - 1 else None)
                    S.op("t", "matmul", pD[:, :],
                         xproj_sb[:, dt_ * 96 + 80: dt_ * 96 + 96],
                         xcb[:, dt_ * LC:(dt_ + 1) * LC],
                         start=(dt_ == 0), stop=(dt_ == DT - 1),
                         inc="pe" if dt_ == DT - 1 else None,
                         mark=f"xdbl2{c}" if dt_ == DT - 1 else None)
                S.wait("v", f"xdbl1{c}")
                S.wait("v", f"xdbl2{c}")
                if c > 0:
                    S.wait("v", f"bounce{c-1}")
                S.op("v", "tensor_copy", dt_in_b[:], pC[0:64, :])
                S.op("v", "tensor_copy", bm_s[:], pC[64:80, :])
                S.op("v", "tensor_copy", cm_s[:], pD[:, :], inc="sv", mark=f"xdblev{c}")

                S.wait("gp", f"xdblev{c}")
                if c >= 2:
                    S.wait("gp", f"bc{c-2}.{NST-1}")
                S.op("gp", "dma_start", out=d_bc[c % 2, 0], in_=bm_s[:],
                     inc=(f"qbo{c % 2}", 16))
                S.op("gp", "dma_start", out=d_bc[c % 2, 1], in_=cm_s[:],
                     inc=(f"qbo{c % 2}", 16), mark=f"bounce{c}")

                for n in range(NST):
                    npar = n % 2
                    S.wait("sync", f"bounce{c}")
                    if c > 0 or n >= 2:
                        pc, pn = (c, n - 2) if n >= 2 else (c - 1, NST - 2 + n)
                        S.wait("sync", f"yacc{pc}.{pn * DT + DT - 1}")
                    S.op("sync", "dma_start", out=Bb[:, npar * LC:(npar + 1) * LC],
                         in_=d_bc[c % 2, 0, n:n + 1, :].broadcast_to([128, LC]),
                         inc=(f"qb{npar}", 16))
                    S.op("sync", "dma_start", out=Cb[:, npar * LC:(npar + 1) * LC],
                         in_=d_bc[c % 2, 1, n:n + 1, :].broadcast_to([128, LC]),
                         inc=(f"qb{npar}", 16), mark=f"bc{c}.{n}")
                    if debug and c == DBG_CHUNK and dbgbc and n in (0, 1):
                        S.wait("sync", f"yacc{c}.{n * DT + DT - 1}")
                        S.op("sync", "dma_start", out=dbg[f"dbg2_Bb{n}"],
                             in_=Bb[:, npar * LC:(npar + 1) * LC], inc=("qd", 16))
                        S.op("sync", "dma_start", out=dbg[f"dbg2_Cb{n}"],
                             in_=Cb[:, npar * LC:(npar + 1) * LC],
                             inc=("qd", 16), mark=f"dump_bc{n}")

                if PARANOID:
                    S.barrier(("xdbl", c))
                for dt_ in range(DT):
                    S.wait("t", f"xdblev{c}")
                    if c > 0 or dt_ >= 2:
                        pc, pdt = (c, dt_ - 2) if dt_ >= 2 else (c - 1, DT - 2 + dt_)
                        S.wait("t", f"dabs{pc}.{pdt}")
                    S.op("t", "matmul", pE[dt_ % 2][:],
                         dtw_sb[:, dt_ * 128:(dt_ + 1) * 128], dt_in_b[:],
                         start=True, stop=True, inc="pe", mark=f"dmm{c}.{dt_}")
                    S.wait("s", f"dmm{c}.{dt_}")
                    if c > 0:
                        S.wait("s", f"uw{c-1}.{DT-1}")
                    if c > 0 or dt_ >= 1:
                        pc, pdt = (c, dt_ - 1) if dt_ >= 1 else (c - 1, DT - 1)
                        S.wait("s", f"dadd{pc}.{pdt}")
                    ppar = pE[dt_ % 2][:]
                    txs = tex[:, 0:LC]
                    dsl = dlt[:, dt_ * LC:(dt_ + 1) * LC]
                    S.op("s", "activation", dsl, ppar, AF.Relu, bias=dtb_sb[:, dt_:dt_ + 1])
                    S.op("s", "activation", txs, ppar, AF.Abs, bias=dtb_sb[:, dt_:dt_ + 1],
                         inc="sS", mark=f"dabs{c}.{dt_}")
                    S.op("s", "activation", txs, txs, AF.Exp, scale=-1.0)
                    S.op("s", "activation", txs, txs, AF.Ln, bias=1.0,
                         inc="sS", mark=f"dln{c}.{dt_}")
                    S.wait("v", f"dln{c}.{dt_}")
                    S.op("v", "tensor_add", dsl, dsl, txs, inc="sv", mark=f"dadd{c}.{dt_}")

                if PARANOID:
                    S.barrier(("delta", c))
                for dt_ in range(DT):
                    sl = slice(dt_ * LC, (dt_ + 1) * LC)
                    if c > 0 and dt_ == 0:
                        S.wait("v", f"omm{c-1}.{MT-1}")
                    S.op("v", "tensor_mul", ub[:, sl], dlt[:, sl], xcb[:, sl],
                         inc="sv", mark=f"uw{c}.{dt_}")

                if PARANOID:
                    S.barrier(("u", c))
                for n in range(NST):
                    npar = n % 2
                    S.wait("v", f"bc{c}.{n}")
                    if c > 0 or n >= 1:
                        pass
                    S.op("v", "tensor_tensor",
                         ball[:].rearrange("p (t f) -> p t f", t=DT),
                         ub[:].rearrange("p (t f) -> p t f", t=DT),
                         Bb[:, npar * LC:(npar + 1) * LC].unsqueeze(1).broadcast_to([128, DT, LC]),
                         ALU.mult, inc="sv", mark=f"ball{c}.{n}")
                    S.wait("v", f"ball{c}.{n}")
                    for dt_ in range(DT):
                        it = n * DT + dt_
                        sl = slice(dt_ * LC, (dt_ + 1) * LC)
                        S.wait("s", f"dadd{c}.{dt_}")
                        if c == 0 and it == 0:
                            S.wait("s", "pre")
                        if c > 0 or it >= 2:
                            pc, pit = (c, it - 2) if it >= 2 else (c - 1, NST * DT - 2 + it)
                            S.wait("s", f"scan{pc}.{pit}")
                        S.op("s", "activation", abuf[:, (it % 2) * LC:(it % 2 + 1) * LC],
                             dlt[:, sl], AF.Exp,
                             scale=A_sb[:, dt_ * NST + n: dt_ * NST + n + 1],
                             inc="sS", mark=f"aexp{c}.{it}")
                        S.wait("v", f"aexp{c}.{it}")
                        S.op("v", "tensor_tensor_scan", ball[:, sl],
                             abuf[:, (it % 2) * LC:(it % 2 + 1) * LC], ball[:, sl],
                             carry[:, dt_ * NST + n: dt_ * NST + n + 1],
                             ALU.mult, ALU.add, inc="sv", mark=f"scan{c}.{it}")
                        S.wait("v", f"scan{c}.{it}")
                        S.op("v", "tensor_copy",
                             carry[:, dt_ * NST + n: dt_ * NST + n + 1],
                             ball[:, dt_ * LC + LC - 1: dt_ * LC + LC])
                    S.op("v", "tensor_tensor",
                         ball[:].rearrange("p (t f) -> p t f", t=DT),
                         ball[:].rearrange("p (t f) -> p t f", t=DT),
                         Cb[:, npar * LC:(npar + 1) * LC].unsqueeze(1).broadcast_to([128, DT, LC]),
                         ALU.mult)
                    if n == 0:
                        S.op("v", "tensor_copy", yb[:], ball[:],
                             inc="sv", mark=f"yacc{c}.{n * DT + DT - 1}")
                    else:
                        S.op("v", "tensor_add", yb[:], yb[:], ball[:],
                             inc="sv", mark=f"yacc{c}.{n * DT + DT - 1}")

                for dt_ in range(DT):
                    sl = slice(dt_ * LC, (dt_ + 1) * LC)
                    S.op("v", "scalar_tensor_tensor", yb[:, sl], xcb[:, sl],
                         dv_sb[:, dt_:dt_ + 1], yb[:, sl], ALU.mult, ALU.add)
                    S.op("v", "tensor_mul", ub[:, sl], yb[:, sl], szb[:, sl],
                         inc="sv", mark=f"gate{c}.{dt_}")

                if PARANOID:
                    S.barrier(("gate", c))
                for mt in range(MT):
                    if c > 0 or mt >= 2:
                        pc, pmt = (c, mt - 2) if mt >= 2 else (c - 1, MT - 2 + mt)
                        S.wait("gp", f"omm{pc}.{pmt}")
                    S.op("gp", "dma_start",
                         out=wobuf[:, (mt % 2) * DT * 128:(mt % 2 + 1) * DT * 128]
                             .rearrange("p (k m) -> p k m", k=DT),
                         in_=woT[:, mt * 128:(mt + 1) * 128]
                             .rearrange("(k p) m -> p k m", p=128),
                         inc=(f"qwB{mt % 2}", 16), mark=f"wo{c}.{mt}")

                for mt in range(MT):
                    S.wait("t", f"wo{c}.{mt}")
                    S.wait("t", f"gate{c}.{DT-1}")
                    if mt >= 2:
                        S.wait("t", f"ofmev{c}.{mt-2}")
                    else:
                        S.wait("t", f"inev{c}.{2 * DT - 2 + mt}")
                    for kt in range(DT):
                        S.op("t", "matmul", pP[mt % 2][:],
                             wobuf[:, (mt % 2) * DT * 128 + kt * 128:(mt % 2) * DT * 128 + (kt + 1) * 128],
                             ub[:, kt * LC:(kt + 1) * LC],
                             start=(kt == 0), stop=(kt == DT - 1),
                             inc="pe" if kt == DT - 1 else None,
                             mark=f"omm{c}.{mt}" if kt == DT - 1 else None)
                    S.wait("s", f"omm{c}.{mt}")
                    if mt >= 2:
                        S.wait("s", f"otr{c}.{mt-2}.3")
                    elif c > 0:
                        S.wait("s", f"otr{c-1}.{MT-2+mt}.3")
                    S.op("s", "activation", ofm[:, (mt % 2) * 512:(mt % 2 + 1) * 512],
                         pP[mt % 2][:], AF.Copy, inc="sS", mark=f"ofmev{c}.{mt}")
                    S.wait("t", f"ofmev{c}.{mt}")
                    for t in range(4):
                        idx = 4 * mt + t
                        if idx >= 2:
                            S.wait("t", f"oadd{c}.{(idx - 2) // 4}.{(idx - 2) % 4}")
                        S.op("t", "transpose", pT[idx % 2][:],
                             ofm[:, (mt % 2) * 512 + t * 128:(mt % 2) * 512 + (t + 1) * 128],
                             id_sb[:], inc="pe", mark=f"otr{c}.{mt}.{t}")
                        S.wait("v", f"otr{c}.{mt}.{t}")
                        dst = xt[:, t * D + mt * 128: t * D + (mt + 1) * 128]
                        S.op("v", "tensor_add", dst, pT[idx % 2][:], dst,
                             inc="sv", mark=f"oadd{c}.{mt}.{t}")

                if debug and c == DBG_CHUNK:
                    dc = DBG_CHUNK
                    for nm, src, mk in [
                        ("dbg_xnT", xnT[:], f"trev{dc}.3.7"), ("dbg_xpad", xpad[:], f"inev{dc}.15"),
                        ("dbg_szb", szb[:], f"inev{dc}.31"), ("dbg_xcb", xcb[:], f"csilu{dc}.15"),
                        ("dbg_dlt", dlt[:], f"dadd{dc}.15"), ("dbg_dtin", dt_in_b[:], f"xdblev{dc}"),
                        ("dbg_bm", bm_s[:], f"xdblev{dc}"), ("dbg_cm", cm_s[:], f"xdblev{dc}"),
                        ("dbg_Bb", Bb[:, 0:LC], f"bc{dc}.0"), ("dbg_ub", ub[:], f"uw{dc}.15"),
                        ("dbg_yb", yb[:], f"yacc{dc}.{NST * DT - 1}"),
                        ("dbg_gate", ub[:], f"gate{dc}.{DT-1}"),
                        ("dbg_wobuf", wobuf[:, 0:DT * 128], f"wo{dc}.0"),
                        ("dbg_ofm", ofm[:, 0:512], f"ofmev{dc}.0"),
                        ("dbg_Asb", A_sb[:], f"gate{dc}.{DT-1}"),
                        ("dbg_carry", carry[:], f"gate{dc}.{DT-1}"),
                    ]:
                        S.wait("sync", mk)
                        S.op("sync", "dma_start", out=dbg[nm], in_=src,
                             inc=("qd", 16), mark=f"dump_{nm}")

                if PARANOID:
                    S.barrier(("oproj", c))
                S.wait("sync", f"oadd{c}.{MT-1}.3")
                S.op("sync", "dma_start",
                     out=acc1[c * LC:(c + 1) * LC, :].rearrange("(t p) d -> p t d", p=128),
                     in_=xt[:].rearrange("p (t d) -> p t d", t=4),
                     inc=("qs", 16), mark=f"store{c}")

            S.wait("sync", f"store{NCH-1}")
            if debug:
                S.wait("sync", "dump_dbg_carry")

        return schedule


    return prep_mamba_weights, make_mamba_phase, E, NST, R, LC, DT


_prep_mamba_weights, _make_mamba_phase, _MA_E, _MA_NST, _MA_R, _MA_LC, _MA_DT = _mamba_mod()


def _moe0_mod():

    F32 = mybir.dt.float32
    BF16 = mybir.dt.bfloat16
    AF = mybir.ActivationFunctionType
    ALU = mybir.AluOpType
    AX = mybir.AxisListType.X

    L, D = 4096, 1024
    NE = 16
    IT = 11
    KD = D // 128
    MT = D // 128
    TH = L // 2
    NTT = TH // 128
    TB = TH // 512
    NXP = NE + 2
    EPS = 1e-5
    POISON = False


    def prep_moe0_weights(ins):
        f32 = np.float32
        g = lambda k: np.asarray(ins[k])
        w = {}
        w["m_gate"] = np.ascontiguousarray(g("L0_ffn_gate_w")).astype(f32)
        w["m_gateb"] = np.ascontiguousarray(g("L0_ffn_gate_b")[None, :]).astype(f32)
        w1 = np.concatenate([g("L0_ffn_w1"),
                             g("L0_ffn_sw1").T.reshape(2, 1408, D).transpose(0, 2, 1)], 0)
        w3 = np.concatenate([g("L0_ffn_w3"),
                             g("L0_ffn_sw3").T.reshape(2, 1408, D).transpose(0, 2, 1)], 0)
        w2 = np.concatenate([g("L0_ffn_w2"),
                             g("L0_ffn_sw2").reshape(2, 1408, D)], 0)
        w["m_w1"] = np.ascontiguousarray(w1).astype(f32)
        w["m_w3"] = np.ascontiguousarray(w3).astype(f32)
        w["m_w2"] = np.ascontiguousarray(w2).astype(f32)
        b1 = np.concatenate([g("L0_ffn_b1"), g("L0_ffn_sb1").reshape(2, 1408)], 0)
        b3 = np.concatenate([g("L0_ffn_b3"), g("L0_ffn_sb3").reshape(2, 1408)], 0)
        w["m_b1"] = np.ascontiguousarray(
            b1.reshape(NXP, IT, 128).transpose(2, 0, 1).reshape(128, NXP * IT)).astype(f32)
        w["m_b3"] = np.ascontiguousarray(
            b3.reshape(NXP, IT, 128).transpose(2, 0, 1).reshape(128, NXP * IT)).astype(f32)
        b2 = np.concatenate([g("L0_ffn_b2"), g("L0_ffn_sb2")[None, :],
                             np.zeros((1, D), f32)], 0)
        w["m_b2"] = np.ascontiguousarray(b2.reshape(1, NXP * D)).astype(f32)
        w["m_ident"] = np.eye(128, dtype=ml_dtypes.bfloat16)
        return w


    def make_moe0_phase(nc, sch, alloc, P, io):
        x = io["x"]; acc = io["acc"]; out = io["out"]
        gate = io["m_gate"]; gateb = io["m_gateb"]
        w1 = io["m_w1"]; w3 = io["m_w3"]; w2 = io["m_w2"]
        b1 = io["m_b1"]; b3 = io["m_b3"]; b2 = io["m_b2"]
        ident_d = io["m_ident"]; d_cw = io["m_d_cw"]
        A = alloc
        xq = A("xq", [128, 4 * D], F32)
        xnb = A("xnb", [128, D], BF16)
        xnT = A("xnT", [128, KD * TH], BF16)
        yac = A("yac", [128, MT * TH], BF16)
        w13b = A("w13b", [128, IT * 2 * KD * 128], BF16)
        w2r = A("w2r", [128, MT * IT * 128], BF16)
        abuf = A("abuf", [128, IT * 512], BF16)
        hsb = A("hsb", [128, 2 * 512], BF16)
        cwt = A("cwt", [128, NTT * NE], F32)
        cwb16 = A("cwb16", [128, NE], BF16)
        cwT = A("cwT", [16, NTT * 128], F32)
        cwbc = A("cwbc", [128, 2 * TH], BF16)
        cwrow = A("cwrow", [1, 2 * TH], BF16)
        onesr = A("onesr", [1, TH], BF16)
        gb_sb = A("gb_sb", [128, NE], F32)
        gate_sb = A("gate_sb", [128, NE * KD], BF16)
        b1_sb = A("b1_sb", [128, NXP * IT], F32)
        b3_sb = A("b3_sb", [128, NXP * IT], F32)
        b2row = A("b2row", [1, 2 * D], BF16)
        id_sb = A("id_sb", [128, 128], BF16)
        lnst = A("lnst", [128, 16], F32)
        gst = A("gst", [128, 8], F32)
        msk = A("msk", [128, NE], F32)
        eps_sb = A("eps_sb", [128, 1], F32)
        nbig_sb = A("nbig_sb", [128, 1], F32)
        sq_f = A("sq_f", [128, D], F32)


        pT = [P[6][:, 0:64].bitcast(BF16), P[7][:, 0:64].bitcast(BF16)]
        pH = [P[0], P[1]]
        pG = [P[2], P[3]]
        pY = [P[4], P[5]]

        LSUM, LSQ, NEGM, M2, VAR, SD, RSTD = 0, 2, 4, 6, 8, 10, 12

        def schedule(S):
            S.op("gp", "dma_start", out=gb_sb[:], in_=gateb.broadcast_to([128, NE]),
                 inc=("qc", 16))
            S.op("gp", "dma_start",
                 out=gate_sb[:].rearrange("p (k n) -> p k n", k=KD),
                 in_=gate.rearrange("(k p) n -> p k n", p=128), inc=("qc", 16))
            S.op("gp", "dma_start", out=b1_sb[:], in_=b1, inc=("qc", 16))
            S.op("gp", "dma_start", out=b3_sb[:], in_=b3, inc=("qc", 16))
            S.op("gp", "dma_start", out=id_sb[:], in_=ident_d, inc=("qc", 16),
                 mark="consts")
            S.wait("v", "consts")
            S.op("v", "memset", eps_sb[:], EPS)
            S.op("v", "memset", nbig_sb[:], -100.0)
            S.op("v", "memset", onesr[:], 1.0, inc="sv", mark="pre")

            for hf in range(2):
                t0 = hf * TH
                for q in range(4):
                    if hf > 0 or q > 0:
                        ph, pq = (hf, q - 1) if q > 0 else (hf - 1, 3)
                        S.wait("sync", f"xtr{ph}.{pq * 4 + 3}.7")
                        if q == 0:
                            S.wait("sync", f"st{hf-1}.3")
                    S.op("sync", "dma_start",
                         out=xq[:].rearrange("p (t d) -> p t d", t=4),
                         in_=x[t0 + q * 512: t0 + (q + 1) * 512, :]
                             .rearrange("(t p) d -> p t d", p=128),
                         inc=("qx", 16), mark=f"x{hf}.{q}")
                    for tl in range(4):
                        t = q * 4 + tl
                        S.wait("s", f"x{hf}.{q}")
                        if hf > 0 or t >= 1:
                            ph, pt = (hf, t - 1) if t >= 1 else (hf - 1, NTT - 1)
                            S.wait("s", f"var{ph}.{pt}")
                        xsl = xq[:, tl * D:(tl + 1) * D]
                        S.op("s", "activation", sq_f[:, 0:D], xsl, AF.Square,
                             inc="sS", mark=f"lnstat{hf}.{t}")
                        S.wait("v", f"lnstat{hf}.{t}")
                        S.wait("v", f"x{hf}.{q}")
                        S.op("v", "reduce_sum", lnst[:, LSUM:LSUM + 1], xsl, axis=AX)
                        S.op("v", "reduce_sum", lnst[:, LSQ:LSQ + 1], sq_f[:, 0:D],
                             axis=AX, inc="sv", mark=f"lnred{hf}.{t}")
                        S.wait("v", f"lnred{hf}.{t}")
                        S.op("v", "tensor_scalar_mul", lnst[:, NEGM:NEGM + 1],
                             lnst[:, LSUM:LSUM + 1], -1.0 / D, inc="sv", mark=f"negm{hf}.{t}")
                        S.wait("v", f"negm{hf}.{t}")
                        S.op("v", "tensor_mul", lnst[:, M2:M2 + 1],
                             lnst[:, NEGM:NEGM + 1], lnst[:, NEGM:NEGM + 1],
                             inc="sv", mark=f"m2_{hf}.{t}")
                        S.wait("v", f"m2_{hf}.{t}")
                        S.op("v", "scalar_tensor_tensor", lnst[:, VAR:VAR + 1],
                             lnst[:, LSQ:LSQ + 1], 1.0 / D, lnst[:, M2:M2 + 1],
                             ALU.mult, ALU.subtract, inc="sv", mark=f"var{hf}.{t}")
                        S.wait("s", f"var{hf}.{t}")
                        S.op("s", "activation", lnst[:, SD:SD + 1], lnst[:, VAR:VAR + 1],
                             AF.Sqrt, bias=eps_sb[:, 0:1], inc="sS", mark=f"sd{hf}.{t}")
                        S.wait("v", f"sd{hf}.{t}")
                        if hf > 0 or t >= 1:
                            ph, pt = (hf, t - 1) if t >= 1 else (hf - 1, NTT - 1)
                            S.wait("v", f"xtr{ph}.{pt}.7")
                        S.op("v", "reciprocal", lnst[:, RSTD:RSTD + 1],
                             lnst[:, SD:SD + 1], inc="sv", mark=f"rstd{hf}.{t}")
                        S.wait("v", f"rstd{hf}.{t}")
                        S.op("v", "tensor_scalar", xnb[:, 0:D], xsl,
                             lnst[:, NEGM:NEGM + 1], lnst[:, RSTD:RSTD + 1],
                             ALU.add, ALU.mult, inc="sv", mark=f"xn{hf}.{t}")
                        S.wait("t", f"xn{hf}.{t}")
                        if hf == 0 and t == 0:
                            S.wait("t", "consts")
                        for kd in range(KD):
                            idx = 8 * t + kd
                            if idx >= 2:
                                S.wait("t", f"trev{hf}.{(idx-2) // 8}.{(idx-2) % 8}")
                            elif hf > 0:
                                S.wait("t", f"fadd{hf-1}.{NTT-1}.{7}")
                            for_mark = f"xtr{hf}.{t}.{kd}"
                            S.op("t", "transpose", pT[idx % 2][:],
                                 xnb[:, kd * 128:(kd + 1) * 128], id_sb[:],
                                 inc="pe", mark=for_mark)
                            S.wait("s", for_mark)
                            S.op("s", "copy",
                                 xnT[:, kd * TH + t * 128: kd * TH + (t + 1) * 128],
                                 pT[idx % 2][:], inc="sS", mark=f"trev{hf}.{t}.{kd}")

                for t in range(NTT):
                    S.wait("t", f"trev{hf}.{t}.7")
                    if hf > 0 and t == 0:
                        S.wait("t", f"yadd{hf-1}.{NXP-1}.{TB-1}.{MT-1}")
                    if t >= 1:
                        S.wait("t", f"glog{hf}.{t-1}")
                    for kd in range(KD):
                        S.op("t", "matmul", pY[0][:, 0:NE],
                             xnT[:, kd * TH + t * 128: kd * TH + (t + 1) * 128],
                             gate_sb[:, kd * NE:(kd + 1) * NE],
                             start=(kd == 0), stop=(kd == KD - 1),
                             inc="pe" if kd == KD - 1 else None,
                             mark=f"gmm{hf}.{t}" if kd == KD - 1 else None)
                    S.wait("v", f"gmm{hf}.{t}")
                    csl = cwt[:, t * NE:(t + 1) * NE]
                    S.op("v", "tensor_add", csl, pY[0][:, 0:NE], gb_sb[:],
                         inc="sv", mark=f"glog{hf}.{t}")
                    S.wait("v", f"glog{hf}.{t}")
                    S.op("v", "reduce_max", gst[:, 0:1], csl, axis=AX, negate=True,
                         inc="sv", mark=f"gmax{hf}.{t}")
                    S.wait("s", f"gmax{hf}.{t}")
                    S.op("s", "activation", csl, csl, AF.Exp, bias=gst[:, 0:1],
                         inc="sS", mark=f"gexp{hf}.{t}")
                    S.wait("v", f"gexp{hf}.{t}")
                    S.op("v", "reduce_sum", gst[:, 1:2], csl, axis=AX,
                         inc="sv", mark=f"gsum{hf}.{t}")
                    S.wait("v", f"gsum{hf}.{t}")
                    S.op("v", "reciprocal", gst[:, 2:3], gst[:, 1:2],
                         inc="sv", mark=f"grec{hf}.{t}")
                    S.wait("v", f"grec{hf}.{t}")
                    S.op("v", "tensor_scalar_mul", csl, csl, gst[:, 2:3],
                         inc="sv", mark=f"gsc{hf}.{t}")
                    S.wait("v", f"gsc{hf}.{t}")
                    S.op("v", "reduce_max", gst[:, 3:4], csl, axis=AX,
                         inc="sv", mark=f"gm1{hf}.{t}")
                    S.wait("v", f"gm1{hf}.{t}")
                    S.op("v", "tensor_single_scalar", msk[:], csl, gst[:, 3:4],
                         ALU.is_equal, inc="sv", mark=f"geq{hf}.{t}")
                    S.wait("v", f"geq{hf}.{t}")
                    S.op("v", "scalar_tensor_tensor", msk[:], msk[:],
                         nbig_sb[:, 0:1], csl, ALU.mult, ALU.add,
                         inc="sv", mark=f"gmsk{hf}.{t}")
                    S.wait("v", f"gmsk{hf}.{t}")
                    S.op("v", "reduce_max", gst[:, 5:6], msk[:], axis=AX,
                         inc="sv", mark=f"gm2{hf}.{t}")
                    S.wait("v", f"gm2{hf}.{t}")
                    S.op("v", "tensor_single_scalar", msk[:], csl, gst[:, 5:6],
                         ALU.is_ge, inc="sv", mark=f"gge{hf}.{t}")
                    S.wait("v", f"gge{hf}.{t}")
                    S.op("v", "tensor_mul", csl, csl, msk[:],
                         inc="sv", mark=f"gcw{hf}.{t}")
                    S.wait("v", f"gcw{hf}.{t}")
                    S.op("v", "tensor_copy", cwb16[:], csl, inc="sv", mark=f"gcb{hf}.{t}")
                    S.wait("t", f"gcb{hf}.{t}")
                    if t >= 2:
                        S.wait("t", f"cwev{hf}.{t-2}")
                    else:
                        S.wait("t", f"trev{hf}.{NTT-1}.{6 + t}")
                    S.op("t", "transpose", pT[t % 2][0:16, :], cwb16[:], id_sb[:],
                         inc="pe", mark=f"ctr{hf}.{t}")
                    S.wait("s", f"ctr{hf}.{t}")
                    S.op("s", "copy", cwT[:, t * 128:(t + 1) * 128], pT[t % 2][0:16, :],
                         inc="sS", mark=f"cwev{hf}.{t}")
                S.wait("gp", f"cwev{hf}.{NTT-1}")
                if hf > 0:
                    S.wait("gp", f"cwb{hf-1}.{NE-1}")
                S.op("gp", "dma_start", out=d_cw, in_=cwT[:], inc=("qcw", 16),
                     mark=f"cwd{hf}")

                for ei in range(NXP):
                    epar = ei % 2
                    shared = ei >= NE
                    if not shared:
                        S.wait("sync", f"cwd{hf}")
                    if hf > 0 or ei >= 2:
                        ph, pei = (hf, ei - 2) if ei >= 2 else (hf - 1, NXP - 2 + ei)
                        S.wait("sync", f"yadd{ph}.{pei}.{TB-1}.{MT-1}")
                    if not shared:
                        S.op("sync", "dma_start", out=cwbc[:, epar * TH:(epar + 1) * TH],
                             in_=d_cw[ei:ei + 1, :].broadcast_to([128, TH]),
                             inc=(f"qbc{epar}", 16))
                        S.op("sync", "dma_start", out=cwrow[:, epar * TH:(epar + 1) * TH],
                             in_=d_cw[ei:ei + 1, :], inc=(f"qbc{epar}", 16))
                    if hf > 0 or ei >= 2:
                        S.wait("gp", f"yadd{ph}.{pei}.{TB-1}.{MT-1}")
                    S.op("gp", "dma_start", out=b2row[0:1, epar * D:(epar + 1) * D],
                         in_=b2[0:1, ei * D:(ei + 1) * D], inc=(f"qbc{epar}", 16),
                         mark=f"cwb{hf}.{ei}")
                    for it in range(IT):
                        if hf > 0 or ei > 0:
                            ph, pei = (hf, ei - 1) if ei > 0 else (hf - 1, NXP - 1)
                            S.wait("gp", f"hgmm{ph}.{pei}.{TB-1}.{it}")
                        if it >= 2:
                            S.wait("gp", f"hgmm{hf}.{ei}.0.{it-2}")
                        wsl = w13b[:, it * 2 * KD * 128:(it * 2 + 2) * KD * 128]
                        S.op("gp", "dma_start",
                             out=wsl[:, 0:KD * 128].rearrange("p (k e) -> p k e", k=KD),
                             in_=w1[ei, :, it * 128:(it + 1) * 128]
                                 .rearrange("(k p) e -> p k e", p=128),
                             inc=(f"qw{it % 2}", 16))
                        S.op("gp", "dma_start",
                             out=wsl[:, KD * 128:2 * KD * 128].rearrange("p (k e) -> p k e", k=KD),
                             in_=w3[ei, :, it * 128:(it + 1) * 128]
                                 .rearrange("(k p) e -> p k e", p=128),
                             inc=(f"qw{it % 2}", 16), mark=f"w13{hf}.{ei}.{it}")
                    for dt in range(MT):
                        if hf > 0 or ei > 0:
                            ph, pei = (hf, ei - 1) if ei > 0 else (hf - 1, NXP - 1)
                            S.wait("gp", f"w2mm{ph}.{pei}.{TB-1}.{dt}")
                        if dt >= 2:
                            S.wait("gp", f"w2mm{hf}.{ei}.0.{dt-2}")
                        S.op("gp", "dma_start",
                             out=w2r[:, dt * IT * 128:(dt + 1) * IT * 128]
                                 .rearrange("p (k m) -> p k m", k=IT),
                             in_=w2[ei, :, dt * 128:(dt + 1) * 128]
                                 .rearrange("(k p) m -> p k m", p=128),
                             inc=(f"qv{dt % 2}", 16), mark=f"w2l{hf}.{ei}.{dt}")
                    for tb in range(TB):
                        for it in range(IT):
                            ipar = it % 2
                            wsl = w13b[:, it * 2 * KD * 128:(it * 2 + 2) * KD * 128]
                            S.wait("t", f"w13{hf}.{ei}.{it}")
                            if hf == 0 and ei == 0 and tb == 0:
                                S.wait("t", f"cwev{hf}.{NTT-1}")
                            if hf > 0 or ei > 0 or tb > 0 or it >= 2:
                                pit, ph, pei, ptb = it - 2, hf, ei, tb
                                if pit < 0:
                                    ptb = tb - 1
                                    if ptb < 0:
                                        pei, ptb = ei - 1, TB - 1
                                        if pei < 0:
                                            ph, pei = hf - 1, NXP - 1
                                    pit = IT + pit
                                S.wait("t", f"aprime{ph}.{pei}.{ptb}.{pit}")
                            for kd in range(KD):
                                S.op("t", "matmul", pH[ipar][:],
                                     wsl[:, kd * 128:(kd + 1) * 128],
                                     xnT[:, kd * TH + tb * 512: kd * TH + (tb + 1) * 512],
                                     start=(kd == 0), stop=(kd == KD - 1))
                            for kd in range(KD):
                                S.op("t", "matmul", pG[ipar][:],
                                     wsl[:, KD * 128 + kd * 128: KD * 128 + (kd + 1) * 128],
                                     xnT[:, kd * TH + tb * 512: kd * TH + (tb + 1) * 512],
                                     start=(kd == 0), stop=(kd == KD - 1),
                                     inc="pe" if kd == KD - 1 else None,
                                     mark=f"hgmm{hf}.{ei}.{tb}.{it}" if kd == KD - 1 else None)
                            S.wait("s", f"hgmm{hf}.{ei}.{tb}.{it}")
                            if hf > 0 or ei > 0 or tb > 0 or it >= 2:
                                S.wait("s", f"aprime{ph}.{pei}.{ptb}.{pit}")
                            hss = hsb[:, ipar * 512:(ipar + 1) * 512]
                            S.op("s", "activation", hss, pH[ipar][:], AF.Silu,
                                 bias=b1_sb[:, ei * IT + it: ei * IT + it + 1],
                                 inc="sS", mark=f"hsil{hf}.{ei}.{tb}.{it}")
                            S.wait("v", f"hsil{hf}.{ei}.{tb}.{it}")
                            if not shared:
                                S.wait("v", f"cwb{hf}.{ei}")
                            if hf > 0 or ei > 0 or tb > 0:
                                ptb2, ph2, pei2 = tb - 1, hf, ei
                                if ptb2 < 0:
                                    pei2 = ei - 1
                                    ptb2 = TB - 1
                                    if pei2 < 0:
                                        ph2, pei2 = hf - 1, NXP - 1
                                S.wait("v", f"w2mm{ph2}.{pei2}.{ptb2}.{MT-1}")
                            asl = abuf[:, it * 512:(it + 1) * 512]
                            S.op("v", "scalar_tensor_tensor", asl, pG[ipar][:],
                                 b3_sb[:, ei * IT + it: ei * IT + it + 1], hss,
                                 ALU.add, ALU.mult,
                                 inc=None if not shared else "sv",
                                 mark=None if not shared else f"aprime{hf}.{ei}.{tb}.{it}")
                            if not shared:
                                S.op("v", "tensor_mul", asl, asl,
                                     cwbc[:, epar * TH + tb * 512: epar * TH + (tb + 1) * 512],
                                     inc="sv", mark=f"aprime{hf}.{ei}.{tb}.{it}")
                        for dt in range(MT):
                            dpar = dt % 2
                            S.wait("t", f"w2l{hf}.{ei}.{dt}")
                            S.wait("t", f"aprime{hf}.{ei}.{tb}.{IT-1}")
                            if hf > 0 or ei > 0 or tb > 0 or dt >= 2:
                                pdt, ph3, pei3, ptb3 = dt - 2, hf, ei, tb
                                if pdt < 0:
                                    ptb3 = tb - 1
                                    if ptb3 < 0:
                                        pei3, ptb3 = ei - 1, TB - 1
                                        if pei3 < 0:
                                            ph3, pei3 = hf - 1, NXP - 1
                                    pdt = MT + pdt
                                S.wait("t", f"yadd{ph3}.{pei3}.{ptb3}.{pdt}")
                            for it in range(IT):
                                S.op("t", "matmul", pY[dpar][:],
                                     w2r[:, dt * IT * 128 + it * 128: dt * IT * 128 + (it + 1) * 128],
                                     abuf[:, it * 512:(it + 1) * 512],
                                     start=(it == 0), stop=False)
                            rrow = (onesr[0:1, tb * 512:(tb + 1) * 512] if shared
                                    else cwrow[0:1, epar * TH + tb * 512: epar * TH + (tb + 1) * 512])
                            S.wait("t", f"cwb{hf}.{ei}")
                            S.op("t", "matmul", pY[dpar][:],
                                 b2row[0:1, epar * D + dt * 128: epar * D + (dt + 1) * 128],
                                 rrow, start=False, stop=True,
                                 inc="pe", mark=f"w2mm{hf}.{ei}.{tb}.{dt}")
                            S.wait("v", f"w2mm{hf}.{ei}.{tb}.{dt}")
                            ysl = yac[:, dt * TH + tb * 512: dt * TH + (tb + 1) * 512]
                            if ei == 0:
                                if hf > 0:
                                    S.wait("v", f"ftr{hf-1}.{NTT-1}.{MT-1}")
                                S.op("v", "tensor_copy", ysl, pY[dpar][:],
                                     inc="sv", mark=f"yadd{hf}.{ei}.{tb}.{dt}")
                            else:
                                S.op("v", "tensor_add", ysl, pY[dpar][:], ysl,
                                     inc="sv", mark=f"yadd{hf}.{ei}.{tb}.{dt}")

                for q in range(4):
                    S.wait("sync", f"xtr{hf}.{NTT-1}.{7}")
                    if q > 0:
                        S.wait("sync", f"st{hf}.{q-1}")
                    S.op("sync", "dma_start",
                         out=xq[:].rearrange("p (t d) -> p t d", t=4),
                         in_=acc[t0 + q * 512: t0 + (q + 1) * 512, :]
                             .rearrange("(t p) d -> p t d", p=128),
                         inc=("qa", 16), mark=f"a{hf}.{q}")
                    for tl in range(4):
                        t = q * 4 + tl
                        for dt in range(MT):
                            idx = 8 * t + dt
                            S.wait("t", f"yadd{hf}.{NXP-1}.{t // 4}.{dt}")
                            if idx >= 2:
                                S.wait("t", f"fadd{hf}.{(idx-2) // 8}.{(idx-2) % 8}")
                            elif hf == 0 and q == 0:
                                S.wait("t", f"cwev{hf}.{NTT-1}")
                            else:
                                pass
                            S.op("t", "transpose", pT[idx % 2][:],
                                 yac[:, dt * TH + t * 128: dt * TH + (t + 1) * 128],
                                 id_sb[:], inc="pe", mark=f"ftr{hf}.{t}.{dt}")
                            S.wait("v", f"ftr{hf}.{t}.{dt}")
                            S.wait("v", f"a{hf}.{q}")
                            dst = xq[:, tl * D + dt * 128: tl * D + (dt + 1) * 128]
                            S.op("v", "tensor_add", dst, pT[idx % 2][:], dst,
                                 inc="sv", mark=f"fadd{hf}.{t}.{dt}")
                    S.wait("sync", f"fadd{hf}.{q * 4 + 3}.{7}")
                    S.op("sync", "dma_start",
                         out=out[t0 + q * 512: t0 + (q + 1) * 512, :]
                             .rearrange("(t p) d -> p t d", p=128),
                         in_=xq[:].rearrange("p (t d) -> p t d", t=4),
                         inc=("qs", 16), mark=f"st{hf}.{q}")

            S.wait("sync", f"st{1}.3")

        return schedule


    return prep_moe0_weights, make_moe0_phase, NE, NXP, IT, TH


_prep_moe0_weights, _make_moe0_phase, _MO_NE, _MO_NXP, _MO_IT, _MO_TH = _moe0_mod()


def _attn1_mod():

    F32 = mybir.dt.float32
    BF16 = mybir.dt.bfloat16
    AF = mybir.ActivationFunctionType
    ALU = mybir.AluOpType
    AX = mybir.AxisListType.X

    L, D = 4096, 1024
    NH, HD = 16, 64
    LC = 512
    NCH = L // LC
    KD = D // 128
    OT = D // 128
    EPS = 1e-5
    ATT_EPS = 1e-12
    POISON = False


    def prep_attn1_weights(ins):
        f32 = np.float32
        g = lambda k: np.asarray(ins[k])
        w = {}
        w["a_wqT"] = np.ascontiguousarray(g("L1_attn_Wq").T).astype(f32)
        w["a_wkT"] = np.ascontiguousarray(g("L1_attn_Wk").T).astype(f32)
        w["a_wvT"] = np.ascontiguousarray(g("L1_attn_Wv").T).astype(f32)
        w["a_woT"] = np.ascontiguousarray(g("L1_attn_Wo").T).astype(f32)
        w["a_ident"] = np.eye(128, dtype=ml_dtypes.bfloat16)
        w["a_maskT"] = np.triu(np.ones((128, 128))).astype(ml_dtypes.bfloat16)
        return w


    def make_attn1_phase(nc, sch, alloc, P, io):
        x = io["x"]; out = io["out"]
        wqT = io["a_wqT"]; wkT = io["a_wkT"]; wvT = io["a_wvT"]; woT = io["a_woT"]
        ident_d = io["a_ident"]; mask_d = io["a_maskT"]
        A = alloc
        xt = A("xt", [128, 4 * D], F32)
        xnb = A("xnb", [128, D], BF16)
        xnT = A("xnT", [128, KD * LC], BF16)
        wq_sb = A("wq_sb", [128, KD * D], BF16)
        wk_sb = A("wk_sb", [128, KD * D], BF16)
        wv_sb = A("wv_sb", [128, KD * D], BF16)
        wo_sb = A("wo_sb", [128, KD * D], BF16)
        qTb = A("qTb", [128, OT * LC], BF16)
        kb = A("kb", [128, 4 * D], BF16)
        kTb = A("kTb", [128, OT * LC], BF16)
        vb = A("vb", [128, 4 * D], BF16)
        attnT = A("attnT", [128, OT * LC], BF16)
        Sst_f = A("Sst_f", [128, (NH // 2) * HD], F32)
        Sst_b = A("Sst_b", [128, (NH // 2) * HD], BF16)
        z_f = A("z_f", [128, NH // 2], F32)
        zb = A("zb", [128, NH // 2], BF16)
        ones_sb = A("ones_sb", [128, 1], BF16)
        ones64 = A("ones64", [1, 64], BF16)
        scm = A("scm", [128, 256], BF16)
        dbc = A("dbc", [128, 256], F32)
        denr = A("denr", [1, 128], F32)
        denrb = A("denrb", [1, 128], BF16)
        otmp = A("otmp", [128, 256], F32)
        qf = A("qf", [128, 512], F32)
        tmin = A("tmin", [128, 512], F32)
        tmax = A("tmax", [128, 512], F32)
        mask_sb = A("mask_sb", [128, 128], BF16)
        id_sb = A("id_sb", [128, 128], BF16)
        lnst = A("lnst", [128, 16], F32)
        eps_sb = A("eps_sb", [128, 1], F32)
        e12_sb = A("e12_sb", [128, 1], F32)
        sq_f = A("sq_f", [128, D], F32)
        ofm = A("ofm", [128, 2 * 512], BF16)


        pQ = [P[0], P[1]]
        pT = P[6][:, 0:64].bitcast(BF16)
        pScs = [P[2][:, 0:128], P[0][:, 0:128]]
        pNs = [P[3][:, 0:128], P[1][:, 0:128]]
        pden = P[4][0:1, 0:128]
        pDbs = [P[5][:, 0:128], P[6][:, 128:256]]
        pSt = P[7][:, 0:64]

        LSUM, LSQ, NEGM, M2, VAR, SD, RSTD = 0, 2, 4, 6, 8, 10, 12

        def schedule(S):
            for nm, dst, src in [("wq", wq_sb, wqT), ("wk", wk_sb, wkT),
                                 ("wv", wv_sb, wvT), ("wo", wo_sb, woT)]:
                S.op("gp", "dma_start",
                     out=dst[:].rearrange("p (k o) -> p k o", k=KD),
                     in_=src.rearrange("(k p) o -> p k o", p=128), inc=("qc", 16))
            S.op("gp", "dma_start", out=id_sb[:], in_=ident_d, inc=("qc", 16))
            S.op("gp", "dma_start", out=mask_sb[:], in_=mask_d, inc=("qc", 16),
                 mark="consts")
            S.wait("v", "consts")
            S.op("v", "memset", eps_sb[:], EPS)
            S.op("v", "memset", e12_sb[:], ATT_EPS)
            S.op("v", "memset", ones_sb[:], 1.0)
            S.op("v", "memset", ones64[:], 1.0)
            S.op("v", "memset", Sst_f[:], 0.0)
            S.op("v", "memset", Sst_b[:], 0.0)
            S.op("v", "memset", z_f[:], 0.0)
            S.op("v", "memset", zb[:], 0.0, inc="sv", mark="pre")

            for c in range(NCH):
                if c > 0:
                    S.wait("sync", f"st{c-1}")
                S.op("sync", "dma_start",
                     out=xt[:].rearrange("p (t d) -> p t d", t=4),
                     in_=x[c * LC:(c + 1) * LC, :].rearrange("(t p) d -> p t d", p=128),
                     inc=("qx", 16), mark=f"x{c}")
                for t in range(4):
                    S.wait("s", f"x{c}")
                    if c > 0 or t >= 1:
                        pc, pt_ = (c, t - 1) if t >= 1 else (c - 1, 3)
                        S.wait("s", f"var{pc}.{pt_}")
                    xsl = xt[:, t * D:(t + 1) * D]
                    S.op("s", "activation", sq_f[:, 0:D], xsl, AF.Square,
                         inc="sS", mark=f"lnstat{c}.{t}")
                    S.wait("v", f"lnstat{c}.{t}")
                    S.wait("v", f"x{c}")
                    S.op("v", "reduce_sum", lnst[:, LSUM:LSUM + 1], xsl, axis=AX)
                    S.op("v", "reduce_sum", lnst[:, LSQ:LSQ + 1], sq_f[:, 0:D],
                         axis=AX, inc="sv", mark=f"lnred{c}.{t}")
                    S.wait("v", f"lnred{c}.{t}")
                    S.op("v", "tensor_scalar_mul", lnst[:, NEGM:NEGM + 1],
                         lnst[:, LSUM:LSUM + 1], -1.0 / D, inc="sv", mark=f"negm{c}.{t}")
                    S.wait("v", f"negm{c}.{t}")
                    S.op("v", "tensor_mul", lnst[:, M2:M2 + 1],
                         lnst[:, NEGM:NEGM + 1], lnst[:, NEGM:NEGM + 1],
                         inc="sv", mark=f"m2_{c}.{t}")
                    S.wait("v", f"m2_{c}.{t}")
                    S.op("v", "scalar_tensor_tensor", lnst[:, VAR:VAR + 1],
                         lnst[:, LSQ:LSQ + 1], 1.0 / D, lnst[:, M2:M2 + 1],
                         ALU.mult, ALU.subtract, inc="sv", mark=f"var{c}.{t}")
                    S.wait("s", f"var{c}.{t}")
                    S.op("s", "activation", lnst[:, SD:SD + 1], lnst[:, VAR:VAR + 1],
                         AF.Sqrt, bias=eps_sb[:, 0:1], inc="sS", mark=f"sd{c}.{t}")
                    S.wait("v", f"sd{c}.{t}")
                    if c > 0 or t >= 1:
                        pc, pt_ = (c, t - 1) if t >= 1 else (c - 1, 3)
                        S.wait("v", f"xtr{pc}.{pt_}.7")
                    S.op("v", "reciprocal", lnst[:, RSTD:RSTD + 1],
                         lnst[:, SD:SD + 1], inc="sv", mark=f"rstd{c}.{t}")
                    S.wait("v", f"rstd{c}.{t}")
                    S.op("v", "tensor_scalar", xnb[:, 0:D], xsl,
                         lnst[:, NEGM:NEGM + 1], lnst[:, RSTD:RSTD + 1],
                         ALU.add, ALU.mult, inc="sv", mark=f"xn{c}.{t}")
                    S.wait("t", f"xn{c}.{t}")
                    if c == 0 and t == 0:
                        S.wait("t", "consts")
                    for kd in range(KD):
                        idx = 8 * t + kd
                        if idx >= 1:
                            S.wait("t", f"trev{c}.{(idx-1) // 8}.{(idx-1) % 8}")
                        elif c > 0:
                            S.wait("t", f"oadd{c-1}.{OT-1}.3")
                        S.op("t", "transpose", pT[:],
                             xnb[:, kd * 128:(kd + 1) * 128], id_sb[:],
                             inc="pe", mark=f"xtr{c}.{t}.{kd}")
                        S.wait("s", f"xtr{c}.{t}.{kd}")
                        S.op("s", "copy",
                             xnT[:, kd * LC + t * 128: kd * LC + (t + 1) * 128],
                             pT[:], inc="sS", mark=f"trev{c}.{t}.{kd}")

                for ot in range(OT):
                    par = ot % 2
                    S.wait("t", f"trev{c}.3.7")
                    if ot >= 2:
                        S.wait("t", f"qel{c}.{ot - 2}")
                    elif c > 0:
                        S.wait("t", f"ofm{c-1}.{OT - 2 + ot}")
                    for kd in range(KD):
                        S.op("t", "matmul", pQ[par][:],
                             wq_sb[:, kd * D + ot * 128: kd * D + (ot + 1) * 128],
                             xnT[:, kd * LC:(kd + 1) * LC],
                             start=(kd == 0), stop=(kd == KD - 1),
                             inc="pe" if kd == KD - 1 else None,
                             mark=f"qmm{c}.{ot}" if kd == KD - 1 else None)
                    S.wait("v", f"qmm{c}.{ot}")
                    S.op("v", "tensor_copy", qf[:], pQ[par][:], inc="sv",
                         mark=f"qev{c}.{ot}")
                    S.wait("v", f"qev{c}.{ot}")
                    S.op("v", "tensor_scalar_min", tmin[:], qf[:], 0.0,
                         inc="sv", mark=f"qmin{c}.{ot}")
                    S.wait("s", f"qmin{c}.{ot}")
                    S.op("s", "activation", tmin[:], tmin[:], AF.Exp,
                         inc="sS", mark=f"qexp{c}.{ot}")
                    S.op("v", "tensor_scalar_max", tmax[:], qf[:], 0.0,
                         inc="sv", mark=f"qmax{c}.{ot}")
                    S.wait("v", f"qexp{c}.{ot}")
                    S.wait("v", f"qmax{c}.{ot}")
                    S.op("v", "tensor_add", qTb[:, ot * LC:(ot + 1) * LC],
                         tmin[:], tmax[:], inc="sv", mark=f"qel{c}.{ot}")

                for t in range(4):
                    par = t % 2
                    if c > 0 or t >= 1:
                        pass
                    S.wait("t", f"qel{c}.{OT-2}")
                    S.wait("t", f"qel{c}.{OT-1}")
                    if t >= 1:
                        S.wait("t", f"vev{c}.{t-1}.1")
                    elif c > 0:
                        S.wait("t", f"kel{c-1}.3")
                    for half in range(2):
                        if half == 1:
                            S.wait("t", f"kev{c}.{t}.0")
                        for kd in range(KD):
                            S.op("t", "matmul", pQ[par][:],
                                 xnT[:, kd * LC + t * 128: kd * LC + (t + 1) * 128],
                                 wk_sb[:, kd * D + half * 512: kd * D + (half + 1) * 512],
                                 start=(kd == 0), stop=(kd == KD - 1),
                                 inc="pe" if kd == KD - 1 else None,
                                 mark=f"kmm{c}.{t}.{half}" if kd == KD - 1 else None)
                        S.wait("v", f"kmm{c}.{t}.{half}")
                        S.op("v", "tensor_copy", qf[:], pQ[par][:], inc="sv",
                             mark=f"kev{c}.{t}.{half}")
                        S.wait("v", f"kev{c}.{t}.{half}")
                        S.op("v", "tensor_scalar_min", tmin[:], qf[:], 0.0,
                             inc="sv", mark=f"kmin{c}.{t}.{half}")
                        S.wait("s", f"kmin{c}.{t}.{half}")
                        S.op("s", "activation", tmin[:], tmin[:], AF.Exp,
                             inc="sS", mark=f"kexp{c}.{t}.{half}")
                        S.op("v", "tensor_scalar_max", tmax[:], qf[:], 0.0,
                             inc="sv", mark=f"kmax{c}.{t}.{half}")
                        S.wait("v", f"kexp{c}.{t}.{half}")
                        S.wait("v", f"kmax{c}.{t}.{half}")
                        S.op("v", "tensor_add",
                             kb[:, t * D + half * 512: t * D + (half + 1) * 512],
                             tmin[:], tmax[:], inc="sv",
                             mark=f"kel{c}.{t}" if half == 1 else None)
                    vpar = 1 - par
                    if t >= 1:
                        S.wait("t", f"kev{c}.{t-1}.1")
                    elif c > 0:
                        S.wait("t", f"ofm{c-1}.{OT-1}")
                    for half in range(2):
                        if half == 1:
                            S.wait("t", f"vev{c}.{t}.0")
                        for kd in range(KD):
                            S.op("t", "matmul", pQ[vpar][:],
                                 xnT[:, kd * LC + t * 128: kd * LC + (t + 1) * 128],
                                 wv_sb[:, kd * D + half * 512: kd * D + (half + 1) * 512],
                                 start=(kd == 0), stop=(kd == KD - 1),
                                 inc="pe" if kd == KD - 1 else None,
                                 mark=f"vmm{c}.{t}.{half}" if kd == KD - 1 else None)
                        S.wait("s", f"vmm{c}.{t}.{half}")
                        S.op("s", "copy",
                             vb[:, t * D + half * 512: t * D + (half + 1) * 512],
                             pQ[vpar][:], inc="sS", mark=f"vev{c}.{t}.{half}")
                for t in range(4):
                    for kd in range(KD):
                        idx = 8 * t + kd
                        S.wait("t", f"kel{c}.{t}")
                        if idx >= 1:
                            S.wait("t", f"ktev{c}.{(idx-1) // 8}.{(idx-1) % 8}")
                        else:
                            S.wait("t", f"trev{c}.3.7")
                        S.op("t", "transpose", pT[:],
                             kb[:, t * D + kd * 128: t * D + (kd + 1) * 128],
                             id_sb[:], inc="pe", mark=f"ktr{c}.{t}.{kd}")
                        S.wait("s", f"ktr{c}.{t}.{kd}")
                        S.op("s", "copy",
                             kTb[:, kd * LC + t * 128: kd * LC + (t + 1) * 128],
                             pT[:], inc="sS", mark=f"ktev{c}.{t}.{kd}")

                for sb_ in range(4):
                    for h in range(NH):
                        it = sb_ * NH + h
                        par = it % 2
                        pSc = pScs[par]
                        pN = pNs[par]
                        pDb = pDbs[par]
                        ot, ph = h // 2, (h % 2) * 64
                        hc = h // 2
                        qsl = qTb[ph:ph + 64, ot * LC + sb_ * 128: ot * LC + (sb_ + 1) * 128]
                        ksl = kTb[ph:ph + 64, ot * LC + sb_ * 128: ot * LC + (sb_ + 1) * 128]
                        vsl = vb[:, sb_ * D + h * 64: sb_ * D + (h + 1) * 64]
                        kbsl = kb[:, sb_ * D + h * 64: sb_ * D + (h + 1) * 64]

                        def pv(k):
                            return (c, it - 2) if it >= 2 else (c - 1, 62 + it)

                        S.wait("t", f"qel{c}.{ot}")
                        S.wait("t", f"ktev{c}.{sb_}.{7}")
                        if it == 1:
                            S.wait("t", f"vev{c}.3.1")
                        if c > 0 or it >= 2:
                            pc2, pit = pv(0)
                            S.wait("t", f"scm{pc2}.{pit}")
                        S.op("t", "matmul", pSc, ksl, qsl, start=True, stop=True,
                             inc="pe", mark=f"sc{c}.{it}")
                        S.wait("v", f"sc{c}.{it}")
                        scs = scm[:, par * 128:(par + 1) * 128]
                        S.op("v", "tensor_mul", scs, pSc, mask_sb[:],
                             inc="sv", mark=f"scm{c}.{it}")
                        S.wait("t", f"scm{c}.{it}")
                        if c > 0 or sb_ >= 1:
                            pit2, pc2 = it - 16, c
                            if pit2 < 0:
                                pc2, pit2 = c - 1, 48 + h
                            S.wait("t", f"zbf{pc2}.{pit2}")
                            S.wait("t", f"sbf{pc2}.{pit2}")
                        if c > 0 or it >= 1:
                            pc2, pit = (c, it - 1) if it >= 1 else (c - 1, 63)
                            S.wait("t", f"denr{pc2}.{pit}")
                        S.op("t", "matmul", pden, zb[ph:ph + 64, hc:hc + 1], qsl,
                             start=True, stop=False)
                        S.op("t", "matmul", pden, ones_sb[:], scs,
                             start=False, stop=True, inc="pe", mark=f"den{c}.{it}")
                        if it == 1:
                            S.wait("t", f"kel{c}.3")
                        if c > 0 or it >= 2:
                            pc2, pit = pv(0)
                            S.wait("t", f"omul{pc2}.{pit}")
                        S.op("t", "matmul", pN[ph:ph + 64, :],
                             Sst_b[ph:ph + 64, hc * HD:(hc + 1) * HD], qsl,
                             start=True, stop=False, tile_position=(ph, ph))
                        S.op("t", "matmul", pN[ph:ph + 64, :], vsl, scs,
                             start=False, stop=True, tile_position=(0, ph),
                             inc="pe", mark=f"num{c}.{it}")
                        S.wait("v", f"den{c}.{it}")
                        S.op("v", "reciprocal", denr[:], pden, inc="sv",
                             mark=f"denr{c}.{it}")
                        S.wait("v", f"denr{c}.{it}")
                        S.op("v", "tensor_copy", denrb[:], denr[:], inc="sv",
                             mark=f"denb{c}.{it}")
                        S.wait("t", f"denb{c}.{it}")
                        if c > 0 or it >= 2:
                            pc2, pit = pv(0)
                            S.wait("t", f"dbc{pc2}.{pit}")
                        S.op("t", "matmul", pDb[ph:ph + 64, :], ones64[:], denrb[:],
                             start=True, stop=True, tile_position=(0, ph),
                             inc="pe", mark=f"dbm{c}.{it}")
                        S.wait("v", f"dbm{c}.{it}")
                        dbs = dbc[:, par * 128:(par + 1) * 128]
                        S.op("v", "tensor_copy", dbs[ph:ph + 64, :], pDb[ph:ph + 64, :],
                             inc="sv", mark=f"dbc{c}.{it}")
                        S.wait("v", f"num{c}.{it}")
                        S.wait("v", f"dbc{c}.{it}")
                        ots = otmp[:, par * 128:(par + 1) * 128]
                        S.op("v", "tensor_mul", ots[ph:ph + 64, :], pN[ph:ph + 64, :],
                             dbs[ph:ph + 64, :], inc="sv", mark=f"omul{c}.{it}")
                        S.wait("v", f"omul{c}.{it}")
                        S.op("v", "tensor_scalar_add",
                             attnT[ph:ph + 64, ot * LC + sb_ * 128: ot * LC + (sb_ + 1) * 128],
                             ots[ph:ph + 64, :], e12_sb[ph:ph + 64, 0:1],
                             inc="sv", mark=f"oev{c}.{it}")
                        if c > 0 or it >= 1:
                            pc2, pit = (c, it - 1) if it >= 1 else (c - 1, 63)
                            S.wait("t", f"sadd{pc2}.{pit}")
                        S.op("t", "matmul", pSt[ph:ph + 64, :], kbsl, vsl,
                             start=True, stop=True, tile_position=(0, ph),
                             inc="pe", mark=f"stu{c}.{it}")
                        S.wait("v", f"stu{c}.{it}")
                        S.op("v", "tensor_add", Sst_f[ph:ph + 64, hc * HD:(hc + 1) * HD],
                             pSt[ph:ph + 64, :], Sst_f[ph:ph + 64, hc * HD:(hc + 1) * HD],
                             inc="sv", mark=f"sadd{c}.{it}")
                        S.wait("v", f"sadd{c}.{it}")
                        S.op("v", "tensor_copy", Sst_b[ph:ph + 64, hc * HD:(hc + 1) * HD],
                             Sst_f[ph:ph + 64, hc * HD:(hc + 1) * HD],
                             inc="sv", mark=f"sbf{c}.{it}")
                        S.op("v", "reduce_sum", lnst[ph:ph + 64, 14:15], ksl, axis=AX,
                             inc="sv", mark=f"zred{c}.{it}")
                        S.wait("v", f"zred{c}.{it}")
                        S.op("v", "tensor_add", z_f[ph:ph + 64, hc:hc + 1],
                             lnst[ph:ph + 64, 14:15], z_f[ph:ph + 64, hc:hc + 1],
                             inc="sv", mark=f"zadd{c}.{it}")
                        S.wait("v", f"zadd{c}.{it}")
                        S.op("v", "tensor_copy", zb[ph:ph + 64, hc:hc + 1],
                             z_f[ph:ph + 64, hc:hc + 1],
                             inc="sv", mark=f"zbf{c}.{it}")

                for mt in range(OT):
                    par = mt % 2
                    S.wait("t", f"oev{c}.{63}")
                    S.wait("t", f"vev{c}.3.1")
                    S.wait("t", f"kel{c}.3")
                    if c > 0 or mt >= 2:
                        pc2, pmt = (c, mt - 2) if mt >= 2 else (c - 1, OT - 2 + mt)
                        S.wait("t", f"ofm{pc2}.{pmt}")
                    for kd in range(KD):
                        S.op("t", "matmul", pQ[par][:],
                             wo_sb[:, kd * D + mt * 128: kd * D + (mt + 1) * 128],
                             attnT[:, kd * LC:(kd + 1) * LC],
                             start=(kd == 0), stop=(kd == KD - 1),
                             inc="pe" if kd == KD - 1 else None,
                             mark=f"omm{c}.{mt}" if kd == KD - 1 else None)
                    S.wait("s", f"omm{c}.{mt}")
                    if c > 0 or mt >= 2:
                        pc2, pmt = (c, mt - 2) if mt >= 2 else (c - 1, OT - 2 + mt)
                        S.wait("s", f"otr{pc2}.{pmt}.3")
                    S.op("s", "activation", ofm[:, par * 512:(par + 1) * 512],
                         pQ[par][:], AF.Copy, inc="sS", mark=f"ofm{c}.{mt}")
                    S.wait("t", f"ofm{c}.{mt}")
                    for t in range(4):
                        idx = 4 * mt + t
                        if c > 0 or idx >= 1:
                            pc2, pidx = (c, idx - 1) if idx >= 1 else (c - 1, 31)
                            S.wait("t", f"oadd{pc2}.{pidx // 4}.{pidx % 4}")
                        S.op("t", "transpose", pT[:],
                             ofm[:, par * 512 + t * 128: par * 512 + (t + 1) * 128],
                             id_sb[:], inc="pe", mark=f"otr{c}.{mt}.{t}")
                        S.wait("v", f"otr{c}.{mt}.{t}")
                        dst = xt[:, t * D + mt * 128: t * D + (mt + 1) * 128]
                        S.op("v", "tensor_add", dst, pT[:], dst,
                             inc="sv", mark=f"oadd{c}.{mt}.{t}")
                S.wait("sync", f"oadd{c}.{OT-1}.3")
                S.op("sync", "dma_start",
                     out=out[c * LC:(c + 1) * LC, :].rearrange("(t p) d -> p t d", p=128),
                     in_=xt[:].rearrange("p (t d) -> p t d", t=4),
                     inc=("qs", 16), mark=f"st{c}")

            S.wait("sync", f"st{NCH-1}")

        return schedule


    return prep_attn1_weights, make_attn1_phase


_prep_attn1_weights, _make_attn1_phase = _attn1_mod()


def _moe1_mod():

    F32 = mybir.dt.float32
    BF16 = mybir.dt.bfloat16
    AF = mybir.ActivationFunctionType
    ALU = mybir.AluOpType
    AX = mybir.AxisListType.X

    L, D = 4096, 1024
    NE = 8
    NPE = 16
    IT = 16
    KD = D // 128
    MT = D // 128
    TQ = L // 4
    NTT = TQ // 128
    TB = TQ // 512
    EPS = 1e-5
    POISON = False


    def prep_moe1_weights(ins):
        f32 = np.float32
        g = lambda k: np.asarray(ins[k])
        w = {}
        w["n_gate"] = np.ascontiguousarray(g("L1_ffn_gate_kernel")).astype(f32)
        w["n_w0"] = np.ascontiguousarray(g("L1_ffn_w0")).astype(f32)
        w["n_w1"] = np.ascontiguousarray(g("L1_ffn_w1")).astype(f32)
        w["n_wo"] = np.ascontiguousarray(g("L1_ffn_wo")).astype(f32)
        w["n_ident"] = np.eye(128, dtype=ml_dtypes.bfloat16)
        return w


    def make_moe1_phase(nc, sch, alloc, P, io):
        debug = False
        x = io["x"]; acc = io["acc"]; out = io["out"]
        gate = io["n_gate"]
        w0 = io["n_w0"]; w1 = io["n_w1"]; wo = io["n_wo"]
        ident_d = io["n_ident"]; d_cw = io["n_d_cw"]
        A = alloc
        xq = A("xq", [128, 4 * D], F32)
        xnb = A("xnb", [128, D], BF16)
        xnT = A("xnT", [128, KD * TQ], BF16)
        yac = A("yac", [128, MT * TQ], BF16)
        w13b = A("w13b", [128, IT * 2 * KD * 128], BF16)
        w2r = A("w2r", [128, MT * IT * 128], BF16)
        abuf = A("abuf", [128, IT * 512], BF16)
        hsb = A("hsb", [128, 2 * 512], BF16)
        cwt = A("cwt", [128, NTT * NE], F32)
        cwb16 = A("cwb16", [128, NE], BF16)
        cwT = A("cwT", [NE, NTT * 128], F32)
        cwbc = A("cwbc", [128, 2 * TQ], BF16)
        gate_sb = A("gate_sb", [128, NE * KD], BF16)
        id_sb = A("id_sb", [128, 128], BF16)
        lnst = A("lnst", [128, 16], F32)
        gst = A("gst", [128, 8], F32)
        msk = A("msk", [128, NE], F32)
        eps_sb = A("eps_sb", [128, 1], F32)
        nbig_sb = A("nbig_sb", [128, 1], F32)
        sq_f = A("sq_f", [128, D], F32)


        pT = [P[6][:, 0:64].bitcast(BF16), P[7][:, 0:64].bitcast(BF16)]
        pH = [P[0], P[1]]
        pG = [P[2], P[3]]
        pY = [P[4], P[5]]

        LSUM, LSQ, NEGM, M2, VAR, SD, RSTD = 0, 2, 4, 6, 8, 10, 12

        def schedule(S):
            S.op("gp", "dma_start",
                 out=gate_sb[:].rearrange("p (k n) -> p k n", k=KD),
                 in_=gate.rearrange("(k p) n -> p k n", p=128), inc=("qc", 16))
            S.op("gp", "dma_start", out=id_sb[:], in_=ident_d, inc=("qc", 16),
                 mark="consts")
            S.wait("v", "consts")
            S.op("v", "memset", eps_sb[:], EPS)
            S.op("v", "memset", nbig_sb[:], -100.0, inc="sv", mark="pre")

            for hf in range(L // TQ):
                t0 = hf * TQ
                for q in range(2):
                    if hf > 0 or q > 0:
                        ph, pq = (hf, q - 1) if q > 0 else (hf - 1, 1)
                        S.wait("sync", f"xtr{ph}.{pq * 4 + 3}.7")
                        if q == 0:
                            S.wait("sync", f"st{hf-1}.1")
                    S.op("sync", "dma_start",
                         out=xq[:].rearrange("p (t d) -> p t d", t=4),
                         in_=x[t0 + q * 512: t0 + (q + 1) * 512, :]
                             .rearrange("(t p) d -> p t d", p=128),
                         inc=("qx", 16), mark=f"x{hf}.{q}")
                    for tl in range(4):
                        t = q * 4 + tl
                        S.wait("s", f"x{hf}.{q}")
                        if hf > 0 or t >= 1:
                            ph, pt = (hf, t - 1) if t >= 1 else (hf - 1, NTT - 1)
                            S.wait("s", f"var{ph}.{pt}")
                        xsl = xq[:, tl * D:(tl + 1) * D]
                        S.op("s", "activation", sq_f[:, 0:D], xsl, AF.Square,
                             inc="sS", mark=f"lnstat{hf}.{t}")
                        S.wait("v", f"lnstat{hf}.{t}")
                        S.wait("v", f"x{hf}.{q}")
                        S.op("v", "reduce_sum", lnst[:, LSUM:LSUM + 1], xsl, axis=AX)
                        S.op("v", "reduce_sum", lnst[:, LSQ:LSQ + 1], sq_f[:, 0:D],
                             axis=AX, inc="sv", mark=f"lnred{hf}.{t}")
                        S.wait("v", f"lnred{hf}.{t}")
                        S.op("v", "tensor_scalar_mul", lnst[:, NEGM:NEGM + 1],
                             lnst[:, LSUM:LSUM + 1], -1.0 / D, inc="sv", mark=f"negm{hf}.{t}")
                        S.wait("v", f"negm{hf}.{t}")
                        S.op("v", "tensor_mul", lnst[:, M2:M2 + 1],
                             lnst[:, NEGM:NEGM + 1], lnst[:, NEGM:NEGM + 1],
                             inc="sv", mark=f"m2_{hf}.{t}")
                        S.wait("v", f"m2_{hf}.{t}")
                        S.op("v", "scalar_tensor_tensor", lnst[:, VAR:VAR + 1],
                             lnst[:, LSQ:LSQ + 1], 1.0 / D, lnst[:, M2:M2 + 1],
                             ALU.mult, ALU.subtract, inc="sv", mark=f"var{hf}.{t}")
                        S.wait("s", f"var{hf}.{t}")
                        S.op("s", "activation", lnst[:, SD:SD + 1], lnst[:, VAR:VAR + 1],
                             AF.Sqrt, bias=eps_sb[:, 0:1], inc="sS", mark=f"sd{hf}.{t}")
                        S.wait("v", f"sd{hf}.{t}")
                        if hf > 0 or t >= 1:
                            ph, pt = (hf, t - 1) if t >= 1 else (hf - 1, NTT - 1)
                            S.wait("v", f"xtr{ph}.{pt}.7")
                        S.op("v", "reciprocal", lnst[:, RSTD:RSTD + 1],
                             lnst[:, SD:SD + 1], inc="sv", mark=f"rstd{hf}.{t}")
                        S.wait("v", f"rstd{hf}.{t}")
                        S.op("v", "tensor_scalar", xnb[:, 0:D], xsl,
                             lnst[:, NEGM:NEGM + 1], lnst[:, RSTD:RSTD + 1],
                             ALU.add, ALU.mult, inc="sv", mark=f"xn{hf}.{t}")
                        S.wait("t", f"xn{hf}.{t}")
                        if hf == 0 and t == 0:
                            S.wait("t", "consts")
                        for kd in range(KD):
                            idx = 8 * t + kd
                            if idx >= 2:
                                S.wait("t", f"trev{hf}.{(idx-2) // 8}.{(idx-2) % 8}")
                            elif hf > 0:
                                S.wait("t", f"fadd{hf-1}.{NTT-1}.{7}")
                            S.op("t", "transpose", pT[idx % 2][:],
                                 xnb[:, kd * 128:(kd + 1) * 128], id_sb[:],
                                 inc="pe", mark=f"xtr{hf}.{t}.{kd}")
                            S.wait("s", f"xtr{hf}.{t}.{kd}")
                            S.op("s", "copy",
                                 xnT[:, kd * TQ + t * 128: kd * TQ + (t + 1) * 128],
                                 pT[idx % 2][:], inc="sS", mark=f"trev{hf}.{t}.{kd}")

                for t in range(NTT):
                    S.wait("t", f"trev{hf}.{t}.7")
                    if hf > 0 and t == 0:
                        S.wait("t", f"yadd{hf-1}.{NPE-1}.{TB-1}.{MT-1}")
                    if t >= 1:
                        S.wait("t", f"glog{hf}.{t-1}")
                    for kd in range(KD):
                        S.op("t", "matmul", pY[0][:, 0:NE],
                             xnT[:, kd * TQ + t * 128: kd * TQ + (t + 1) * 128],
                             gate_sb[:, kd * NE:(kd + 1) * NE],
                             start=(kd == 0), stop=(kd == KD - 1),
                             inc="pe" if kd == KD - 1 else None,
                             mark=f"gmm{hf}.{t}" if kd == KD - 1 else None)
                    S.wait("v", f"gmm{hf}.{t}")
                    csl = cwt[:, t * NE:(t + 1) * NE]
                    S.op("v", "tensor_copy", csl, pY[0][:, 0:NE],
                         inc="sv", mark=f"glog{hf}.{t}")
                    S.wait("v", f"glog{hf}.{t}")
                    S.op("v", "reduce_max", gst[:, 0:1], csl, axis=AX, negate=True,
                         inc="sv", mark=f"gmax{hf}.{t}")
                    S.wait("v", f"gmax{hf}.{t}")
                    S.op("v", "tensor_scalar_mul", gst[:, 3:4], gst[:, 0:1], -1.0,
                         inc="sv", mark=f"gm1p{hf}.{t}")
                    S.wait("v", f"gm1p{hf}.{t}")
                    S.op("v", "tensor_single_scalar", msk[:], csl, gst[:, 3:4],
                         ALU.is_equal, inc="sv", mark=f"geq{hf}.{t}")
                    S.wait("v", f"geq{hf}.{t}")
                    S.op("v", "scalar_tensor_tensor", msk[:], msk[:],
                         nbig_sb[:, 0:1], csl, ALU.mult, ALU.add,
                         inc="sv", mark=f"gmsk{hf}.{t}")
                    S.wait("v", f"gmsk{hf}.{t}")
                    S.op("v", "reduce_max", gst[:, 5:6], msk[:], axis=AX,
                         inc="sv", mark=f"gm2{hf}.{t}")
                    S.wait("v", f"gm2{hf}.{t}")
                    S.op("v", "tensor_single_scalar", msk[:], csl, gst[:, 5:6],
                         ALU.is_ge, inc="sv", mark=f"gge{hf}.{t}")
                    S.wait("s", f"gmax{hf}.{t}")
                    S.wait("s", f"gge{hf}.{t}")
                    S.op("s", "activation", csl, csl, AF.Exp, bias=gst[:, 0:1],
                         inc="sS", mark=f"gexp{hf}.{t}")
                    S.wait("v", f"gexp{hf}.{t}")
                    S.wait("v", f"gge{hf}.{t}")
                    S.op("v", "tensor_mul", csl, csl, msk[:],
                         inc="sv", mark=f"gcun{hf}.{t}")
                    S.wait("v", f"gcun{hf}.{t}")
                    S.op("v", "reduce_sum", gst[:, 1:2], csl, axis=AX,
                         inc="sv", mark=f"gsum{hf}.{t}")
                    S.wait("v", f"gsum{hf}.{t}")
                    S.op("v", "reciprocal", gst[:, 2:3], gst[:, 1:2],
                         inc="sv", mark=f"grec{hf}.{t}")
                    S.wait("v", f"grec{hf}.{t}")
                    S.op("v", "tensor_scalar_mul", csl, csl, gst[:, 2:3],
                         inc="sv", mark=f"gcw{hf}.{t}")
                    S.wait("v", f"gcw{hf}.{t}")
                    S.op("v", "tensor_copy", cwb16[:], csl, inc="sv", mark=f"gcb{hf}.{t}")
                    S.wait("t", f"gcb{hf}.{t}")
                    if t >= 2:
                        S.wait("t", f"cwev{hf}.{t-2}")
                    else:
                        S.wait("t", f"trev{hf}.{NTT-1}.{6 + t}")
                    S.op("t", "transpose", pT[t % 2][0:NE, :], cwb16[:], id_sb[:],
                         inc="pe", mark=f"ctr{hf}.{t}")
                    S.wait("s", f"ctr{hf}.{t}")
                    S.op("s", "copy", cwT[:, t * 128:(t + 1) * 128], pT[t % 2][0:NE, :],
                         inc="sS", mark=f"cwev{hf}.{t}")
                S.wait("gp", f"cwev{hf}.{NTT-1}")
                if hf > 0:
                    S.wait("gp", f"cwb{hf-1}.{NPE-1}")
                S.op("gp", "dma_start", out=d_cw, in_=cwT[:], inc=("qcw", 16),
                     mark=f"cwd{hf}")
                if debug and hf == 0:
                    S.wait("sync", f"gcw{hf}.{NTT-1}")
                    S.op("sync", "dma_start", out=dbg_cwt, in_=cwt[:], inc=("qs", 16))
                    S.wait("sync", f"cwev{hf}.{NTT-1}")
                    S.op("sync", "dma_start", out=dbg_cwT, in_=cwT[:], inc=("qs", 16))

                for ei in range(NPE):
                    e, gh = ei // 2, ei % 2
                    epar = ei % 2
                    S.wait("sync", f"cwd{hf}")
                    if hf > 0 or ei >= 2:
                        ph, pei = (hf, ei - 2) if ei >= 2 else (hf - 1, NPE - 2 + ei)
                        S.wait("sync", f"yadd{ph}.{pei}.{TB-1}.{MT-1}")
                    S.op("sync", "dma_start", out=cwbc[:, epar * TQ:(epar + 1) * TQ],
                         in_=d_cw[e:e + 1, :].broadcast_to([128, TQ]),
                         inc=(f"qbc{epar}", 16), mark=f"cwb{hf}.{ei}")
                    if debug and hf == 0 and ei == 0:
                        S.wait("sync", f"cwb{hf}.{ei}")
                        S.op("sync", "dma_start", out=dbg_cwbc,
                             in_=cwbc[:, 0:TQ], inc=("qs", 16))
                    if hf > 0 or ei > 0:
                        ph, pei = (hf, ei - 1) if ei > 0 else (hf - 1, NPE - 1)
                        S.wait("gp", f"hgmm{ph}.{pei}.{TB-1}.{IT-1}")
                    S.op("gp", "dma_start",
                         out=w13b[:, 0:IT * KD * 128]
                             .rearrange("p (k i e) -> p k i e", k=KD, i=IT),
                         in_=w0[e, :, gh * 2048:(gh + 1) * 2048]
                             .rearrange("(k p) (i e) -> p k i e", p=128, e=128),
                         inc=("qw0", 16), mark=f"w13a{hf}.{ei}")
                    S.op("gp", "dma_start",
                         out=w13b[:, IT * KD * 128:2 * IT * KD * 128]
                             .rearrange("p (k i e) -> p k i e", k=KD, i=IT),
                         in_=w1[e, :, gh * 2048:(gh + 1) * 2048]
                             .rearrange("(k p) (i e) -> p k i e", p=128, e=128),
                         inc=("qw1", 16), mark=f"w13{hf}.{ei}")
                    if hf > 0 or ei > 0:
                        ph, pei = (hf, ei - 1) if ei > 0 else (hf - 1, NPE - 1)
                        S.wait("gp", f"w2mm{ph}.{pei}.{TB-1}.{MT-1}")
                    S.op("gp", "dma_start",
                         out=w2r[:].rearrange("p (k d m) -> p k d m", k=IT, d=MT),
                         in_=wo[e, gh * 2048:(gh + 1) * 2048, :]
                             .rearrange("(k p) (d m) -> p k d m", p=128, m=128),
                         inc=("qv", 16), mark=f"w2l{hf}.{ei}")
                    for tb in range(TB):
                        for it in range(IT):
                            ipar = it % 2
                            S.wait("t", f"w13a{hf}.{ei}")
                            S.wait("t", f"w13{hf}.{ei}")
                            if hf == 0 and ei == 0 and tb == 0:
                                S.wait("t", f"cwev{hf}.{NTT-1}")
                            if hf > 0 or ei > 0 or tb > 0 or it >= 2:
                                pit, ph, pei, ptb = it - 2, hf, ei, tb
                                if pit < 0:
                                    ptb = tb - 1
                                    if ptb < 0:
                                        pei, ptb = ei - 1, TB - 1
                                        if pei < 0:
                                            ph, pei = hf - 1, NPE - 1
                                    pit = IT + pit
                                S.wait("t", f"aprime{ph}.{pei}.{ptb}.{pit}")
                            for kd in range(KD):
                                S.op("t", "matmul", pH[ipar][:],
                                     w13b[:, (kd * IT + it) * 128:(kd * IT + it + 1) * 128],
                                     xnT[:, kd * TQ + tb * 512: kd * TQ + (tb + 1) * 512],
                                     start=(kd == 0), stop=(kd == KD - 1))
                            for kd in range(KD):
                                S.op("t", "matmul", pG[ipar][:],
                                     w13b[:, IT * KD * 128 + (kd * IT + it) * 128:
                                          IT * KD * 128 + (kd * IT + it + 1) * 128],
                                     xnT[:, kd * TQ + tb * 512: kd * TQ + (tb + 1) * 512],
                                     start=(kd == 0), stop=(kd == KD - 1),
                                     inc="pe" if kd == KD - 1 else None,
                                     mark=f"hgmm{hf}.{ei}.{tb}.{it}" if kd == KD - 1 else None)
                            S.wait("s", f"hgmm{hf}.{ei}.{tb}.{it}")
                            if hf > 0 or ei > 0 or tb > 0 or it >= 2:
                                S.wait("s", f"aprime{ph}.{pei}.{ptb}.{pit}")
                            hss = hsb[:, ipar * 512:(ipar + 1) * 512]
                            S.op("s", "activation", hss, pH[ipar][:], AF.Silu,
                                 inc="sS", mark=f"hsil{hf}.{ei}.{tb}.{it}")
                            S.wait("v", f"hsil{hf}.{ei}.{tb}.{it}")
                            S.wait("v", f"cwb{hf}.{ei}")
                            if hf > 0 or ei > 0 or tb > 0:
                                ptb2, ph2, pei2 = tb - 1, hf, ei
                                if ptb2 < 0:
                                    pei2, ptb2 = ei - 1, TB - 1
                                    if pei2 < 0:
                                        ph2, pei2 = hf - 1, NPE - 1
                                S.wait("v", f"w2mm{ph2}.{pei2}.{ptb2}.{MT-1}")
                            asl = abuf[:, it * 512:(it + 1) * 512]
                            S.op("v", "tensor_mul", asl, pG[ipar][:], hss)
                            S.op("v", "tensor_mul", asl, asl,
                                 cwbc[:, epar * TQ + tb * 512: epar * TQ + (tb + 1) * 512],
                                 inc="sv", mark=f"aprime{hf}.{ei}.{tb}.{it}")
                        for dt in range(MT):
                            dpar = dt % 2
                            S.wait("t", f"w2l{hf}.{ei}")
                            S.wait("t", f"aprime{hf}.{ei}.{tb}.{IT-1}")
                            if hf > 0 or ei > 0 or tb > 0 or dt >= 2:
                                pdt, ph3, pei3, ptb3 = dt - 2, hf, ei, tb
                                if pdt < 0:
                                    ptb3 = tb - 1
                                    if ptb3 < 0:
                                        pei3, ptb3 = ei - 1, TB - 1
                                        if pei3 < 0:
                                            ph3, pei3 = hf - 1, NPE - 1
                                    pdt = MT + pdt
                                S.wait("t", f"yadd{ph3}.{pei3}.{ptb3}.{pdt}")
                            for it in range(IT):
                                S.op("t", "matmul", pY[dpar][:],
                                     w2r[:, (it * MT + dt) * 128:(it * MT + dt + 1) * 128],
                                     abuf[:, it * 512:(it + 1) * 512],
                                     start=(it == 0), stop=(it == IT - 1),
                                     inc="pe" if it == IT - 1 else None,
                                     mark=f"w2mm{hf}.{ei}.{tb}.{dt}" if it == IT - 1 else None)
                            S.wait("v", f"w2mm{hf}.{ei}.{tb}.{dt}")
                            ysl = yac[:, dt * TQ + tb * 512: dt * TQ + (tb + 1) * 512]
                            if ei == 0:
                                if hf > 0:
                                    S.wait("v", f"ftr{hf-1}.{NTT-1}.{MT-1}")
                                S.op("v", "tensor_copy", ysl, pY[dpar][:],
                                     inc="sv", mark=f"yadd{hf}.{ei}.{tb}.{dt}")
                            else:
                                S.op("v", "tensor_add", ysl, pY[dpar][:], ysl,
                                     inc="sv", mark=f"yadd{hf}.{ei}.{tb}.{dt}")

                for q in range(2):
                    S.wait("sync", f"xtr{hf}.{NTT-1}.{7}")
                    if q > 0:
                        S.wait("sync", f"st{hf}.{q-1}")
                    S.op("sync", "dma_start",
                         out=xq[:].rearrange("p (t d) -> p t d", t=4),
                         in_=acc[t0 + q * 512: t0 + (q + 1) * 512, :]
                             .rearrange("(t p) d -> p t d", p=128),
                         inc=("qa", 16), mark=f"a{hf}.{q}")
                    for tl in range(4):
                        t = q * 4 + tl
                        for dt in range(MT):
                            idx = 8 * t + dt
                            S.wait("t", f"yadd{hf}.{NPE-1}.{t // 4}.{dt}")
                            if idx >= 2:
                                S.wait("t", f"fadd{hf}.{(idx-2) // 8}.{(idx-2) % 8}")
                            else:
                                S.wait("t", f"cwev{hf}.{NTT-1}")
                            S.op("t", "transpose", pT[idx % 2][:],
                                 yac[:, dt * TQ + t * 128: dt * TQ + (t + 1) * 128],
                                 id_sb[:], inc="pe", mark=f"ftr{hf}.{t}.{dt}")
                            S.wait("v", f"ftr{hf}.{t}.{dt}")
                            S.wait("v", f"a{hf}.{q}")
                            dst = xq[:, tl * D + dt * 128: tl * D + (dt + 1) * 128]
                            S.op("v", "tensor_add", dst, pT[idx % 2][:], dst,
                                 inc="sv", mark=f"fadd{hf}.{t}.{dt}")
                    S.wait("sync", f"fadd{hf}.{q * 4 + 3}.{7}")
                    S.op("sync", "dma_start",
                         out=out[t0 + q * 512: t0 + (q + 1) * 512, :]
                             .rearrange("(t p) d -> p t d", p=128),
                         in_=xq[:].rearrange("p (t d) -> p t d", t=4),
                         inc=("qs", 16), mark=f"st{hf}.{q}")

            S.wait("sync", f"st{L // TQ - 1}.1")

        return schedule


    return prep_moe1_weights, make_moe1_phase, NE, TQ


_prep_moe1_weights, _make_moe1_phase, _MQ_NE, _MQ_TQ = _moe1_mod()



_L, _D = 4096, 1024
F32 = mybir.dt.float32
BF16 = mybir.dt.bfloat16


def _prep_weights(ins):
    w = {}
    w.update(_prep_mamba_weights(ins))
    w.update(_prep_moe0_weights(ins))
    w.update(_prep_attn1_weights(ins))
    w.update(_prep_moe1_weights(ins))
    return w


def _build_fused():
    nc = bass.Bass("TRN2", target_bir_lowering=False)
    L, D = _L, _D
    dx = nc.dram_tensor("x", [L, D], F32, kind="ExternalInput").ap()
    dout = nc.dram_tensor("out", [L, D], F32, kind="ExternalOutput").ap()
    d_acc1 = nc.dram_tensor("d_acc1", [L, D], F32, kind="Internal").ap()
    d_x1 = nc.dram_tensor("d_x1", [L, D], F32, kind="Internal").ap()
    d_acc2 = nc.dram_tensor("d_acc2", [L, D], F32, kind="Internal").ap()

    def din(nm, shp, dt=F32):
        return nc.dram_tensor(nm, shp, dt, kind="ExternalInput").ap()

    E, NST, R, LCm, DTm = _MA_E, _MA_NST, _MA_R, _MA_LC, _MA_DT
    ioA = dict(x=dx, out=d_acc1,
               wxzT=din("wxzT", [D, 2 * E]), convw=din("convw", [128, DTm * 4]),
               convb=din("convb", [128, DTm]), xprojT=din("xprojT", [E, 96]),
               dtwT=din("dtwT", [R, E]), dtb=din("dtb", [128, DTm]),
               alog=din("alog", [128, DTm * NST]), dvec=din("dvec", [128, DTm]),
               woT=din("woT", [E, D]), ident=din("ident", [128, 128], BF16),
               d_bc=nc.dram_tensor("d_bc", [2, 2, NST, LCm], BF16, kind="Internal").ap())
    NEb, NXP, ITb, TH = _MO_NE, _MO_NXP, _MO_IT, _MO_TH
    ioB = dict(x=dx, acc=d_acc1, out=d_x1,
               m_gate=din("m_gate", [D, NEb]), m_gateb=din("m_gateb", [1, NEb]),
               m_w1=din("m_w1", [NXP, D, ITb * 128]),
               m_w3=din("m_w3", [NXP, D, ITb * 128]),
               m_w2=din("m_w2", [NXP, ITb * 128, D]),
               m_b1=din("m_b1", [128, NXP * ITb]),
               m_b3=din("m_b3", [128, NXP * ITb]),
               m_b2=din("m_b2", [1, NXP * D]),
               m_ident=din("m_ident", [128, 128], BF16),
               m_d_cw=nc.dram_tensor("d_cw0", [NEb, TH], BF16, kind="Internal").ap())
    ioC = dict(x=d_x1, out=d_acc2,
               a_wqT=din("a_wqT", [D, D]), a_wkT=din("a_wkT", [D, D]),
               a_wvT=din("a_wvT", [D, D]), a_woT=din("a_woT", [D, D]),
               a_ident=din("a_ident", [128, 128], BF16),
               a_maskT=din("a_maskT", [128, 128], BF16))
    NEd, TQ = _MQ_NE, _MQ_TQ
    ioD = dict(x=d_x1, acc=d_acc2, out=dout,
               n_gate=din("n_gate", [D, NEd]),
               n_w0=din("n_w0", [NEd, D, 4096]),
               n_w1=din("n_w1", [NEd, D, 4096]),
               n_wo=din("n_wo", [NEd, 4096, D]),
               n_ident=din("n_ident", [128, 128], BF16),
               n_d_cw=nc.dram_tensor("d_cw1", [NEd, TQ], BF16, kind="Internal").ap())

    sch = Sched(nc, ["qx", "qa", "qs", "qc", "qwA0", "qwA1", "qwB0", "qwB1",
                     "qbo0", "qbo1", "qb0", "qb1", "qw0", "qw1", "qv", "qv0",
                     "qv1", "qcw", "qbc0", "qbc1", "pe", "sv", "sS"])
    ar = Arena(nc, 212000)
    P = [nc.alloc_psum_tensor(f"P{i}", [128, 512], F32) for i in range(8)]

    phases = []
    for pfx, maker, io in [("A", _make_mamba_phase, ioA),
                           ("B", _make_moe0_phase, ioB),
                           ("C", _make_attn1_phase, ioC),
                           ("D", _make_moe1_phase, ioD)]:
        ar.reset()
        phases.append((pfx, maker(nc, sch, ar.alloc, P, io)))

    def schedule(S):
        for pfx, ph in phases:
            S.prefix = pfx
            ph(S)
            S.barrier("end")
        S.prefix = ""

    schedule(sch)
    with nc.Block() as blk:
        @blk.sync
        def _(h):
            sch.begin_emit("sync", h)
            schedule(sch)

        @blk.gpsimd
        def _(h):
            sch.begin_emit("gp", h)
            schedule(sch)

        @blk.vector
        def _(h):
            sch.begin_emit("v", h)
            schedule(sch)

        @blk.scalar
        def _(h):
            sch.begin_emit("s", h)
            schedule(sch)

        @blk.tensor
        def _(h):
            sch.begin_emit("t", h)
            schedule(sch)

    return nc


_NCORES = 8


def kernel(**inputs):
    ins = {k: np.asarray(v) for k, v in inputs.items()}
    X = np.ascontiguousarray(ins["L0_attn_hidden_states"].astype(np.float32))
    w = _prep_weights(ins)
    nc = _build_fused()
    res = run_bass_kernel_spmd(
        nc, [dict(x=X[b], **w) for b in range(_NCORES)],
        core_ids=list(range(_NCORES)))
    out = np.stack([np.asarray(r["out"]) for r in res.results])
    return out.astype(np.float32)
